# version 11
# baseline (speedup 1.0000x reference)
import contextlib
import numpy as np
import ml_dtypes
import concourse.bass as bass
import concourse.mybir as mybir
from concourse.bass_utils import run_bass_kernel_spmd

F32 = mybir.dt.float32
BF16 = mybir.dt.bfloat16
I32 = mybir.dt.int32
U8 = mybir.dt.uint8
ALU = mybir.AluOpType
AF = mybir.ActivationFunctionType
AX = mybir.AxisListType

T, D, H, DH, II = 4096, 1024, 16, 64, 2816
ICN, ICW = 8, 512
JCN = T // 128
TCN, TCW = 8, 512
NPAIR = H // 2
JG = 2


def fbits(x):
    return int(np.float32(x).view(np.int32))


B_0875, B_025, B_075 = fbits(0.875), fbits(0.25), fbits(0.75)
MAGIC_ADD = 0x00A00000
MASK_TOP = -4194304


class Task:
    __slots__ = ("eng", "fn", "waits", "inc", "n")

    def __init__(self, eng, fn, waits, inc, n):
        self.eng, self.fn, self.waits, self.inc, self.n = eng, fn, waits, inc, n


class Builder:
    def __init__(self, debug=False):
        self.debug = debug
        self.nc = bass.Bass("TRN2", target_bir_lowering=False, debug=False)
        self.tasks = []
        self.tot = {}
        self.sems = {}
        self.marks = {}
        self._q_tmp_user = None
        self._es = contextlib.ExitStack()
        self._declare()

    def sem(self, name):
        if name not in self.sems:
            self.sems[name] = self._es.enter_context(self.nc.semaphore(name))
            self.tot[name] = 0
        return self.sems[name]

    def at(self, semname):
        self.sem(semname)
        return (semname, self.tot[semname])

    def t(self, eng, fn, waits=(), inc=None, n=1):
        ws = [self.marks[w] if isinstance(w, str) else w for w in waits]
        for s, v in ws:
            assert v <= self.tot.get(s, 0), f"forward wait {s}:{v}>{self.tot.get(s, 0)}"
        if inc is not None:
            self.sem(inc)
            self.tot[inc] += n
        self.tasks.append(Task(eng, fn, ws, inc, n))

    def mark(self, name, semname):
        self.sem(semname)
        self.marks[name] = (semname, self.tot[semname])

    def _declare(self):
        nc = self.nc
        ein, eout = "ExternalInput", "ExternalOutput"
        scr = eout if self.debug else "Internal"
        d = lambda name, shape, dt, kind: nc.dram_tensor(name, shape, dt, kind=kind).ap()
        self.x = d("x", [T, D], F32, ein)
        self.wqkvT = d("wqkvT", [D, 3 * D], F32, ein)
        self.woT = d("woT", [D, D], F32, ein)
        self.bqkv = d("bqkv", [3 * D], F32, ein)
        self.bo = d("bo", [D], F32, ein)
        self.wg = d("wg", [II, D], F32, ein)
        self.wu = d("wu", [II, D], F32, ein)
        self.wd = d("wd", [D, II], F32, ein)
        self.ones_col_d = d("ones_col", [128, 1], BF16, ein)
        self.ones_row_d = d("ones_row", [1, 128], BF16, ein)
        self.out = d("out", [T, D], F32, eout)
        self.xln_s = d("xln_s", [T, D], BF16, scr)
        self.qT_s = d("qT_s", [D, T], BF16, scr)
        self.x1_s = d("x1_s", [T, D], F32, scr)
        self.xq_s = d("xq_s", [T, D], BF16, scr)
        self.hq_s = d("hq_s", [T, II], BF16, scr)
        self.qwg_s = d("qwg_s", [II, D], BF16, scr)
        self.qwu_s = d("qwu_s", [II, D], BF16, scr)
        self.qwd_s = d("qwd_s", [D, II], BF16, scr)

        AREN = 157696
        self.arena = nc.alloc_sbuf_tensor("arena", [128, AREN], U8)
        self.off = {}
        for name, off, nb in [
            ("KT", 0, 65536),
            ("V", 65536, 66560),
            ("WSL", 132096, 16384),
            ("OT", 148480, 8192),
            ("QWG", 0, 45056), ("QWU", 45056, 45056), ("QWD", 90112, 45056),
            ("HQT", 135168, 22528),
        ]:
            assert off + nb <= AREN
            self.off[name] = ("A", off, nb)

        SCRN = 49152
        self.scrb = nc.alloc_sbuf_tensor("scrb", [128, SCRN], U8)
        o = 0

        def sseg(name, nb):
            nonlocal o
            o = (o + 63) // 64 * 64
            assert o + nb <= SCRN, (name, o, nb)
            self.off[name] = ("S", o, nb)
            o += nb

        for n_, nb in [("WF", 2048), ("WQX", 2048), ("WQA", 2048), ("WQG", 2048),
                       ("WQT", 2048), ("WQC", 2048), ("WQM", 2048), ("WQO", 1024),
                       ("WQSC", 512), ("BNS", 128), ("BNV", 128),
                       ("XR", 8192), ("X2", 4096), ("XLN", 2048), ("ST", 192),
                       ("UN", 18432)]:
            sseg(n_, nb)
        u = self.off["UN"][1]
        for name, du, nb in [
            ("XLT", 0, 16384),
            ("ST2", 16384, 192),
            ("QS", 16384, 2048), ("PA", 0, 6144), ("PB", 6144, 6144),
            ("BOROW", 0, 4096),
            ("OE", 12288, 2048), ("T1", 12288, 2048), ("RSBF", 14336, 2048),
            ("XQT", 0, 8192), ("SGT", 8192, 2048), ("HT", 10240, 2048),
            ("HQ", 12288, 2048), ("OUTT", 14336, 4096),
        ]:
            assert du + nb <= 18432
            self.off[name] = ("S", u + du, nb)
        self.off["BVROW"] = ("S", self.off["XR"][1] + 4096, 4096)
        self.off["X1T"] = self.off["X2"]

        self.psA = self._es.enter_context(nc.psum_tensor("psA", [128, 1024], F32))
        self.psB = self._es.enter_context(nc.psum_tensor("psB", [128, 1024], F32))
        self.psC = self._es.enter_context(nc.psum_tensor("psC", [128, 1024], F32))
        self.psOA = self._es.enter_context(nc.psum_tensor("psOA", [128, 512], F32))
        self.psOB = self._es.enter_context(nc.psum_tensor("psOB", [128, 512], F32))

        sb = nc.alloc_sbuf_tensor
        self.ones_col = sb("ones_col_t", [128, 1], BF16).ap()
        self.ones_row = sb("ones_row_t", [1, 128], BF16).ap()
        self.ones_row_f = sb("ones_row_f", [1, 128], F32).ap()
        self.bK_col = sb("bK_col", [128, 8], F32).ap()
        self.bQ_col = sb("bQ_col", [128, 8], F32).ap()
        self.bias_tile = sb("bias_tile", [128, 1024], F32).ap()
        self.cneg = sb("cneg", [128, 16], F32).ap()
        self.mq2 = sb("mq2", [1, 16], F32).ap()
        self.mk2 = sb("mk2", [1, 16], F32).ap()
        self.sm1 = sb("sm1", [1, 64], F32).ap()
        rb_kind, rb_off, rb_nb = self.off["RSBF"]
        self.rs_bf = self.scrb.ap()[0:1, rb_off:rb_off + rb_nb].bitcast(BF16)
        self.cneg_bf = sb("cneg_bf", [1, 16], BF16).ap()

    def _view(self, name, esz, cols, sub):
        kind, off, nb = self.off[name]
        base = self.arena if kind == "A" else self.scrb
        assert sub + cols * esz <= nb, (name, sub, cols)
        return base.ap()[:, off + sub:off + sub + cols * esz]

    def bf(self, name, cols, sub=0):
        return self._view(name, 2, cols, sub).bitcast(BF16)

    def f32v(self, name, cols, sub=0):
        return self._view(name, 4, cols, sub).bitcast(F32)

    def i32v(self, name, cols, sub=0):
        return self._view(name, 4, cols, sub).bitcast(I32)

    def u8v(self, name, cols, sub=0):
        return self._view(name, 1, cols, sub)

    def emit_quant2(self, src_f32, dst_bf16, F, waits_first=(), waits_last=(),
                    inc_last=None, pool=True):
        nb = F // 16
        bm = self.f32v("WQSC", nb, sub=0)
        s_ = self.f32v("WQSC", nb, sub=128)
        rs = self.f32v("WQSC", nb, sub=256)
        sh = self.f32v("WQSC", nb, sub=384)
        xn, ai = self.f32v("WQX", F), self.i32v("WQA", F)
        sg = self.f32v("WQG", F)
        tt, cc = self.i32v("WQT", F), self.f32v("WQC", F)
        mm = self.f32v("WQM", F)
        sv = src_f32.rearrange("p (n b) -> p n b", b=16)
        rsb = rs.unsqueeze(2).broadcast_to([128, nb, 16])
        shb = sh.unsqueeze(2).broadcast_to([128, nb, 16])
        dv = dst_bf16.rearrange("p (n b) -> p n b", b=16)
        xnv = xn.rearrange("p (n b) -> p n b", b=16)
        mmv = mm.rearrange("p (n b) -> p n b", b=16)
        fin = inc_last if inc_last is not None else ("s_qp" if pool else "s_qv")
        tw = list(waits_first)
        if self._q_tmp_user is not None:
            tw.append(self._q_tmp_user)

        if pool:
            def A(v):
                v.tensor_reduce(bm, sv, AX.X, ALU.max, apply_absolute_value=True)
                v.tensor_scalar(s_, bm, 1.0 / 6.0, 1e-12, ALU.mult, ALU.max)
                v.reciprocal(rs, s_)
                return v.tensor_scalar(sh, s_, 0.5, None, ALU.mult)
            self.t("vector", A, waits=tw, inc="s_qv")

            def B(g):
                return g.tensor_tensor(xnv, sv, rsb, ALU.mult)
            self.t("pool", B, waits=[self.at("s_qv")], inc="s_qp")
            srcfree = self.at("s_qp")

            def C(v):
                v.tensor_scalar(ai, xn.bitcast(I32), 0x7FFFFFFF, None, ALU.bitwise_and)
                v.tensor_scalar(sg.bitcast(I32), xn.bitcast(I32), -2147483648,
                                1065353216, ALU.bitwise_and, ALU.bitwise_or)
                v.tensor_scalar(tt, ai, B_0875, MAGIC_ADD, ALU.max, ALU.add)
                return v.tensor_scalar(tt, tt, MASK_TOP, None, ALU.bitwise_and)
            self.t("vector", C, waits=[self.at("s_qp")], inc="s_qv")

            def D(g):
                g.tensor_scalar(cc, ai.bitcast(F32), 0.25, None, ALU.is_gt)
                g.tensor_scalar(mm, ai.bitcast(F32), 0.75, None, ALU.is_gt)
                g.tensor_scalar(xn, tt.bitcast(F32), -1.0, None, ALU.add)
                g.tensor_tensor(mm, mm, xn, ALU.mult)
                g.tensor_add(mm, mm, cc)
                g.tensor_tensor(mm, mm, sg, ALU.mult)
                return g.tensor_tensor(dv, mmv, shb, ALU.mult)
            self.t("pool", D, waits=[self.at("s_qv")] + list(waits_last), inc=fin)
            self._q_tmp_user = self.at(fin)
            return srcfree, self._q_tmp_user

        u8m = self.u8v("WQM", F)

        def E(v):
            v.tensor_reduce(bm, sv, AX.X, ALU.max, apply_absolute_value=True)
            v.tensor_scalar(s_, bm, 1.0 / 6.0, 1e-12, ALU.mult, ALU.max)
            v.reciprocal(rs, s_)
            v.tensor_scalar(sh, s_, 0.5, None, ALU.mult)
            v.tensor_tensor(xnv, sv, rsb, ALU.mult)
            v.tensor_scalar(ai, xn.bitcast(I32), 0x7FFFFFFF, None, ALU.bitwise_and)
            v.tensor_scalar(sg.bitcast(I32), xn.bitcast(I32), -2147483648,
                            1065353216, ALU.bitwise_and, ALU.bitwise_or)
            v.tensor_scalar(tt, ai, B_0875, MAGIC_ADD, ALU.max, ALU.add)
            v.tensor_scalar(tt, tt, MASK_TOP, None, ALU.bitwise_and)
            v.tensor_scalar(cc, ai, B_025, None, ALU.is_gt)
            v.tensor_scalar(u8m, ai, B_075, None, ALU.is_gt)
            v.select(xn, u8m, tt.bitcast(F32), cc)
            v.tensor_tensor(xn, xn, sg, ALU.mult)
            return v.tensor_tensor(dv, xnv, shb, ALU.mult)
        self.t("vector", E, waits=tw + list(waits_last), inc=fin)
        self._q_tmp_user = self.at(fin)
        return self._q_tmp_user, self._q_tmp_user

    def schedule(self):
        self.prologue()
        self.stage1a()
        self.stage1b()
        self.stage2()
        self.stage3()

    def prologue(self):
        def ld_consts(e):
            e.dma_start(out=self.ones_col, in_=self.ones_col_d)
            e.dma_start(out=self.ones_row, in_=self.ones_row_d)
            for cc in range(8):
                e.dma_start(out=self.bQ_col[:, cc:cc + 1],
                            in_=self.bqkv[cc * 128:(cc + 1) * 128].unsqueeze(1))
                e.dma_start(out=self.bK_col[:, cc:cc + 1],
                            in_=self.bqkv[1024 + cc * 128:1024 + (cc + 1) * 128].unsqueeze(1))
            return e.dma_start(out=self.f32v("BVROW", 1024)[0:1, :],
                               in_=self.bqkv[2048:3072].unsqueeze(0))
        self.t("sync", ld_consts, inc="s_ld", n=19 * 16)

        def consts_prep(v):
            vones = self.bf("V", 16 * 65 * 32).rearrange("p (jh e) -> p jh e", e=65)
            v.memset(vones[:, :, 64:65], 1.0)
            v.memset(self.f32v("ST", 1, sub=128), 1e-5)
            return v.tensor_copy(self.ones_row_f, self.ones_row)
        self.t("vector", consts_prep, waits=[self.at("s_ld")], inc="s_dve")

        def bv_mm(e):
            bv = self.f32v("BVROW", 1024)[0:1, :]
            e.matmul(self.psA.ap()[:, 0:512], self.ones_row_f, bv[:, 0:512],
                     start=True, stop=True)
            return e.matmul(self.psA.ap()[:, 512:1024], self.ones_row_f,
                            bv[:, 512:1024], start=True, stop=True)
        self.t("tensor", bv_mm, waits=[self.at("s_ld"), self.at("s_dve")], inc="s_pe")
        self.mark("bv_mm_done", "s_pe")
        self.t("vector", lambda v: v.tensor_copy(self.bias_tile, self.psA.ap()[:]),
               waits=[self.at("s_pe")], inc="s_dve")
        self.mark("bv_ready", "s_dve")

        def ld_wk(e):
            last = None
            for kc in range(8):
                last = e.dma_start(out=self.bf("WSL", 1024, sub=kc * 2048),
                                   in_=self.wqkvT[kc * 128:(kc + 1) * 128, 1024:2048])
            return last
        self.t("gpsimd", ld_wk, inc="s_wld", n=8 * 16)
        self.mark("wk_ready", "s_wld")

        jobs = []
        for (w_in, w_out, rows, cols) in [(self.wg, self.qwg_s, II, D),
                                          (self.wu, self.qwu_s, II, D),
                                          (self.wd, self.qwd_s, D, II)]:
            for r0 in range(0, rows, 128):
                for c0 in range(0, cols, 512):
                    jobs.append((w_in, w_out, r0, c0, min(512, cols - c0)))
        self.wq_jobs, self.wq_idx = jobs, 0
        self.wq_wf_free = [None, None]
        self.wq_out_free = [None, None]

    def emit_wquant(self, k=1, pool=False):
        for _ in range(k):
            if self.wq_idx >= len(self.wq_jobs):
                return
            w_in, w_out, r0, c0, cw = self.wq_jobs[self.wq_idx]
            slot = self.wq_idx % 2
            self.wq_idx += 1
            wf = self.f32v("WF", cw)
            wo = self.bf("WQO", cw)
            ldsem = "s_wl2"
            waits = []
            if self.wq_wf_free[0] is not None:
                waits.append(self.wq_wf_free[0])
            self.t("gpsimd",
                   lambda e, w_in=w_in, r0=r0, c0=c0, cw=cw, wf=wf:
                   e.dma_start(out=wf, in_=w_in[r0:r0 + 128, c0:c0 + cw]),
                   waits=waits, inc=ldsem, n=16)
            wl = []
            if self.wq_out_free[0] is not None:
                wl.append(self.wq_out_free[0])
            srcfree, qdone = self.emit_quant2(wf, wo, cw,
                                              waits_first=[self.at(ldsem)],
                                              waits_last=wl, pool=pool)
            self.wq_wf_free[0] = srcfree
            spsem = f"s_wsp{slot}"
            self.t("gpsimd",
                   lambda e, w_out=w_out, r0=r0, c0=c0, cw=cw, wo=wo:
                   e.dma_start(out=w_out[r0:r0 + 128, c0:c0 + cw], in_=wo),
                   waits=[qdone], inc=spsem, n=16)
            self.wq_out_free[0] = self.at(spsem)

    def stage1a(self):
        def ln_views(tt):
            slot = tt % 2
            return dict(
                xr=self.f32v("XR", 1024, sub=slot * 4096),
                bns0=self.f32v("BNS", 6, sub=slot * 64),
                bns1=self.f32v("BNS", 6, sub=slot * 64 + 24),
                bns=self.f32v("BNS", 12, sub=slot * 64),
                mean=self.f32v("BNV", 1, sub=slot * 64),
                var=self.f32v("BNV", 1, sub=slot * 64 + 4),
                mv=self.f32v("BNV", 2, sub=slot * 64),
                rstd=self.f32v("BNV", 1, sub=slot * 64 + 8),
                rstd2=self.f32v("BNV", 1, sub=slot * 64 + 12),
                xln=self.bf("XLN", 1024))

        def emit_ln_stats(tt):
            vw = ln_views(tt)
            waits = []
            if tt >= 2:
                waits.append(f"norm1_{tt - 2}")
            if tt == 1:
                waits.append("bv_mm_done")
            self.t("sync",
                   lambda e, tt=tt, xr=vw["xr"]: e.dma_start(
                       out=xr, in_=self.x[tt * 128:(tt + 1) * 128, :]),
                   waits=waits, inc=f"s_xld{tt % 2}", n=16)

            def ln(v, vw=vw):
                v.bn_stats(vw["bns0"], vw["xr"][:, 0:512])
                v.bn_stats(vw["bns1"], vw["xr"][:, 512:1024])
                return v.bn_aggr(vw["mv"], vw["bns"])
            self.t("vector", ln, waits=[self.at(f"s_xld{tt % 2}")], inc="s_dve")
            self.t("scalar",
                   lambda a, vw=vw: a.activation(vw["rstd"], vw["var"], AF.Sqrt,
                                                 bias=self.f32v("ST", 1, sub=128)),
                   waits=[self.at("s_dve")], inc="s_act")
            self.mark(f"sqrt1_{tt}", "s_act")

        def emit_ln_norm(tt):
            vw = ln_views(tt)

            def norm(v, vw=vw):
                v.reciprocal(vw["rstd2"], vw["rstd"])
                return v.tensor_scalar(vw["xln"], vw["xr"], vw["mean"], vw["rstd2"],
                                       ALU.subtract, ALU.mult)
            nwaits = [f"sqrt1_{tt}"]
            if tt >= 1:
                nwaits.append(self.at(f"s_xsp{(tt - 1) % 2}"))
            self.t("vector", norm, waits=nwaits, inc="s_dve")
            self.mark(f"norm1_{tt}", "s_dve")
            self.t("sync",
                   lambda e, tt=tt, xln=vw["xln"]: e.dma_start(
                       out=self.xln_s[tt * 128:(tt + 1) * 128, :], in_=xln),
                   waits=[self.at("s_dve")], inc=f"s_xsp{tt % 2}", n=16)
            self.mark(f"xspm_{tt}", f"s_xsp{tt % 2}")

        self.xlt_user = [None, None]

        def k_phase(ic):
            slot = ic % 2

            def tld(e, ic=ic, slot=slot):
                last = None
                for kc in range(8):
                    last = e.dma_start(
                        out=self.bf("XLT", 512, sub=slot * 8192 + kc * 1024),
                        in_=self.xln_s[ic * 512:(ic + 1) * 512, kc * 128:(kc + 1) * 128],
                        transpose=True)
                return last
            waits = [f"xspm_{4 * ic + 3}", f"xspm_{4 * ic + 2}"]
            if self.xlt_user[slot] is not None:
                waits.append(self.xlt_user[slot])
            self.t("sync", tld, waits=waits, inc=f"s_tp{slot}", n=8 * 16)
            tmark = self.at(f"s_tp{slot}")

            for cc in range(8):
                bank = cc % 2
                gidx = ic * 8 + cc
                ps = self.psA.ap()[:, bank * 512:(bank + 1) * 512]
                waits = [tmark, "wk_ready", "bv_ready"]
                if gidx >= 2:
                    waits.append(f"kev_{gidx - 2}")

                def kmm(e, cc=cc, ps=ps, slot=slot):
                    last = None
                    for kc in range(8):
                        last = e.matmul(
                            ps, self.bf("WSL", 128, sub=kc * 2048 + cc * 256),
                            self.bf("XLT", 512, sub=slot * 8192 + kc * 1024),
                            start=(kc == 0), stop=(kc == 7))
                    return last
                self.t("tensor", kmm, waits=waits, inc="s_pe")
                self.t("vector",
                       lambda v, cc=cc, ic=ic, ps=ps: v.tensor_scalar(
                           self.bf("KT", 512, sub=cc * 8192 + ic * 1024), ps,
                           self.bK_col[:, cc:cc + 1], None, ALU.add),
                       waits=[self.at("s_pe")], inc="s_dve")
                self.mark(f"kev_{gidx}", "s_dve")
            self.mark(f"kvmm_K_{ic}", "s_pe")
            self.xlt_user[slot] = f"kvmm_K_{ic}"
            self.emit_wquant(2)

        NT = T // 128
        emit_ln_stats(0)
        for tt in range(1, NT):
            emit_ln_stats(tt)
            emit_ln_norm(tt - 1)
            if tt % 4 == 0:
                k_phase(tt // 4 - 1)
        emit_ln_norm(NT - 1)
        k_phase(ICN - 1)
        self.ln1_done_waits = [self.at("s_xsp0"), self.at("s_xsp1")]

        for phase in ("V",):
            if phase == "V":
                def ld_wv(e):
                    last = None
                    for kc in range(8):
                        last = e.dma_start(out=self.bf("WSL", 1024, sub=kc * 2048),
                                           in_=self.wqkvT[kc * 128:(kc + 1) * 128, 2048:3072])
                    return last
                self.t("gpsimd", ld_wv, waits=[self.at("s_pe")], inc="s_wld", n=8 * 16)
                self.mark("wv_ready", "s_wld")
            for ic in range(ICN):
                slot = ic % 2
                def tld(e, ic=ic, slot=slot):
                    last = None
                    for kc in range(8):
                        last = e.dma_start(
                            out=self.bf("XLT", 512, sub=slot * 8192 + kc * 1024),
                            in_=self.xln_s[ic * 512:(ic + 1) * 512, kc * 128:(kc + 1) * 128],
                            transpose=True)
                    return last
                waits = list(self.ln1_done_waits)
                if self.xlt_user[slot] is not None:
                    waits.append(self.xlt_user[slot])
                self.t("sync", tld, waits=waits, inc=f"s_tp{slot}", n=8 * 16)
                tmark = self.at(f"s_tp{slot}")

                if True:
                    for ts_ in range(4):
                        for vc in range(2):
                            bank = (ts_ * 2 + vc) % 2
                            gidx = (ic * 4 + ts_) * 2 + vc
                            ps = self.psB.ap()[:, bank * 512:(bank + 1) * 512]
                            waits = [tmark, "wv_ready", "bv_ready"]
                            if gidx >= 2:
                                waits.append(f"vev_{gidx - 2}")

                            def vmm(e, ts_=ts_, vc=vc, ps=ps, slot=slot):
                                last = None
                                for kc in range(8):
                                    last = e.matmul(
                                        ps,
                                        self.bf("XLT", 128,
                                                sub=slot * 8192 + kc * 1024 + ts_ * 256),
                                        self.bf("WSL", 512, sub=kc * 2048 + vc * 1024),
                                        start=(kc == 0), stop=(kc == 7))
                                return last
                            self.t("tensor", vmm, waits=waits, inc="s_pe")
                            jc = ic * 4 + ts_

                            def vev(v, jc=jc, vc=vc, ps=ps):
                                dst = self.bf("V", 8 * 65, sub=(jc * 16 + vc * 8) * 130)
                                dst = dst.rearrange("p (h e) -> p h e", e=65)[:, :, 0:64]
                                src = ps.rearrange("p (h e) -> p h e", e=64)
                                bt = self.bias_tile[:, vc * 512:(vc + 1) * 512]
                                bt = bt.rearrange("p (h e) -> p h e", e=64)
                                return v.tensor_tensor(dst, src, bt, ALU.add)
                            self.t("vector", vev, waits=[self.at("s_pe")], inc="s_dve")
                            self.mark(f"vev_{gidx}", "s_dve")
                self.mark(f"kvmm_{phase}_{ic}", "s_pe")
                self.xlt_user[slot] = f"kvmm_{phase}_{ic}"
                self.emit_wquant(2)
        self.mark("kv_done", "s_dve")

    def stage1b(self):
        def ld_wq(e):
            last = None
            for kc in range(8):
                last = e.dma_start(out=self.bf("WSL", 1024, sub=kc * 2048),
                                   in_=self.wqkvT[kc * 128:(kc + 1) * 128, 0:1024])
            return last
        self.t("gpsimd", ld_wq, waits=["kv_done", self.at("s_pe")], inc="s_wld", n=8 * 16)
        self.mark("wq_ready", "s_wld")

        def zmax(v):
            v.memset(self.mq2, 0.0)
            return v.memset(self.mk2, 0.0)
        self.t("vector", zmax, inc="s_dve")

        for ic in range(ICN):
            slot = ic % 2
            def tld(e, ic=ic, slot=slot):
                last = None
                for kc in range(8):
                    last = e.dma_start(
                        out=self.bf("XLT", 512, sub=slot * 8192 + kc * 1024),
                        in_=self.xln_s[ic * 512:(ic + 1) * 512, kc * 128:(kc + 1) * 128],
                        transpose=True)
                return last
            waits = []
            if self.xlt_user[slot] is not None:
                waits.append(self.xlt_user[slot])
            self.t("sync", tld, waits=waits, inc=f"s_tp{slot}", n=8 * 16)
            tmark = self.at(f"s_tp{slot}")

            for cc in range(8):
                bank = cc % 2
                ps = self.psA.ap()[:, bank * 512:(bank + 1) * 512]
                waits = [tmark, "wq_ready"]
                if ic > 0 or cc > 1:
                    waits.append(self.at("s_dve2"))

                def qmm(e, cc=cc, ps=ps, slot=slot):
                    last = None
                    for kc in range(8):
                        last = e.matmul(ps, self.bf("WSL", 128, sub=kc * 2048 + cc * 256),
                                        self.bf("XLT", 512, sub=slot * 8192 + kc * 1024),
                                        start=(kc == 0), stop=(kc == 7))
                    return last
                self.t("tensor", qmm, waits=waits, inc="s_pe")
                qtile = self.bf("XLN", 512, sub=bank * 1024)

                def qev(v, cc=cc, ps=ps, qtile=qtile):
                    return v.tensor_scalar(qtile, ps, self.bQ_col[:, cc:cc + 1], 0.125,
                                           ALU.add, ALU.mult)
                qpar = f"s_qsp{(ic * 8 + cc) % 2}"
                evw = [self.at("s_pe"), self.at(qpar)]
                self.t("vector", qev, waits=evw, inc="s_dve")
                self.t("sync",
                       lambda e, cc=cc, ic=ic, qtile=qtile: e.dma_start(
                           out=self.qT_s[cc * 128:(cc + 1) * 128, ic * 512:(ic + 1) * 512],
                           in_=qtile),
                       waits=[self.at("s_dve")], inc=qpar, n=16)

                q2 = self.bf("X2", 512)
                self.t("vector",
                       lambda v, qtile=qtile, q2=q2: v.tensor_tensor(q2, qtile, qtile,
                                                                     ALU.mult),
                       waits=[self.at("s_dve")], inc="s_dve")
                for half in range(2):
                    h = cc * 2 + half
                    pss = self.psOA.ap()[0:1, :]

                    def qones(e, half=half, q2=q2, pss=pss):
                        return e.matmul(pss, self.ones_col[half * 64:(half + 1) * 64, :],
                                        q2[half * 64:(half + 1) * 64, :],
                                        start=True, stop=True)
                    self.t("tensor", qones,
                           waits=[self.at("s_dve"), self.at("s_dve2")], inc="s_pe")

                    def qmax(v, h=h, pss=pss):
                        v.reduce_max(self.sm1[:, 0:1], pss, AX.X)
                        return v.tensor_scalar(self.mq2[:, h:h + 1], self.mq2[:, h:h + 1],
                                               self.sm1[:, 0:1], None, ALU.max)
                    self.t("vector", qmax, waits=[self.at("s_pe")], inc="s_dve2")
            self.mark(f"qmm_{ic}", "s_pe")
            self.xlt_user[slot] = f"qmm_{ic}"
            self.emit_wquant(2)

        k2 = self.bf("X2", 512)
        for cc in range(8):
            for js in range(ICN):
                def ksq(v, cc=cc, js=js, k2=k2):
                    sl = self.bf("KT", 512, sub=cc * 8192 + js * 1024)
                    return v.tensor_tensor(k2, sl, sl, ALU.mult)
                self.t("vector", ksq, waits=[self.at("s_dve2")], inc="s_dve")
                for half in range(2):
                    h = cc * 2 + half
                    pss = self.psOA.ap()[0:1, :]

                    def kones(e, half=half, k2=k2, pss=pss):
                        return e.matmul(pss, self.ones_col[half * 64:(half + 1) * 64, :],
                                        k2[half * 64:(half + 1) * 64, :],
                                        start=True, stop=True)
                    self.t("tensor", kones,
                           waits=[self.at("s_dve"), self.at("s_dve2")], inc="s_pe")

                    def kmax(v, h=h, pss=pss):
                        v.reduce_max(self.sm1[:, 1:2], pss, AX.X)
                        return v.tensor_scalar(self.mk2[:, h:h + 1], self.mk2[:, h:h + 1],
                                               self.sm1[:, 1:2], None, ALU.max)
                    self.t("vector", kmax, waits=[self.at("s_pe")], inc="s_dve2")

        self.t("vector",
               lambda v: v.tensor_tensor(self.sm1[:, 16:32], self.mq2, self.mk2, ALU.mult),
               waits=[self.at("s_dve2")], inc="s_dve")
        self.t("scalar",
               lambda a: a.activation(self.sm1[:, 32:48], self.sm1[:, 16:32], AF.Sqrt),
               waits=[self.at("s_dve")], inc="s_act")
        self.t("vector",
               lambda v: v.tensor_scalar(self.cneg_bf, self.sm1[:, 32:48], -1.0, None,
                                         ALU.mult),
               waits=[self.at("s_act")], inc="s_dve")
        for h in range(16):
            psx = self.psOB.ap()[:, 0:1]

            def cbc(e, psx=psx, h=h):
                return e.matmul(psx, self.ones_row, self.cneg_bf[:, h:h + 1],
                                start=True, stop=True)
            self.t("tensor", cbc, waits=[self.at("s_dve"), self.at("s_dve2")], inc="s_pe")
            self.t("vector",
                   lambda v, h=h, psx=psx: v.tensor_copy(self.cneg[:, h:h + 1], psx),
                   waits=[self.at("s_pe")], inc="s_dve2")
        self.mark("cneg_ready", "s_dve2")
        self.mark("s1b_evacs", "s_dve")
        self.mark("s1b_evacs2", "s_dve2")

    def stage2(self):
        def ld_wo(e):
            last = None
            for dc in range(8):
                last = e.dma_start(out=self.bf("WSL", 1024, sub=dc * 2048),
                                   in_=self.woT[dc * 128:(dc + 1) * 128, :])
            return last
        self.t("gpsimd", ld_wo, waits=["kv_done", self.at("s_pe")], inc="s_wld", n=8 * 16)
        self.mark("wo_ready", "s_wld")

        self.t("sync",
               lambda e: e.dma_start(out=self.f32v("BOROW", 1024)[0:1, :],
                                     in_=self.bo.unsqueeze(0)),
               waits=[self.at("s_pe")], inc="s_ld", n=16)

        def bo_mm(e):
            br = self.f32v("BOROW", 1024)[0:1, :]
            e.matmul(self.psC.ap()[:, 0:512], self.ones_row_f, br[:, 0:512],
                     start=True, stop=True)
            return e.matmul(self.psC.ap()[:, 512:1024], self.ones_row_f,
                            br[:, 512:1024], start=True, stop=True)
        self.t("tensor", bo_mm,
               waits=["s1b_evacs2", "s1b_evacs", self.at("s_ld")], inc="s_pe")

        def bo_ev(v):
            return v.tensor_copy(self.bias_tile, self.psC.ap()[:])
        self.t("vector", bo_ev, waits=[self.at("s_pe")], inc="s_dve")
        self.mark("bo_ready", "s_dve")

        ring = [self.psA, self.psB, self.psC]
        ring_consumer = ["s1b_evacs", "kv_done", "bo_ready"]
        ring_i = [0]

        def ring_alloc():
            idx = ring_i[0] % 3
            ring_i[0] += 1
            return idx, ring_consumer[idx]

        psO_free = ["bo_ready"]
        pslot_user = [None, None, None]
        pcount = [0]
        tail = [None]
        pending_op = [None]
        s2xr_user = ["cneg_ready", "cneg_ready"]

        def emit_tail_recips():
            td = tail[0]

            def rsf(a):
                lns = self.f32v("OE", 512)[0:1, :]
                with self.nc.allow_low_precision(reason="1/s as bf16 matmul operand"):
                    a.activation(lns, self.psOA.ap()[64:65, :], AF.Ln)
                    a.activation(self.rs_bf[:, 0:512], lns, AF.Exp, scale=-1.0)
                    a.activation(lns, self.psOB.ap()[64:65, :], AF.Ln)
                    return a.activation(self.rs_bf[:, 512:1024], lns, AF.Exp,
                                        scale=-1.0)
            self.t("scalar", rsf, waits=[td["avdone"]], inc="s_act")
            td["rsm"] = self.at("s_act")

        def emit_tail_rest():
            td = tail[0]
            gi, pair = td["gi"], td["pair"]
            iR, wR = ring_alloc()
            sR = ring[iR]

            def rsbc(e, sR=sR):
                e.matmul(sR.ap()[0:64, 0:512], self.ones_row[0:1, 0:64],
                         self.rs_bf[:, 0:512], start=True, stop=True,
                         tile_position=(0, 0), skip_group_check=True)
                return e.matmul(sR.ap()[64:128, 0:512], self.ones_row[0:1, 0:64],
                                self.rs_bf[:, 512:1024], start=True, stop=True,
                                tile_position=(0, 64), skip_group_check=True)
            self.t("tensor", rsbc, waits=[td["rsm"], wR], inc="s_pe")

            oe = self.f32v("OE", 512)

            def oev(v, pair=pair, oe=oe, sR=sR):
                v.tensor_copy(oe[0:64, :], self.psOA.ap()[0:64, :])
                v.tensor_copy(oe[64:128, :], self.psOB.ap()[0:64, :])
                return v.tensor_tensor(self.bf("OT", 512, sub=pair * 1024), oe,
                                       sR.ap()[:, 0:512], ALU.mult)
            ow = [self.at("s_pe")]
            if pair == 0 and td["ic"] > 0:
                ow.append(f"opdone_{td['ic'] - 1}")
            self.t("vector", oev, waits=ow, inc="s_dve")
            self.mark(f"oev_{gi}", "s_dve")
            ring_consumer[iR] = f"oev_{gi}"
            psO_free[0] = f"oev_{gi}"
            tail[0] = None

        def emit_op_chunk():
            ic_o, isub = pending_op[0]["ic"], pending_op[0]["isub"]
            pending_op[0]["isub"] += 1
            if pending_op[0]["isub"] >= 4:
                pending_op[0] = None
            trow = ic_o * 512 + isub * 128
            slot = isub % 2
            xr = self.f32v("XR", 1024, sub=slot * 4096)
            self.t("sync",
                   lambda e, trow=trow, xr=xr: e.dma_start(
                       out=xr, in_=self.x[trow:trow + 128, :]),
                   waits=[s2xr_user[slot]], inc=f"s_xld{slot}", n=16)
            iO, wO = ring_alloc()
            sO = ring[iO]

            def opmm(e, isub=isub, sO=sO):
                last = None
                for e2 in range(2):
                    for p in range(8):
                        last = e.matmul(
                            sO.ap()[:, e2 * 512:(e2 + 1) * 512],
                            self.bf("OT", 128, sub=p * 1024 + isub * 256),
                            self.bf("WSL", 512, sub=p * 2048 + e2 * 1024),
                            start=(p == 0), stop=(p == 7))
                return last
            self.t("tensor", opmm,
                   waits=[f"oev_{ic_o * NPAIR + NPAIR - 1}", "wo_ready", wO],
                   inc="s_pe")
            if isub == 3:
                self.mark(f"opdone_{ic_o}", "s_pe")
            opm = self.at("s_pe")
            x1t = self.f32v("X1T", 1024)

            def opev(v, x1t=x1t, xr=xr, sO=sO):
                t1 = self.f32v("T1", 512)
                last = None
                for e2 in range(2):
                    ps = sO.ap()[:, e2 * 512:(e2 + 1) * 512]
                    v.tensor_tensor(t1, ps, self.bias_tile[:, e2 * 512:(e2 + 1) * 512],
                                    ALU.add)
                    last = v.tensor_tensor(x1t[:, e2 * 512:(e2 + 1) * 512], t1,
                                           xr[:, e2 * 512:(e2 + 1) * 512], ALU.add)
                return last
            self.t("vector", opev,
                   waits=[opm, self.at(f"s_xld{slot}"), self.at("s_x1sp")],
                   inc="s_dve")
            self.mark(f"opev_{ic_o}_{isub}", "s_dve")
            ring_consumer[iO] = f"opev_{ic_o}_{isub}"
            s2xr_user[slot] = f"opev_{ic_o}_{isub}"
            self.t("sync",
                   lambda e, trow=trow, x1t=x1t: e.dma_start(
                       out=self.x1_s[trow:trow + 128, :], in_=x1t),
                   waits=[self.at("s_dve")], inc="s_x1sp", n=16)
            if isub == 3:
                self.mark(f"x1spm_{ic_o}", "s_x1sp")

        for ic in range(ICN):
            for pair in range(NPAIR):
                hA, hB = 2 * pair, 2 * pair + 1
                gi = ic * NPAIR + pair
                qslot = gi % 2
                qs = self.bf("QS", 512, sub=qslot * 1024)
                waits = [self.at("s_qsp0"), self.at("s_qsp1"), "cneg_ready", "bo_ready"]
                if gi >= 2:
                    waits.append(self.marks[f"qfree_{gi - 2}"])
                qldsem = f"s_tpq{gi % 2}"
                self.t("sync",
                       lambda e, pair=pair, ic=ic, qs=qs: e.dma_start(
                           out=qs, in_=self.qT_s[pair * 128:(pair + 1) * 128,
                                                 ic * 512:(ic + 1) * 512]),
                       waits=waits, inc=qldsem, n=16)
                qld = self.at(qldsem)

                ngrp = JCN // JG
                pend = []

                def emit_attnv(jg, pslot, gi=gi, hA=hA, hB=hB):
                    pa = self.bf("PA", 1024, sub=pslot * 2048)
                    pb = self.bf("PB", 1024, sub=pslot * 2048)

                    def avmm(e, jg=jg, pa=pa, pb=pb):
                        last = None
                        for u in range(JG):
                            jc = jg * JG + u
                            st, sp = (jc == 0), (jc == JCN - 1)
                            vA = self.bf("V", 65, sub=(jc * 16 + hA) * 130)
                            vB = self.bf("V", 65, sub=(jc * 16 + hB) * 130)
                            e.matmul(self.psOA.ap()[0:65, :], vA,
                                     pa[:, u * 512:(u + 1) * 512], start=st, stop=sp)
                            last = e.matmul(self.psOB.ap()[0:65, :], vB,
                                            pb[:, u * 512:(u + 1) * 512],
                                            start=st, stop=sp)
                        return last
                    awaits = [f"expA_{gi}_{jg}", f"expB_{gi}_{jg}"]
                    if jg == 0:
                        awaits.append(psO_free[0])
                    self.t("tensor", avmm, waits=awaits, inc="s_pe")
                    self.mark(f"av_{gi}_{jg}", "s_pe")

                for jg in range(ngrp):
                    pslot = pcount[0] % 3
                    pcount[0] += 1
                    iA, wA = ring_alloc()
                    iB, wB = ring_alloc()
                    sA, sB = ring[iA], ring[iB]

                    def smm(e, jg=jg, pair=pair, qs=qs, sA=sA, sB=sB):
                        last = None
                        for u in range(JG):
                            jc = jg * JG + u
                            kv = self.bf("KT", 128, sub=pair * 8192 + jc * 256)
                            last = e.matmul(sA.ap()[:, u * 512:(u + 1) * 512],
                                            kv[0:64, :], qs[0:64, :],
                                            start=True, stop=True,
                                            tile_position=(0, 0))
                            last = e.matmul(sB.ap()[:, u * 512:(u + 1) * 512],
                                            kv[64:128, :], qs[64:128, :],
                                            start=True, stop=True,
                                            tile_position=(64, 0))
                        return last
                    self.t("tensor", smm, waits=[qld, wA, wB], inc="s_pe")
                    smark = self.at("s_pe")

                    pa = self.bf("PA", 1024, sub=pslot * 2048)
                    pb = self.bf("PB", 1024, sub=pslot * 2048)
                    ewA = [smark]
                    ewB = []
                    if pslot_user[pslot] is not None:
                        ewA.append(pslot_user[pslot])
                    pslot_user[pslot] = f"av_{gi}_{jg}"
                    self.t("scalar",
                           lambda a, pa=pa, hA=hA, sA=sA: a.activation(
                               pa, sA.ap()[:], AF.Exp, bias=self.cneg[:, hA:hA + 1]),
                           waits=ewA, inc="s_act")
                    self.mark(f"expA_{gi}_{jg}", "s_act")
                    ring_consumer[iA] = f"expA_{gi}_{jg}"
                    self.t("scalar",
                           lambda a, pb=pb, hB=hB, sB=sB: a.activation(
                               pb, sB.ap()[:], AF.Exp, bias=self.cneg[:, hB:hB + 1]),
                           waits=ewB, inc="s_act")
                    self.mark(f"expB_{gi}_{jg}", "s_act")
                    ring_consumer[iB] = f"expB_{gi}_{jg}"

                    pend.append((jg, pslot))
                    if len(pend) > 2:
                        emit_attnv(*pend.pop(0))
                    if jg == 0 and tail[0] is not None:
                        emit_tail_recips()
                    elif jg == 1 and tail[0] is not None:
                        emit_tail_rest()
                    elif jg in (2, 4, 6, 8) and pending_op[0] is not None:
                        emit_op_chunk()
                    elif jg in (3, 7, 11):
                        self.emit_wquant(1, pool=True)
                while pend:
                    emit_attnv(*pend.pop(0))
                self.mark(f"qfree_{gi}", "s_pe")
                tail[0] = {"gi": gi, "pair": pair, "ic": ic,
                           "avdone": self.at("s_pe")}

            pending_op[0] = {"ic": ic, "isub": 0}

        emit_tail_recips()
        emit_tail_rest()
        while pending_op[0] is not None:
            emit_op_chunk()
        self.mark("stage2_done", "s_x1sp")
        self.mark("attn_pe_done", "s_pe")
        self.mark("stage2_dve", "s_dve")
        self.mark("stage2_act", "s_act")

    def stage3(self):
        while self.wq_idx < len(self.wq_jobs):
            self.emit_wquant(1)

        def ld_qw(e):
            last = None
            for kc in range(8):
                last = e.dma_start(out=self.bf("QWG", 2816, sub=kc * 5632),
                                   in_=self.qwg_s[:, kc * 128:(kc + 1) * 128],
                                   transpose=True)
                last = e.dma_start(out=self.bf("QWU", 2816, sub=kc * 5632),
                                   in_=self.qwu_s[:, kc * 128:(kc + 1) * 128],
                                   transpose=True)
            for icc in range(22):
                last = e.dma_start(out=self.bf("QWD", 1024, sub=icc * 2048),
                                   in_=self.qwd_s[:, icc * 128:(icc + 1) * 128],
                                   transpose=True)
            return last
        self.t("sync", ld_qw,
               waits=[self.at("s_wsp0"), self.at("s_wsp1"), "attn_pe_done",
                      "stage2_done"],
               inc="s_tpw", n=38 * 16)
        self.mark("qw_ready", "s_tpw")

        CCH = [(i * 512, min(512, II - i * 512)) for i in range((II + 511) // 512)]
        self.xr_user = ["stage2_dve", "stage2_dve"]

        def emit_ln2(tc):
            for ts_ in range(4):
                trow = tc * 512 + ts_ * 128
                t = tc * 4 + ts_
                slot = t % 2
                xr = self.f32v("XR", 1024, sub=slot * 4096)
                lw = [self.xr_user[slot], f"x1spm_{tc}"]
                self.t("sync",
                       lambda e, trow=trow, xr=xr: e.dma_start(
                           out=xr, in_=self.x1_s[trow:trow + 128, :]),
                       waits=lw, inc=f"s_xld{slot}", n=16)
                bns0 = self.f32v("BNS", 6, sub=slot * 64)
                bns1 = self.f32v("BNS", 6, sub=slot * 64 + 24)
                bns = self.f32v("BNS", 12, sub=slot * 64)
                mean = self.f32v("BNV", 1, sub=slot * 64)
                var = self.f32v("BNV", 1, sub=slot * 64 + 4)
                mv = self.f32v("BNV", 2, sub=slot * 64)
                rstd = self.f32v("BNV", 1, sub=slot * 64 + 8)
                rstd2 = self.f32v("BNV", 1, sub=slot * 64 + 12)

                def ln(v, xr=xr, bns0=bns0, bns1=bns1, bns=bns, mv=mv):
                    v.bn_stats(bns0, xr[:, 0:512])
                    v.bn_stats(bns1, xr[:, 512:1024])
                    return v.bn_aggr(mv, bns)
                self.t("vector", ln, waits=[self.at(f"s_xld{slot}")], inc="s_dve")
                self.t("scalar",
                       lambda a, var=var, rstd=rstd: a.activation(
                           rstd, var, AF.Sqrt, bias=self.f32v("ST", 1, sub=128)),
                       waits=[self.at("s_dve")], inc="s_act")
                self.mark(f"sqrt3_{t}", "s_act")
                x2 = self.f32v("X2", 1024)

                def norm(v, xr=xr, mean=mean, rstd=rstd, rstd2=rstd2, x2=x2):
                    v.reciprocal(rstd2, rstd)
                    return v.tensor_scalar(x2, xr, mean, rstd2,
                                           ALU.subtract, ALU.mult)
                nw = [f"sqrt3_{t}"]
                if t == 0:
                    nw.append("stage2_done")
                self.t("vector", norm, waits=nw, inc="s_dve")
                self.mark(f"ln3n_{t}", "s_dve")
                self.xr_user[slot] = f"ln3n_{t}"

                xq = self.bf("XLN", 1024)
                qw0 = [self.at("s_xqsp")]
                for half in range(2):
                    self.emit_quant2(x2[:, half * 512:(half + 1) * 512],
                                     xq[:, half * 512:(half + 1) * 512], 512,
                                     waits_first=qw0 if half == 0 else (),
                                     inc_last="s_dve", pool=False)
                self.t("sync",
                       lambda e, trow=trow, xq=xq: e.dma_start(
                           out=self.xq_s[trow:trow + 128, :], in_=xq),
                       waits=[self.at("s_dve")], inc="s_xqsp", n=16)

        def emit_xqtld(tc):
            def xqtld(e, tc=tc):
                last = None
                for kc in range(8):
                    last = e.dma_start(
                        out=self.bf("XQT", 512, sub=kc * 1024),
                        in_=self.xq_s[tc * 512:(tc + 1) * 512, kc * 128:(kc + 1) * 128],
                        transpose=True)
                return last
            waits = [self.at("s_xqsp")]
            if tc == 0:
                waits += ["attn_pe_done", "qw_ready"]
            else:
                waits.append(f"gu_done_{tc - 1}")
            self.t("sync", xqtld, waits=waits, inc="s_tpx", n=8 * 16)
            self.mark(f"xqm_{tc}", "s_tpx")

        def emit_hqtld(tc, part, i0, i1):
            def hqtld(e, tc=tc, i0=i0, i1=i1):
                last = None
                for icc in range(i0, i1):
                    last = e.dma_start(
                        out=self.bf("HQT", 512, sub=icc * 1024),
                        in_=self.hq_s[tc * 512:(tc + 1) * 512, icc * 128:(icc + 1) * 128],
                        transpose=True)
                return last
            waits = [self.at("s_hqsp0"), self.at("s_hqsp1")]
            if tc > 0:
                waits.append(f"dmm{part}_{tc - 1}")
            self.t("sync", hqtld, waits=waits, inc="s_tph", n=(i1 - i0) * 16)
            self.mark(f"hqld{part}_{tc}", "s_tph")

        def emit_gu(tc):
            for ci, (c0, cw) in enumerate(CCH):
                for ts_ in range(4):
                    ch = tc * 24 + ci * 4 + ts_

                    def gmm(e, ts_=ts_, c0=c0, cw=cw):
                        last = None
                        for kc in range(8):
                            last = e.matmul(
                                self.psOA.ap()[:, 0:cw],
                                self.bf("XQT", 128, sub=kc * 1024 + ts_ * 256),
                                self.bf("QWG", cw, sub=kc * 5632 + c0 * 2),
                                start=(kc == 0), stop=(kc == 7))
                        return last
                    gw = [f"xqm_{tc}"]
                    if ch == 0:
                        gw.append("stage2_dve")
                    else:
                        gw.append(f"silu_{ch - 1}")
                    self.t("tensor", gmm, waits=gw, inc="s_pe")
                    gm = self.at("s_pe")
                    sgt = self.f32v("SGT", 512)
                    sw = [gm]
                    if ch > 0:
                        sw.append(f"hm_{ch - 1}")
                    self.t("scalar",
                           lambda a, cw=cw, sgt=sgt: a.activation(
                               sgt[:, 0:cw], self.psOA.ap()[:, 0:cw], AF.Silu),
                           waits=sw, inc="s_act")
                    self.mark(f"silu_{ch}", "s_act")

                    def umm(e, ts_=ts_, c0=c0, cw=cw):
                        last = None
                        for kc in range(8):
                            last = e.matmul(
                                self.psOB.ap()[:, 0:cw],
                                self.bf("XQT", 128, sub=kc * 1024 + ts_ * 256),
                                self.bf("QWU", cw, sub=kc * 5632 + c0 * 2),
                                start=(kc == 0), stop=(kc == 7))
                        return last
                    uw = [f"xqm_{tc}"]
                    if ch == 0:
                        uw.append("stage2_dve")
                    else:
                        uw.append(f"hm_{ch - 1}")
                    self.t("tensor", umm, waits=uw, inc="s_pe")
                    um = self.at("s_pe")
                    ht = self.f32v("HT", 512)

                    def hmul(v, cw=cw, sgt=sgt, ht=ht):
                        return v.tensor_tensor(ht[:, 0:cw], sgt[:, 0:cw],
                                               self.psOB.ap()[:, 0:cw], ALU.mult)
                    self.t("vector", hmul, waits=[um, f"silu_{ch}"], inc="s_dve")
                    self.mark(f"hm_{ch}", "s_dve")
                    hq = self.bf("HQ", 512, sub=(ch % 2) * 1024)
                    hqsem = f"s_hqsp{ch % 2}"
                    self.emit_quant2(ht[:, 0:cw], hq[:, 0:cw], cw,
                                     waits_first=[self.at(hqsem)],
                                     inc_last="s_dve", pool=False)
                    trow = tc * 512 + ts_ * 128
                    self.t("sync",
                           lambda e, trow=trow, c0=c0, cw=cw, hq=hq: e.dma_start(
                               out=self.hq_s[trow:trow + 128, c0:c0 + cw],
                               in_=hq[:, 0:cw]),
                           waits=[self.at("s_dve")], inc=hqsem, n=16)
                if ci == 2:
                    emit_hqtld(tc, "A", 0, 12)
            emit_hqtld(tc, "B", 12, 22)
            self.mark(f"gu_done_{tc}", "s_pe")

        def emit_dmm(tc):
            for ts_ in range(4):
                t = tc * 4 + ts_
                slot = t % 2
                trow = tc * 512 + ts_ * 128
                xr = self.f32v("XR", 1024, sub=slot * 4096)
                self.t("sync",
                       lambda e, trow=trow, xr=xr: e.dma_start(
                           out=xr, in_=self.x1_s[trow:trow + 128, :]),
                       waits=[self.xr_user[slot]], inc=f"s_xld{slot}", n=16)
                for part, i0, i1 in [("A", 0, 12), ("B", 12, 22)]:
                    for e2 in range(2):
                        ps = self.psA.ap()[:, e2 * 512:(e2 + 1) * 512]

                        def dmm(e, ts_=ts_, e2=e2, i0=i0, i1=i1, ps=ps):
                            last = None
                            for icc in range(i0, i1):
                                last = e.matmul(
                                    ps, self.bf("HQT", 128, sub=icc * 1024 + ts_ * 256),
                                    self.bf("QWD", 512, sub=icc * 2048 + e2 * 1024),
                                    start=(icc == 0), stop=(icc == 21))
                            return last
                        dw = [f"hqld{part}_{tc}"]
                        if part == "A" and e2 == 0:
                            if ts_ > 0:
                                dw.append(f"dev3_{tc}_{ts_ - 1}")
                            elif tc > 0:
                                dw.append(f"dev3_{tc - 1}_3")
                            else:
                                dw += ["stage2_act", "stage2_dve"]
                        self.t("tensor", dmm, waits=dw, inc="s_pe")
                    self.mark(f"dmm{part}_{tc}", "s_pe")
                outt = self.f32v("OUTT", 1024)

                def dev(v, outt=outt, xr=xr):
                    return v.tensor_tensor(outt, self.psA.ap()[:], xr, ALU.add)
                self.t("vector", dev,
                       waits=[self.at("s_pe"), self.at(f"s_xld{slot}"),
                              self.at("s_out")],
                       inc="s_dve")
                self.mark(f"dev3_{tc}_{ts_}", "s_dve")
                self.xr_user[slot] = f"dev3_{tc}_{ts_}"
                self.t("sync",
                       lambda e, trow=trow, outt=outt: e.dma_start(
                           out=self.out[trow:trow + 128, :], in_=outt),
                       waits=[self.at("s_dve")], inc="s_out", n=16)

        emit_ln2(0)
        emit_xqtld(0)
        for tc in range(TCN):
            if tc + 1 < TCN:
                emit_ln2(tc + 1)
            emit_gu(tc)
            if tc + 1 < TCN:
                emit_xqtld(tc + 1)
            emit_dmm(tc)

        self.t("sync", lambda e: None, waits=[self.at("s_out")])

    def finalize(self):
        nc = self.nc
        per_eng = {"sync": [], "tensor": [], "vector": [], "scalar": [], "gpsimd": []}
        for task in self.tasks:
            key = {"pool": "gpsimd", "scalardma": "scalar"}.get(task.eng, task.eng)
            per_eng[key].append(task)
        last_wait = {}

        class DmaProxy:
            def __init__(self, eng, sem):
                self._eng, self._sem = eng, sem
                self.count = 0

            def dma_start(self, **kw):
                self.count += 1
                return self._eng.dma_start(**kw).then_inc(self._sem, 16)

            def __getattr__(self, name):
                return getattr(self._eng, name)

        class SerialProxy:

            _EXEMPT = {"wait_ge", "then_inc", "dma_start"}

            def __init__(self, eng, selfsem):
                object.__setattr__(self, "_eng", eng)
                object.__setattr__(self, "_sem", selfsem)
                object.__setattr__(self, "_count", 0)
                object.__setattr__(self, "_prev", None)
                object.__setattr__(self, "_token", None)
                object.__setattr__(self, "_emitted", {})

            def _pre_op(self):
                if self._prev is not None:
                    object.__setattr__(self, "_count", self._count + 1)
                    self._prev.then_inc(self._sem, 1)
                    object.__setattr__(self, "_token", (self._sem, self._count))
                    object.__setattr__(self, "_prev", None)
                if self._token is not None:
                    self._eng.wait_ge(*self._token)
                    object.__setattr__(self, "_token", None)

            def wait_ge(self, sem, val):
                return self._eng.wait_ge(sem, val)

            def finish_task(self, inc_sem, inc_n, planned_total):
                ins = self._prev
                assert ins is not None
                ins.then_inc(inc_sem, inc_n)
                object.__setattr__(self, "_prev", None)
                object.__setattr__(self, "_token", (inc_sem, planned_total))

            def __getattr__(self, name):
                target = getattr(self._eng, name)
                if name.startswith("_") or not callable(target):
                    return target

                def wrapped(*a, **kw):
                    self._pre_op()
                    ins = target(*a, **kw)
                    object.__setattr__(self, "_prev", ins)
                    return ins
                return wrapped

        def runner(eng_name):
            def run(eng_obj):
                emitted = {}
                proxy = None
                if eng_name in ("vector", "scalar", "gpsimd"):
                    proxy = SerialProxy(eng_obj, self.sems[f"s_self_{eng_name}"])
                for task in per_eng[eng_name]:
                    for s, vv in task.waits:
                        if vv <= 0:
                            continue
                        key = (eng_name, s)
                        if last_wait.get(key, -1) >= vv:
                            continue
                        last_wait[key] = vv
                        eng_obj.wait_ge(self.sems[s], vv)
                    if task.inc is not None and task.eng in ("sync", "gpsimd",
                                                              "scalardma"):
                        dproxy = DmaProxy(eng_obj, self.sems[task.inc])
                        task.fn(dproxy)
                        assert dproxy.count * 16 == task.n, \
                            f"{eng_name} dma count {dproxy.count}*16 != {task.n}"
                    elif proxy is not None:
                        ret = task.fn(proxy)
                        if task.inc is not None:
                            assert ret is not None, f"{eng_name} needs instr ({task.inc})"
                            emitted[task.inc] = emitted.get(task.inc, 0) + task.n
                            proxy.finish_task(self.sems[task.inc], task.n,
                                              emitted[task.inc])
                    else:
                        ret = task.fn(eng_obj)
                        if task.inc is not None:
                            assert ret is not None, f"{eng_name} needs instr ({task.inc})"
                            ret.then_inc(self.sems[task.inc], task.n)
            return run

        self.sem("s_self_vector")
        self.sem("s_self_scalar")
        self.sem("s_self_gpsimd")
        blk = self._es.enter_context(nc.Block())
        blk.sync(runner("sync"))
        blk.tensor(runner("tensor"))
        blk.vector(runner("vector"))
        blk.scalar(runner("scalar"))
        blk.gpsimd(runner("gpsimd"))
        self._es.close()
        return nc


def build(debug=False):
    b = Builder(debug=debug)
    b.schedule()
    return b.finalize()


_cached = {}


def kernel(L0_attn_x, L0_attn_W_w, L0_attn_W_b, L0_attn_out_w, L0_attn_out_b,
           L0_ffn_w_gate, L0_ffn_w_up, L0_ffn_w_down):
    x = np.asarray(L0_attn_x, dtype=np.float32)
    wqkvT = np.ascontiguousarray(np.asarray(L0_attn_W_w, np.float32).T)
    woT = np.ascontiguousarray(np.asarray(L0_attn_out_w, np.float32).T)
    bqkv = np.ascontiguousarray(np.asarray(L0_attn_W_b, np.float32))
    bo = np.ascontiguousarray(np.asarray(L0_attn_out_b, np.float32))
    wg = np.ascontiguousarray(np.asarray(L0_ffn_w_gate, np.float32))
    wu = np.ascontiguousarray(np.asarray(L0_ffn_w_up, np.float32))
    wd = np.ascontiguousarray(np.asarray(L0_ffn_w_down, np.float32))
    ones_col = np.ones((128, 1), ml_dtypes.bfloat16)
    ones_row = np.ones((1, 128), ml_dtypes.bfloat16)

    if "nc" not in _cached:
        _cached["nc"] = build(debug=False)
    nc = _cached["nc"]
    in_maps = []
    for b_ in range(8):
        in_maps.append({
            "x": np.ascontiguousarray(x[b_]),
            "wqkvT": wqkvT, "woT": woT, "bqkv": bqkv, "bo": bo,
            "wg": wg, "wu": wu, "wd": wd,
            "ones_col": ones_col, "ones_row": ones_row,
        })
    res = run_bass_kernel_spmd(nc, in_maps, core_ids=list(range(8)))
    return np.stack([r["out"] for r in res.results], axis=0)


# version 12
# speedup vs baseline: 1.3191x; 1.3191x over previous
import contextlib
import numpy as np
import ml_dtypes
import concourse.bass as bass
import concourse.mybir as mybir
from concourse.bass_utils import run_bass_kernel_spmd

F32 = mybir.dt.float32
BF16 = mybir.dt.bfloat16
I32 = mybir.dt.int32
U8 = mybir.dt.uint8
ALU = mybir.AluOpType
AF = mybir.ActivationFunctionType
AX = mybir.AxisListType

T, D, H, DH, II = 4096, 1024, 16, 64, 2816
ICN, ICW = 8, 512
JCN = T // 128
TCN, TCW = 8, 512
NPAIR = H // 2
JG = 2


def fbits(x):
    return int(np.float32(x).view(np.int32))


B_0875, B_025, B_075 = fbits(0.875), fbits(0.25), fbits(0.75)
MAGIC_ADD = 0x00A00000
MASK_TOP = -4194304


class Task:
    __slots__ = ("eng", "fn", "waits", "inc", "n")

    def __init__(self, eng, fn, waits, inc, n):
        self.eng, self.fn, self.waits, self.inc, self.n = eng, fn, waits, inc, n


class Builder:
    def __init__(self, debug=False):
        self.debug = debug
        self.nc = bass.Bass("TRN2", target_bir_lowering=False, debug=False)
        self.tasks = []
        self.tot = {}
        self.sems = {}
        self.marks = {}
        self._q_tmp_user = None
        self._es = contextlib.ExitStack()
        self._declare()

    def sem(self, name):
        if name not in self.sems:
            self.sems[name] = self._es.enter_context(self.nc.semaphore(name))
            self.tot[name] = 0
        return self.sems[name]

    def at(self, semname):
        self.sem(semname)
        return (semname, self.tot[semname])

    def t(self, eng, fn, waits=(), inc=None, n=1):
        ws = [self.marks[w] if isinstance(w, str) else w for w in waits]
        for s, v in ws:
            assert v <= self.tot.get(s, 0), f"forward wait {s}:{v}>{self.tot.get(s, 0)}"
        if inc is not None:
            self.sem(inc)
            self.tot[inc] += n
        self.tasks.append(Task(eng, fn, ws, inc, n))

    def mark(self, name, semname):
        self.sem(semname)
        self.marks[name] = (semname, self.tot[semname])

    def _declare(self):
        nc = self.nc
        ein, eout = "ExternalInput", "ExternalOutput"
        scr = eout if self.debug else "Internal"
        d = lambda name, shape, dt, kind: nc.dram_tensor(name, shape, dt, kind=kind).ap()
        self.x = d("x", [T, D], F32, ein)
        self.wqkvT = d("wqkvT", [D, 3 * D], F32, ein)
        self.woT = d("woT", [D, D], F32, ein)
        self.bqkv = d("bqkv", [3 * D], F32, ein)
        self.bo = d("bo", [D], F32, ein)
        self.wg = d("wg", [II, D], F32, ein)
        self.wu = d("wu", [II, D], F32, ein)
        self.wd = d("wd", [D, II], F32, ein)
        self.ones_col_d = d("ones_col", [128, 1], BF16, ein)
        self.ones_row_d = d("ones_row", [1, 128], BF16, ein)
        self.out = d("out", [T, D], F32, eout)
        self.xln_s = d("xln_s", [T, D], BF16, scr)
        self.qT_s = d("qT_s", [D, T], BF16, scr)
        self.x1_s = d("x1_s", [T, D], F32, scr)
        self.xq_s = d("xq_s", [T, D], BF16, scr)
        self.hq_s = d("hq_s", [T, II], BF16, scr)
        self.qwg_s = d("qwg_s", [II, D], BF16, scr)
        self.qwu_s = d("qwu_s", [II, D], BF16, scr)
        self.qwd_s = d("qwd_s", [D, II], BF16, scr)

        AREN = 157696
        self.arena = nc.alloc_sbuf_tensor("arena", [128, AREN], U8)
        self.off = {}
        for name, off, nb in [
            ("KT", 0, 65536),
            ("V", 65536, 66560),
            ("WSL", 132096, 16384),
            ("OT", 148480, 8192),
            ("QWG", 0, 45056), ("QWU", 45056, 45056), ("QWD", 90112, 45056),
            ("HQT", 135168, 22528),
        ]:
            assert off + nb <= AREN
            self.off[name] = ("A", off, nb)

        SCRN = 49152
        self.scrb = nc.alloc_sbuf_tensor("scrb", [128, SCRN], U8)
        o = 0

        def sseg(name, nb):
            nonlocal o
            o = (o + 63) // 64 * 64
            assert o + nb <= SCRN, (name, o, nb)
            self.off[name] = ("S", o, nb)
            o += nb

        for n_, nb in [("WF", 2048), ("WQX", 2048), ("WQA", 2048), ("WQG", 2048),
                       ("WQT", 2048), ("WQC", 2048), ("WQM", 2048), ("WQO", 1024),
                       ("WQSC", 512), ("BNS", 128), ("BNV", 128),
                       ("XR", 8192), ("X2", 4096), ("XLN", 2048), ("ST", 192),
                       ("UN", 18432)]:
            sseg(n_, nb)
        u = self.off["UN"][1]
        for name, du, nb in [
            ("XLT", 0, 16384),
            ("ST2", 16384, 192),
            ("QS", 16384, 2048), ("PA", 0, 6144), ("PB", 6144, 6144),
            ("BOROW", 0, 4096),
            ("OE", 12288, 2048), ("T1", 12288, 2048), ("RSBF", 14336, 2048),
            ("XQT", 0, 8192), ("SGT", 8192, 2048), ("HT", 10240, 2048),
            ("HQ", 12288, 2048), ("OUTT", 14336, 4096),
        ]:
            assert du + nb <= 18432
            self.off[name] = ("S", u + du, nb)
        self.off["BVROW"] = ("S", self.off["XR"][1] + 4096, 4096)
        self.off["X1T"] = self.off["X2"]

        self.psA = self._es.enter_context(nc.psum_tensor("psA", [128, 1024], F32))
        self.psB = self._es.enter_context(nc.psum_tensor("psB", [128, 1024], F32))
        self.psC = self._es.enter_context(nc.psum_tensor("psC", [128, 1024], F32))
        self.psOA = self._es.enter_context(nc.psum_tensor("psOA", [128, 512], F32))
        self.psOB = self._es.enter_context(nc.psum_tensor("psOB", [128, 512], F32))

        sb = nc.alloc_sbuf_tensor
        self.ones_col = sb("ones_col_t", [128, 1], BF16).ap()
        self.ones_row = sb("ones_row_t", [1, 128], BF16).ap()
        self.ones_row_f = sb("ones_row_f", [1, 128], F32).ap()
        self.bK_col = sb("bK_col", [128, 8], F32).ap()
        self.bQ_col = sb("bQ_col", [128, 8], F32).ap()
        self.bias_tile = sb("bias_tile", [128, 1024], F32).ap()
        self.cneg = sb("cneg", [128, 16], F32).ap()
        self.mq2 = sb("mq2", [1, 16], F32).ap()
        self.mk2 = sb("mk2", [1, 16], F32).ap()
        self.sm1 = sb("sm1", [1, 64], F32).ap()
        rb_kind, rb_off, rb_nb = self.off["RSBF"]
        self.rs_bf = self.scrb.ap()[0:1, rb_off:rb_off + rb_nb].bitcast(BF16)
        self.cneg_bf = sb("cneg_bf", [1, 16], BF16).ap()

    def _view(self, name, esz, cols, sub):
        kind, off, nb = self.off[name]
        base = self.arena if kind == "A" else self.scrb
        assert sub + cols * esz <= nb, (name, sub, cols)
        return base.ap()[:, off + sub:off + sub + cols * esz]

    def bf(self, name, cols, sub=0):
        return self._view(name, 2, cols, sub).bitcast(BF16)

    def f32v(self, name, cols, sub=0):
        return self._view(name, 4, cols, sub).bitcast(F32)

    def i32v(self, name, cols, sub=0):
        return self._view(name, 4, cols, sub).bitcast(I32)

    def u8v(self, name, cols, sub=0):
        return self._view(name, 1, cols, sub)

    def emit_quant2(self, src_f32, dst_bf16, F, waits_first=(), waits_last=(),
                    inc_last=None, pool=True):
        nb = F // 16
        bm = self.f32v("WQSC", nb, sub=0)
        s_ = self.f32v("WQSC", nb, sub=128)
        rs = self.f32v("WQSC", nb, sub=256)
        sh = self.f32v("WQSC", nb, sub=384)
        xn, ai = self.f32v("WQX", F), self.i32v("WQA", F)
        sg = self.f32v("WQG", F)
        tt, cc = self.i32v("WQT", F), self.f32v("WQC", F)
        mm = self.f32v("WQM", F)
        sv = src_f32.rearrange("p (n b) -> p n b", b=16)
        rsb = rs.unsqueeze(2).broadcast_to([128, nb, 16])
        shb = sh.unsqueeze(2).broadcast_to([128, nb, 16])
        dv = dst_bf16.rearrange("p (n b) -> p n b", b=16)
        xnv = xn.rearrange("p (n b) -> p n b", b=16)
        mmv = mm.rearrange("p (n b) -> p n b", b=16)
        fin = inc_last if inc_last is not None else ("s_qp" if pool else "s_qv")
        tw = list(waits_first)
        if self._q_tmp_user is not None:
            tw.append(self._q_tmp_user)

        if pool:
            def A(v):
                v.tensor_reduce(bm, sv, AX.X, ALU.max, apply_absolute_value=True)
                v.tensor_scalar(s_, bm, 1.0 / 6.0, 1e-12, ALU.mult, ALU.max)
                v.reciprocal(rs, s_)
                return v.tensor_scalar(sh, s_, 0.5, None, ALU.mult)
            self.t("vector", A, waits=tw, inc="s_qv")

            def B(g):
                return g.tensor_tensor(xnv, sv, rsb, ALU.mult)
            self.t("pool", B, waits=[self.at("s_qv")], inc="s_qp")
            srcfree = self.at("s_qp")

            def C(v):
                v.tensor_scalar(ai, xn.bitcast(I32), 0x7FFFFFFF, None, ALU.bitwise_and)
                v.tensor_scalar(sg.bitcast(I32), xn.bitcast(I32), -2147483648,
                                1065353216, ALU.bitwise_and, ALU.bitwise_or)
                v.tensor_scalar(tt, ai, B_0875, MAGIC_ADD, ALU.max, ALU.add)
                return v.tensor_scalar(tt, tt, MASK_TOP, None, ALU.bitwise_and)
            self.t("vector", C, waits=[self.at("s_qp")], inc="s_qv")

            def D(g):
                g.tensor_scalar(cc, ai.bitcast(F32), 0.25, None, ALU.is_gt)
                g.tensor_scalar(mm, ai.bitcast(F32), 0.75, None, ALU.is_gt)
                g.tensor_scalar(xn, tt.bitcast(F32), -1.0, None, ALU.add)
                g.tensor_tensor(mm, mm, xn, ALU.mult)
                g.tensor_add(mm, mm, cc)
                g.tensor_tensor(mm, mm, sg, ALU.mult)
                return g.tensor_tensor(dv, mmv, shb, ALU.mult)
            self.t("pool", D, waits=[self.at("s_qv")] + list(waits_last), inc=fin)
            self._q_tmp_user = self.at(fin)
            return srcfree, self._q_tmp_user

        u8m = self.u8v("WQM", F)

        def E(v):
            v.tensor_reduce(bm, sv, AX.X, ALU.max, apply_absolute_value=True)
            v.tensor_scalar(s_, bm, 1.0 / 6.0, 1e-12, ALU.mult, ALU.max)
            v.reciprocal(rs, s_)
            v.tensor_scalar(sh, s_, 0.5, None, ALU.mult)
            v.tensor_tensor(xnv, sv, rsb, ALU.mult)
            v.tensor_scalar(ai, xn.bitcast(I32), 0x7FFFFFFF, None, ALU.bitwise_and)
            v.tensor_scalar(sg.bitcast(I32), xn.bitcast(I32), -2147483648,
                            1065353216, ALU.bitwise_and, ALU.bitwise_or)
            v.tensor_scalar(tt, ai, B_0875, MAGIC_ADD, ALU.max, ALU.add)
            v.tensor_scalar(tt, tt, MASK_TOP, None, ALU.bitwise_and)
            v.tensor_scalar(cc, ai, B_025, None, ALU.is_gt)
            v.tensor_scalar(u8m, ai, B_075, None, ALU.is_gt)
            v.select(xn, u8m, tt.bitcast(F32), cc)
            v.tensor_tensor(xn, xn, sg, ALU.mult)
            return v.tensor_tensor(dv, xnv, shb, ALU.mult)
        self.t("vector", E, waits=tw + list(waits_last), inc=fin)
        self._q_tmp_user = self.at(fin)
        return self._q_tmp_user, self._q_tmp_user

    def schedule(self):
        self.prologue()
        self.stage1a()
        self.stage1b()
        self.stage2()
        self.stage3()

    def prologue(self):
        def ld_consts(e):
            e.dma_start(out=self.ones_col, in_=self.ones_col_d)
            e.dma_start(out=self.ones_row, in_=self.ones_row_d)
            for cc in range(8):
                e.dma_start(out=self.bQ_col[:, cc:cc + 1],
                            in_=self.bqkv[cc * 128:(cc + 1) * 128].unsqueeze(1))
                e.dma_start(out=self.bK_col[:, cc:cc + 1],
                            in_=self.bqkv[1024 + cc * 128:1024 + (cc + 1) * 128].unsqueeze(1))
            return e.dma_start(out=self.f32v("BVROW", 1024)[0:1, :],
                               in_=self.bqkv[2048:3072].unsqueeze(0))
        self.t("sync", ld_consts, inc="s_ld", n=19 * 16)

        def consts_prep(v):
            vones = self.bf("V", 16 * 65 * 32).rearrange("p (jh e) -> p jh e", e=65)
            v.memset(vones[:, :, 64:65], 1.0)
            v.memset(self.f32v("ST", 1, sub=128), 1e-5)
            return v.tensor_copy(self.ones_row_f, self.ones_row)
        self.t("vector", consts_prep, waits=[self.at("s_ld")], inc="s_dve")

        def bv_mm(e):
            bv = self.f32v("BVROW", 1024)[0:1, :]
            e.matmul(self.psA.ap()[:, 0:512], self.ones_row_f, bv[:, 0:512],
                     start=True, stop=True)
            return e.matmul(self.psA.ap()[:, 512:1024], self.ones_row_f,
                            bv[:, 512:1024], start=True, stop=True)
        self.t("tensor", bv_mm, waits=[self.at("s_ld"), self.at("s_dve")], inc="s_pe")
        self.mark("bv_mm_done", "s_pe")
        self.t("vector", lambda v: v.tensor_copy(self.bias_tile, self.psA.ap()[:]),
               waits=[self.at("s_pe")], inc="s_dve")
        self.mark("bv_ready", "s_dve")

        def ld_wk(e):
            last = None
            for kc in range(8):
                last = e.dma_start(out=self.bf("WSL", 1024, sub=kc * 2048),
                                   in_=self.wqkvT[kc * 128:(kc + 1) * 128, 1024:2048])
            return last
        self.t("gpsimd", ld_wk, inc="s_wld", n=8 * 16)
        self.mark("wk_ready", "s_wld")

        jobs = []
        for (w_in, w_out, rows, cols) in [(self.wg, self.qwg_s, II, D),
                                          (self.wu, self.qwu_s, II, D),
                                          (self.wd, self.qwd_s, D, II)]:
            for r0 in range(0, rows, 128):
                for c0 in range(0, cols, 512):
                    jobs.append((w_in, w_out, r0, c0, min(512, cols - c0)))
        self.wq_jobs, self.wq_idx = jobs, 0
        self.wq_wf_free = [None, None]
        self.wq_out_free = [None, None]

    def emit_wquant(self, k=1, pool=False):
        for _ in range(k):
            if self.wq_idx >= len(self.wq_jobs):
                return
            w_in, w_out, r0, c0, cw = self.wq_jobs[self.wq_idx]
            slot = self.wq_idx % 2
            self.wq_idx += 1
            wf = self.f32v("WF", cw)
            wo = self.bf("WQO", cw)
            ldsem = "s_wl2"
            waits = []
            if self.wq_wf_free[0] is not None:
                waits.append(self.wq_wf_free[0])
            self.t("gpsimd",
                   lambda e, w_in=w_in, r0=r0, c0=c0, cw=cw, wf=wf:
                   e.dma_start(out=wf, in_=w_in[r0:r0 + 128, c0:c0 + cw]),
                   waits=waits, inc=ldsem, n=16)
            wl = []
            if self.wq_out_free[0] is not None:
                wl.append(self.wq_out_free[0])
            srcfree, qdone = self.emit_quant2(wf, wo, cw,
                                              waits_first=[self.at(ldsem)],
                                              waits_last=wl, pool=pool)
            self.wq_wf_free[0] = srcfree
            spsem = f"s_wsp{slot}"
            self.t("gpsimd",
                   lambda e, w_out=w_out, r0=r0, c0=c0, cw=cw, wo=wo:
                   e.dma_start(out=w_out[r0:r0 + 128, c0:c0 + cw], in_=wo),
                   waits=[qdone], inc=spsem, n=16)
            self.wq_out_free[0] = self.at(spsem)

    def stage1a(self):
        def ln_views(tt):
            slot = tt % 2
            return dict(
                xr=self.f32v("XR", 1024, sub=slot * 4096),
                bns0=self.f32v("BNS", 6, sub=slot * 64),
                bns1=self.f32v("BNS", 6, sub=slot * 64 + 24),
                bns=self.f32v("BNS", 12, sub=slot * 64),
                mean=self.f32v("BNV", 1, sub=slot * 64),
                var=self.f32v("BNV", 1, sub=slot * 64 + 4),
                mv=self.f32v("BNV", 2, sub=slot * 64),
                rstd=self.f32v("BNV", 1, sub=slot * 64 + 8),
                rstd2=self.f32v("BNV", 1, sub=slot * 64 + 12),
                xln=self.bf("XLN", 1024))

        def emit_ln_stats(tt):
            vw = ln_views(tt)
            waits = []
            if tt >= 2:
                waits.append(f"norm1_{tt - 2}")
            if tt == 1:
                waits.append("bv_mm_done")
            self.t("sync",
                   lambda e, tt=tt, xr=vw["xr"]: e.dma_start(
                       out=xr, in_=self.x[tt * 128:(tt + 1) * 128, :]),
                   waits=waits, inc=f"s_xld{tt % 2}", n=16)

            def ln(v, vw=vw):
                v.bn_stats(vw["bns0"], vw["xr"][:, 0:512])
                v.bn_stats(vw["bns1"], vw["xr"][:, 512:1024])
                return v.bn_aggr(vw["mv"], vw["bns"])
            self.t("vector", ln, waits=[self.at(f"s_xld{tt % 2}")], inc="s_dve")
            self.t("scalar",
                   lambda a, vw=vw: a.activation(vw["rstd"], vw["var"], AF.Sqrt,
                                                 bias=self.f32v("ST", 1, sub=128)),
                   waits=[self.at("s_dve")], inc="s_act")
            self.mark(f"sqrt1_{tt}", "s_act")

        def emit_ln_norm(tt):
            vw = ln_views(tt)

            def norm(v, vw=vw):
                v.reciprocal(vw["rstd2"], vw["rstd"])
                return v.tensor_scalar(vw["xln"], vw["xr"], vw["mean"], vw["rstd2"],
                                       ALU.subtract, ALU.mult)
            nwaits = [f"sqrt1_{tt}"]
            if tt >= 1:
                nwaits.append(self.at(f"s_xsp{(tt - 1) % 2}"))
            self.t("vector", norm, waits=nwaits, inc="s_dve")
            self.mark(f"norm1_{tt}", "s_dve")
            self.t("sync",
                   lambda e, tt=tt, xln=vw["xln"]: e.dma_start(
                       out=self.xln_s[tt * 128:(tt + 1) * 128, :], in_=xln),
                   waits=[self.at("s_dve")], inc=f"s_xsp{tt % 2}", n=16)
            self.mark(f"xspm_{tt}", f"s_xsp{tt % 2}")

        self.xlt_user = [None, None]

        def k_phase(ic):
            slot = ic % 2

            def tld(e, ic=ic, slot=slot):
                last = None
                for kc in range(8):
                    last = e.dma_start(
                        out=self.bf("XLT", 512, sub=slot * 8192 + kc * 1024),
                        in_=self.xln_s[ic * 512:(ic + 1) * 512, kc * 128:(kc + 1) * 128],
                        transpose=True)
                return last
            waits = [f"xspm_{4 * ic + 3}", f"xspm_{4 * ic + 2}"]
            if self.xlt_user[slot] is not None:
                waits.append(self.xlt_user[slot])
            self.t("sync", tld, waits=waits, inc=f"s_tp{slot}", n=8 * 16)
            tmark = self.at(f"s_tp{slot}")

            for cc in range(8):
                bank = cc % 2
                gidx = ic * 8 + cc
                ps = self.psA.ap()[:, bank * 512:(bank + 1) * 512]
                waits = [tmark, "wk_ready", "bv_ready"]
                if gidx >= 2:
                    waits.append(f"kev_{gidx - 2}")

                def kmm(e, cc=cc, ps=ps, slot=slot):
                    last = None
                    for kc in range(8):
                        last = e.matmul(
                            ps, self.bf("WSL", 128, sub=kc * 2048 + cc * 256),
                            self.bf("XLT", 512, sub=slot * 8192 + kc * 1024),
                            start=(kc == 0), stop=(kc == 7))
                    return last
                self.t("tensor", kmm, waits=waits, inc="s_pe")
                self.t("vector",
                       lambda v, cc=cc, ic=ic, ps=ps: v.tensor_scalar(
                           self.bf("KT", 512, sub=cc * 8192 + ic * 1024), ps,
                           self.bK_col[:, cc:cc + 1], None, ALU.add),
                       waits=[self.at("s_pe")], inc="s_dve")
                self.mark(f"kev_{gidx}", "s_dve")
            self.mark(f"kvmm_K_{ic}", "s_pe")
            self.xlt_user[slot] = f"kvmm_K_{ic}"
            self.emit_wquant(2)

        NT = T // 128
        emit_ln_stats(0)
        for tt in range(1, NT):
            emit_ln_stats(tt)
            emit_ln_norm(tt - 1)
            if tt % 4 == 0:
                k_phase(tt // 4 - 1)
        emit_ln_norm(NT - 1)
        k_phase(ICN - 1)
        self.ln1_done_waits = [self.at("s_xsp0"), self.at("s_xsp1")]

        for phase in ("V",):
            if phase == "V":
                def ld_wv(e):
                    last = None
                    for kc in range(8):
                        last = e.dma_start(out=self.bf("WSL", 1024, sub=kc * 2048),
                                           in_=self.wqkvT[kc * 128:(kc + 1) * 128, 2048:3072])
                    return last
                self.t("gpsimd", ld_wv, waits=[self.at("s_pe")], inc="s_wld", n=8 * 16)
                self.mark("wv_ready", "s_wld")
            for ic in range(ICN):
                slot = ic % 2
                def tld(e, ic=ic, slot=slot):
                    last = None
                    for kc in range(8):
                        last = e.dma_start(
                            out=self.bf("XLT", 512, sub=slot * 8192 + kc * 1024),
                            in_=self.xln_s[ic * 512:(ic + 1) * 512, kc * 128:(kc + 1) * 128],
                            transpose=True)
                    return last
                waits = list(self.ln1_done_waits)
                if self.xlt_user[slot] is not None:
                    waits.append(self.xlt_user[slot])
                self.t("sync", tld, waits=waits, inc=f"s_tp{slot}", n=8 * 16)
                tmark = self.at(f"s_tp{slot}")

                if True:
                    for ts_ in range(4):
                        for vc in range(2):
                            bank = (ts_ * 2 + vc) % 2
                            gidx = (ic * 4 + ts_) * 2 + vc
                            ps = self.psB.ap()[:, bank * 512:(bank + 1) * 512]
                            waits = [tmark, "wv_ready", "bv_ready"]
                            if gidx >= 2:
                                waits.append(f"vev_{gidx - 2}")

                            def vmm(e, ts_=ts_, vc=vc, ps=ps, slot=slot):
                                last = None
                                for kc in range(8):
                                    last = e.matmul(
                                        ps,
                                        self.bf("XLT", 128,
                                                sub=slot * 8192 + kc * 1024 + ts_ * 256),
                                        self.bf("WSL", 512, sub=kc * 2048 + vc * 1024),
                                        start=(kc == 0), stop=(kc == 7))
                                return last
                            self.t("tensor", vmm, waits=waits, inc="s_pe")
                            jc = ic * 4 + ts_

                            def vev(v, jc=jc, vc=vc, ps=ps):
                                dst = self.bf("V", 8 * 65, sub=(jc * 16 + vc * 8) * 130)
                                dst = dst.rearrange("p (h e) -> p h e", e=65)[:, :, 0:64]
                                src = ps.rearrange("p (h e) -> p h e", e=64)
                                bt = self.bias_tile[:, vc * 512:(vc + 1) * 512]
                                bt = bt.rearrange("p (h e) -> p h e", e=64)
                                return v.tensor_tensor(dst, src, bt, ALU.add)
                            self.t("vector", vev, waits=[self.at("s_pe")], inc="s_dve")
                            self.mark(f"vev_{gidx}", "s_dve")
                self.mark(f"kvmm_{phase}_{ic}", "s_pe")
                self.xlt_user[slot] = f"kvmm_{phase}_{ic}"
                self.emit_wquant(2)
        self.mark("kv_done", "s_dve")

    def stage1b(self):
        def ld_wq(e):
            last = None
            for kc in range(8):
                last = e.dma_start(out=self.bf("WSL", 1024, sub=kc * 2048),
                                   in_=self.wqkvT[kc * 128:(kc + 1) * 128, 0:1024])
            return last
        self.t("gpsimd", ld_wq, waits=["kv_done", self.at("s_pe")], inc="s_wld", n=8 * 16)
        self.mark("wq_ready", "s_wld")

        def zmax(v):
            v.memset(self.mq2, 0.0)
            return v.memset(self.mk2, 0.0)
        self.t("vector", zmax, inc="s_dve")

        for ic in range(ICN):
            slot = ic % 2
            def tld(e, ic=ic, slot=slot):
                last = None
                for kc in range(8):
                    last = e.dma_start(
                        out=self.bf("XLT", 512, sub=slot * 8192 + kc * 1024),
                        in_=self.xln_s[ic * 512:(ic + 1) * 512, kc * 128:(kc + 1) * 128],
                        transpose=True)
                return last
            waits = []
            if self.xlt_user[slot] is not None:
                waits.append(self.xlt_user[slot])
            self.t("sync", tld, waits=waits, inc=f"s_tp{slot}", n=8 * 16)
            tmark = self.at(f"s_tp{slot}")

            for cc in range(8):
                bank = cc % 2
                ps = self.psA.ap()[:, bank * 512:(bank + 1) * 512]
                waits = [tmark, "wq_ready"]
                if ic > 0 or cc > 1:
                    waits.append(self.at("s_dve2"))

                def qmm(e, cc=cc, ps=ps, slot=slot):
                    last = None
                    for kc in range(8):
                        last = e.matmul(ps, self.bf("WSL", 128, sub=kc * 2048 + cc * 256),
                                        self.bf("XLT", 512, sub=slot * 8192 + kc * 1024),
                                        start=(kc == 0), stop=(kc == 7))
                    return last
                self.t("tensor", qmm, waits=waits, inc="s_pe")
                qtile = self.bf("XLN", 512, sub=bank * 1024)

                def qev(v, cc=cc, ps=ps, qtile=qtile):
                    return v.tensor_scalar(qtile, ps, self.bQ_col[:, cc:cc + 1], 0.125,
                                           ALU.add, ALU.mult)
                qpar = f"s_qsp{(ic * 8 + cc) % 2}"
                evw = [self.at("s_pe"), self.at(qpar)]
                self.t("vector", qev, waits=evw, inc="s_dve")
                self.t("sync",
                       lambda e, cc=cc, ic=ic, qtile=qtile: e.dma_start(
                           out=self.qT_s[cc * 128:(cc + 1) * 128, ic * 512:(ic + 1) * 512],
                           in_=qtile),
                       waits=[self.at("s_dve")], inc=qpar, n=16)

                q2 = self.bf("X2", 512)
                self.t("vector",
                       lambda v, qtile=qtile, q2=q2: v.tensor_tensor(q2, qtile, qtile,
                                                                     ALU.mult),
                       waits=[self.at("s_dve")], inc="s_dve")
                for half in range(2):
                    h = cc * 2 + half
                    pss = self.psOA.ap()[0:1, :]

                    def qones(e, half=half, q2=q2, pss=pss):
                        return e.matmul(pss, self.ones_col[half * 64:(half + 1) * 64, :],
                                        q2[half * 64:(half + 1) * 64, :],
                                        start=True, stop=True)
                    self.t("tensor", qones,
                           waits=[self.at("s_dve"), self.at("s_dve2")], inc="s_pe")

                    def qmax(v, h=h, pss=pss):
                        v.reduce_max(self.sm1[:, 0:1], pss, AX.X)
                        return v.tensor_scalar(self.mq2[:, h:h + 1], self.mq2[:, h:h + 1],
                                               self.sm1[:, 0:1], None, ALU.max)
                    self.t("vector", qmax, waits=[self.at("s_pe")], inc="s_dve2")
            self.mark(f"qmm_{ic}", "s_pe")
            self.xlt_user[slot] = f"qmm_{ic}"
            self.emit_wquant(2)

        k2 = self.bf("X2", 512)
        for cc in range(8):
            for js in range(ICN):
                def ksq(v, cc=cc, js=js, k2=k2):
                    sl = self.bf("KT", 512, sub=cc * 8192 + js * 1024)
                    return v.tensor_tensor(k2, sl, sl, ALU.mult)
                self.t("vector", ksq, waits=[self.at("s_dve2")], inc="s_dve")
                for half in range(2):
                    h = cc * 2 + half
                    pss = self.psOA.ap()[0:1, :]

                    def kones(e, half=half, k2=k2, pss=pss):
                        return e.matmul(pss, self.ones_col[half * 64:(half + 1) * 64, :],
                                        k2[half * 64:(half + 1) * 64, :],
                                        start=True, stop=True)
                    self.t("tensor", kones,
                           waits=[self.at("s_dve"), self.at("s_dve2")], inc="s_pe")

                    def kmax(v, h=h, pss=pss):
                        v.reduce_max(self.sm1[:, 1:2], pss, AX.X)
                        return v.tensor_scalar(self.mk2[:, h:h + 1], self.mk2[:, h:h + 1],
                                               self.sm1[:, 1:2], None, ALU.max)
                    self.t("vector", kmax, waits=[self.at("s_pe")], inc="s_dve2")

        self.t("vector",
               lambda v: v.tensor_tensor(self.sm1[:, 16:32], self.mq2, self.mk2, ALU.mult),
               waits=[self.at("s_dve2")], inc="s_dve")
        self.t("scalar",
               lambda a: a.activation(self.sm1[:, 32:48], self.sm1[:, 16:32], AF.Sqrt),
               waits=[self.at("s_dve")], inc="s_act")
        self.t("vector",
               lambda v: v.tensor_scalar(self.cneg_bf, self.sm1[:, 32:48], -1.0, None,
                                         ALU.mult),
               waits=[self.at("s_act")], inc="s_dve")
        for h in range(16):
            psx = self.psOB.ap()[:, 0:1]

            def cbc(e, psx=psx, h=h):
                return e.matmul(psx, self.ones_row, self.cneg_bf[:, h:h + 1],
                                start=True, stop=True)
            self.t("tensor", cbc, waits=[self.at("s_dve"), self.at("s_dve2")], inc="s_pe")
            self.t("vector",
                   lambda v, h=h, psx=psx: v.tensor_copy(self.cneg[:, h:h + 1], psx),
                   waits=[self.at("s_pe")], inc="s_dve2")
        self.mark("cneg_ready", "s_dve2")
        self.mark("s1b_evacs", "s_dve")
        self.mark("s1b_evacs2", "s_dve2")

    def stage2(self):
        def ld_wo(e):
            last = None
            for dc in range(8):
                last = e.dma_start(out=self.bf("WSL", 1024, sub=dc * 2048),
                                   in_=self.woT[dc * 128:(dc + 1) * 128, :])
            return last
        self.t("gpsimd", ld_wo, waits=["kv_done", self.at("s_pe")], inc="s_wld", n=8 * 16)
        self.mark("wo_ready", "s_wld")

        self.t("sync",
               lambda e: e.dma_start(out=self.f32v("BOROW", 1024)[0:1, :],
                                     in_=self.bo.unsqueeze(0)),
               waits=[self.at("s_pe")], inc="s_ld", n=16)

        def bo_mm(e):
            br = self.f32v("BOROW", 1024)[0:1, :]
            e.matmul(self.psC.ap()[:, 0:512], self.ones_row_f, br[:, 0:512],
                     start=True, stop=True)
            return e.matmul(self.psC.ap()[:, 512:1024], self.ones_row_f,
                            br[:, 512:1024], start=True, stop=True)
        self.t("tensor", bo_mm,
               waits=["s1b_evacs2", "s1b_evacs", self.at("s_ld")], inc="s_pe")

        def bo_ev(v):
            return v.tensor_copy(self.bias_tile, self.psC.ap()[:])
        self.t("vector", bo_ev, waits=[self.at("s_pe")], inc="s_dve")
        self.mark("bo_ready", "s_dve")

        ring = [self.psA, self.psB, self.psC]
        ring_consumer = ["s1b_evacs", "kv_done", "bo_ready"]
        ring_i = [0]

        def ring_alloc():
            idx = ring_i[0] % 3
            ring_i[0] += 1
            return idx, ring_consumer[idx]

        psO_free = ["bo_ready"]
        pslot_user = [None, None, None]
        pcount = [0]
        tail = [None]
        pending_op = [None]
        s2xr_user = ["cneg_ready", "cneg_ready"]

        def emit_tail_recips():
            td = tail[0]

            def rsf(a):
                lns = self.f32v("OE", 512)[0:1, :]
                with self.nc.allow_low_precision(reason="1/s as bf16 matmul operand"):
                    a.activation(lns, self.psOA.ap()[64:65, :], AF.Ln)
                    a.activation(self.rs_bf[:, 0:512], lns, AF.Exp, scale=-1.0)
                    a.activation(lns, self.psOB.ap()[64:65, :], AF.Ln)
                    return a.activation(self.rs_bf[:, 512:1024], lns, AF.Exp,
                                        scale=-1.0)
            self.t("scalar", rsf, waits=[td["avdone"]], inc="s_act")
            td["rsm"] = self.at("s_act")

        def emit_tail_rest():
            td = tail[0]
            gi, pair = td["gi"], td["pair"]
            iR, wR = ring_alloc()
            sR = ring[iR]

            def rsbc(e, sR=sR):
                e.matmul(sR.ap()[0:64, 0:512], self.ones_row[0:1, 0:64],
                         self.rs_bf[:, 0:512], start=True, stop=True,
                         tile_position=(0, 0), skip_group_check=True)
                return e.matmul(sR.ap()[64:128, 0:512], self.ones_row[0:1, 0:64],
                                self.rs_bf[:, 512:1024], start=True, stop=True,
                                tile_position=(0, 64), skip_group_check=True)
            self.t("tensor", rsbc, waits=[td["rsm"], wR], inc="s_pe")

            oe = self.f32v("OE", 512)

            def oev(v, pair=pair, oe=oe, sR=sR):
                v.tensor_copy(oe[0:64, :], self.psOA.ap()[0:64, :])
                v.tensor_copy(oe[64:128, :], self.psOB.ap()[0:64, :])
                return v.tensor_tensor(self.bf("OT", 512, sub=pair * 1024), oe,
                                       sR.ap()[:, 0:512], ALU.mult)
            ow = [self.at("s_pe")]
            if pair == 0 and td["ic"] > 0:
                ow.append(f"opdone_{td['ic'] - 1}")
            self.t("vector", oev, waits=ow, inc="s_dve")
            self.mark(f"oev_{gi}", "s_dve")
            ring_consumer[iR] = f"oev_{gi}"
            psO_free[0] = f"oev_{gi}"
            tail[0] = None

        def emit_op_chunk():
            ic_o, isub = pending_op[0]["ic"], pending_op[0]["isub"]
            pending_op[0]["isub"] += 1
            if pending_op[0]["isub"] >= 4:
                pending_op[0] = None
            trow = ic_o * 512 + isub * 128
            slot = isub % 2
            xr = self.f32v("XR", 1024, sub=slot * 4096)
            self.t("sync",
                   lambda e, trow=trow, xr=xr: e.dma_start(
                       out=xr, in_=self.x[trow:trow + 128, :]),
                   waits=[s2xr_user[slot]], inc=f"s_xld{slot}", n=16)
            iO, wO = ring_alloc()
            sO = ring[iO]

            def opmm(e, isub=isub, sO=sO):
                last = None
                for e2 in range(2):
                    for p in range(8):
                        last = e.matmul(
                            sO.ap()[:, e2 * 512:(e2 + 1) * 512],
                            self.bf("OT", 128, sub=p * 1024 + isub * 256),
                            self.bf("WSL", 512, sub=p * 2048 + e2 * 1024),
                            start=(p == 0), stop=(p == 7))
                return last
            self.t("tensor", opmm,
                   waits=[f"oev_{ic_o * NPAIR + NPAIR - 1}", "wo_ready", wO],
                   inc="s_pe")
            if isub == 3:
                self.mark(f"opdone_{ic_o}", "s_pe")
            opm = self.at("s_pe")
            x1t = self.f32v("X1T", 1024)

            def opev(v, x1t=x1t, xr=xr, sO=sO):
                t1 = self.f32v("T1", 512)
                last = None
                for e2 in range(2):
                    ps = sO.ap()[:, e2 * 512:(e2 + 1) * 512]
                    v.tensor_tensor(t1, ps, self.bias_tile[:, e2 * 512:(e2 + 1) * 512],
                                    ALU.add)
                    last = v.tensor_tensor(x1t[:, e2 * 512:(e2 + 1) * 512], t1,
                                           xr[:, e2 * 512:(e2 + 1) * 512], ALU.add)
                return last
            self.t("vector", opev,
                   waits=[opm, self.at(f"s_xld{slot}"), self.at("s_x1sp")],
                   inc="s_dve")
            self.mark(f"opev_{ic_o}_{isub}", "s_dve")
            ring_consumer[iO] = f"opev_{ic_o}_{isub}"
            s2xr_user[slot] = f"opev_{ic_o}_{isub}"
            self.t("sync",
                   lambda e, trow=trow, x1t=x1t: e.dma_start(
                       out=self.x1_s[trow:trow + 128, :], in_=x1t),
                   waits=[self.at("s_dve")], inc="s_x1sp", n=16)
            if isub == 3:
                self.mark(f"x1spm_{ic_o}", "s_x1sp")

        for ic in range(ICN):
            for pair in range(NPAIR):
                hA, hB = 2 * pair, 2 * pair + 1
                gi = ic * NPAIR + pair
                qslot = gi % 2
                qs = self.bf("QS", 512, sub=qslot * 1024)
                waits = [self.at("s_qsp0"), self.at("s_qsp1"), "cneg_ready", "bo_ready"]
                if gi >= 2:
                    waits.append(self.marks[f"qfree_{gi - 2}"])
                qldsem = f"s_tpq{gi % 2}"
                self.t("sync",
                       lambda e, pair=pair, ic=ic, qs=qs: e.dma_start(
                           out=qs, in_=self.qT_s[pair * 128:(pair + 1) * 128,
                                                 ic * 512:(ic + 1) * 512]),
                       waits=waits, inc=qldsem, n=16)
                qld = self.at(qldsem)

                ngrp = JCN // JG
                pend = []

                def emit_attnv(jg, pslot, gi=gi, hA=hA, hB=hB):
                    pa = self.bf("PA", 1024, sub=pslot * 2048)
                    pb = self.bf("PB", 1024, sub=pslot * 2048)

                    def avmm(e, jg=jg, pa=pa, pb=pb):
                        last = None
                        for u in range(JG):
                            jc = jg * JG + u
                            st, sp = (jc == 0), (jc == JCN - 1)
                            vA = self.bf("V", 65, sub=(jc * 16 + hA) * 130)
                            vB = self.bf("V", 65, sub=(jc * 16 + hB) * 130)
                            e.matmul(self.psOA.ap()[0:65, :], vA,
                                     pa[:, u * 512:(u + 1) * 512], start=st, stop=sp)
                            last = e.matmul(self.psOB.ap()[0:65, :], vB,
                                            pb[:, u * 512:(u + 1) * 512],
                                            start=st, stop=sp)
                        return last
                    awaits = [f"expA_{gi}_{jg}", f"expB_{gi}_{jg}"]
                    if jg == 0:
                        awaits.append(psO_free[0])
                    self.t("tensor", avmm, waits=awaits, inc="s_pe")
                    self.mark(f"av_{gi}_{jg}", "s_pe")

                for jg in range(ngrp):
                    pslot = pcount[0] % 3
                    pcount[0] += 1
                    iA, wA = ring_alloc()
                    iB, wB = ring_alloc()
                    sA, sB = ring[iA], ring[iB]

                    def smm(e, jg=jg, pair=pair, qs=qs, sA=sA, sB=sB):
                        last = None
                        for u in range(JG):
                            jc = jg * JG + u
                            kv = self.bf("KT", 128, sub=pair * 8192 + jc * 256)
                            last = e.matmul(sA.ap()[:, u * 512:(u + 1) * 512],
                                            kv[0:64, :], qs[0:64, :],
                                            start=True, stop=True,
                                            tile_position=(0, 0))
                            last = e.matmul(sB.ap()[:, u * 512:(u + 1) * 512],
                                            kv[64:128, :], qs[64:128, :],
                                            start=True, stop=True,
                                            tile_position=(64, 0))
                        return last
                    self.t("tensor", smm, waits=[qld, wA, wB], inc="s_pe")
                    smark = self.at("s_pe")

                    pa = self.bf("PA", 1024, sub=pslot * 2048)
                    pb = self.bf("PB", 1024, sub=pslot * 2048)
                    ewA = [smark]
                    ewB = []
                    if pslot_user[pslot] is not None:
                        ewA.append(pslot_user[pslot])
                    pslot_user[pslot] = f"av_{gi}_{jg}"
                    self.t("scalar",
                           lambda a, pa=pa, hA=hA, sA=sA: a.activation(
                               pa, sA.ap()[:], AF.Exp, bias=self.cneg[:, hA:hA + 1]),
                           waits=ewA, inc="s_act")
                    self.mark(f"expA_{gi}_{jg}", "s_act")
                    ring_consumer[iA] = f"expA_{gi}_{jg}"
                    self.t("scalar",
                           lambda a, pb=pb, hB=hB, sB=sB: a.activation(
                               pb, sB.ap()[:], AF.Exp, bias=self.cneg[:, hB:hB + 1]),
                           waits=ewB, inc="s_act")
                    self.mark(f"expB_{gi}_{jg}", "s_act")
                    ring_consumer[iB] = f"expB_{gi}_{jg}"

                    pend.append((jg, pslot))
                    if len(pend) > 2:
                        emit_attnv(*pend.pop(0))
                    if jg == 0 and tail[0] is not None:
                        emit_tail_recips()
                    elif jg == 1 and tail[0] is not None:
                        emit_tail_rest()
                    elif jg in (2, 4, 6, 8) and pending_op[0] is not None:
                        emit_op_chunk()
                    elif jg in (3, 7, 11):
                        self.emit_wquant(1)
                while pend:
                    emit_attnv(*pend.pop(0))
                self.mark(f"qfree_{gi}", "s_pe")
                tail[0] = {"gi": gi, "pair": pair, "ic": ic,
                           "avdone": self.at("s_pe")}

            pending_op[0] = {"ic": ic, "isub": 0}

        emit_tail_recips()
        emit_tail_rest()
        while pending_op[0] is not None:
            emit_op_chunk()
        self.mark("stage2_done", "s_x1sp")
        self.mark("attn_pe_done", "s_pe")
        self.mark("stage2_dve", "s_dve")
        self.mark("stage2_act", "s_act")

    def stage3(self):
        while self.wq_idx < len(self.wq_jobs):
            self.emit_wquant(1)

        def ld_qw(e):
            last = None
            for kc in range(8):
                last = e.dma_start(out=self.bf("QWG", 2816, sub=kc * 5632),
                                   in_=self.qwg_s[:, kc * 128:(kc + 1) * 128],
                                   transpose=True)
                last = e.dma_start(out=self.bf("QWU", 2816, sub=kc * 5632),
                                   in_=self.qwu_s[:, kc * 128:(kc + 1) * 128],
                                   transpose=True)
            for icc in range(22):
                last = e.dma_start(out=self.bf("QWD", 1024, sub=icc * 2048),
                                   in_=self.qwd_s[:, icc * 128:(icc + 1) * 128],
                                   transpose=True)
            return last
        self.t("sync", ld_qw,
               waits=[self.at("s_wsp0"), self.at("s_wsp1"), "attn_pe_done",
                      "stage2_done"],
               inc="s_tpw", n=38 * 16)
        self.mark("qw_ready", "s_tpw")

        CCH = [(i * 512, min(512, II - i * 512)) for i in range((II + 511) // 512)]
        self.xr_user = ["stage2_dve", "stage2_dve"]

        def emit_ln2(tc):
            for ts_ in range(4):
                trow = tc * 512 + ts_ * 128
                t = tc * 4 + ts_
                slot = t % 2
                xr = self.f32v("XR", 1024, sub=slot * 4096)
                lw = [self.xr_user[slot], f"x1spm_{tc}"]
                self.t("sync",
                       lambda e, trow=trow, xr=xr: e.dma_start(
                           out=xr, in_=self.x1_s[trow:trow + 128, :]),
                       waits=lw, inc=f"s_xld{slot}", n=16)
                bns0 = self.f32v("BNS", 6, sub=slot * 64)
                bns1 = self.f32v("BNS", 6, sub=slot * 64 + 24)
                bns = self.f32v("BNS", 12, sub=slot * 64)
                mean = self.f32v("BNV", 1, sub=slot * 64)
                var = self.f32v("BNV", 1, sub=slot * 64 + 4)
                mv = self.f32v("BNV", 2, sub=slot * 64)
                rstd = self.f32v("BNV", 1, sub=slot * 64 + 8)
                rstd2 = self.f32v("BNV", 1, sub=slot * 64 + 12)

                def ln(v, xr=xr, bns0=bns0, bns1=bns1, bns=bns, mv=mv):
                    v.bn_stats(bns0, xr[:, 0:512])
                    v.bn_stats(bns1, xr[:, 512:1024])
                    return v.bn_aggr(mv, bns)
                self.t("vector", ln, waits=[self.at(f"s_xld{slot}")], inc="s_dve")
                self.t("scalar",
                       lambda a, var=var, rstd=rstd: a.activation(
                           rstd, var, AF.Sqrt, bias=self.f32v("ST", 1, sub=128)),
                       waits=[self.at("s_dve")], inc="s_act")
                self.mark(f"sqrt3_{t}", "s_act")
                x2 = self.f32v("X2", 1024)

                def norm(v, xr=xr, mean=mean, rstd=rstd, rstd2=rstd2, x2=x2):
                    v.reciprocal(rstd2, rstd)
                    return v.tensor_scalar(x2, xr, mean, rstd2,
                                           ALU.subtract, ALU.mult)
                nw = [f"sqrt3_{t}"]
                if t == 0:
                    nw.append("stage2_done")
                self.t("vector", norm, waits=nw, inc="s_dve")
                self.mark(f"ln3n_{t}", "s_dve")
                self.xr_user[slot] = f"ln3n_{t}"

                xq = self.bf("XLN", 1024)
                qw0 = [self.at("s_xqsp")]
                for half in range(2):
                    self.emit_quant2(x2[:, half * 512:(half + 1) * 512],
                                     xq[:, half * 512:(half + 1) * 512], 512,
                                     waits_first=qw0 if half == 0 else (),
                                     inc_last="s_dve", pool=False)
                self.t("sync",
                       lambda e, trow=trow, xq=xq: e.dma_start(
                           out=self.xq_s[trow:trow + 128, :], in_=xq),
                       waits=[self.at("s_dve")], inc="s_xqsp", n=16)

        def emit_xqtld(tc):
            def xqtld(e, tc=tc):
                last = None
                for kc in range(8):
                    last = e.dma_start(
                        out=self.bf("XQT", 512, sub=kc * 1024),
                        in_=self.xq_s[tc * 512:(tc + 1) * 512, kc * 128:(kc + 1) * 128],
                        transpose=True)
                return last
            waits = [self.at("s_xqsp")]
            if tc == 0:
                waits += ["attn_pe_done", "qw_ready"]
            else:
                waits.append(f"gu_done_{tc - 1}")
            self.t("sync", xqtld, waits=waits, inc="s_tpx", n=8 * 16)
            self.mark(f"xqm_{tc}", "s_tpx")

        def emit_hqtld(tc, part, i0, i1):
            def hqtld(e, tc=tc, i0=i0, i1=i1):
                last = None
                for icc in range(i0, i1):
                    last = e.dma_start(
                        out=self.bf("HQT", 512, sub=icc * 1024),
                        in_=self.hq_s[tc * 512:(tc + 1) * 512, icc * 128:(icc + 1) * 128],
                        transpose=True)
                return last
            waits = [self.at("s_hqsp0"), self.at("s_hqsp1")]
            if tc > 0:
                waits.append(f"dmm{part}_{tc - 1}")
            self.t("sync", hqtld, waits=waits, inc="s_tph", n=(i1 - i0) * 16)
            self.mark(f"hqld{part}_{tc}", "s_tph")

        def emit_gu(tc):
            for ci, (c0, cw) in enumerate(CCH):
                for ts_ in range(4):
                    ch = tc * 24 + ci * 4 + ts_

                    def gmm(e, ts_=ts_, c0=c0, cw=cw):
                        last = None
                        for kc in range(8):
                            last = e.matmul(
                                self.psOA.ap()[:, 0:cw],
                                self.bf("XQT", 128, sub=kc * 1024 + ts_ * 256),
                                self.bf("QWG", cw, sub=kc * 5632 + c0 * 2),
                                start=(kc == 0), stop=(kc == 7))
                        return last
                    gw = [f"xqm_{tc}"]
                    if ch == 0:
                        gw.append("stage2_dve")
                    else:
                        gw.append(f"silu_{ch - 1}")
                    self.t("tensor", gmm, waits=gw, inc="s_pe")
                    gm = self.at("s_pe")
                    sgt = self.f32v("SGT", 512)
                    sw = [gm]
                    if ch > 0:
                        sw.append(f"hm_{ch - 1}")
                    self.t("scalar",
                           lambda a, cw=cw, sgt=sgt: a.activation(
                               sgt[:, 0:cw], self.psOA.ap()[:, 0:cw], AF.Silu),
                           waits=sw, inc="s_act")
                    self.mark(f"silu_{ch}", "s_act")

                    def umm(e, ts_=ts_, c0=c0, cw=cw):
                        last = None
                        for kc in range(8):
                            last = e.matmul(
                                self.psOB.ap()[:, 0:cw],
                                self.bf("XQT", 128, sub=kc * 1024 + ts_ * 256),
                                self.bf("QWU", cw, sub=kc * 5632 + c0 * 2),
                                start=(kc == 0), stop=(kc == 7))
                        return last
                    uw = [f"xqm_{tc}"]
                    if ch == 0:
                        uw.append("stage2_dve")
                    else:
                        uw.append(f"hm_{ch - 1}")
                    self.t("tensor", umm, waits=uw, inc="s_pe")
                    um = self.at("s_pe")
                    ht = self.f32v("HT", 512)

                    def hmul(v, cw=cw, sgt=sgt, ht=ht):
                        return v.tensor_tensor(ht[:, 0:cw], sgt[:, 0:cw],
                                               self.psOB.ap()[:, 0:cw], ALU.mult)
                    self.t("vector", hmul, waits=[um, f"silu_{ch}"], inc="s_dve")
                    self.mark(f"hm_{ch}", "s_dve")
                    hq = self.bf("HQ", 512, sub=(ch % 2) * 1024)
                    hqsem = f"s_hqsp{ch % 2}"
                    self.emit_quant2(ht[:, 0:cw], hq[:, 0:cw], cw,
                                     waits_first=[self.at(hqsem)],
                                     inc_last="s_dve", pool=False)
                    trow = tc * 512 + ts_ * 128
                    self.t("sync",
                           lambda e, trow=trow, c0=c0, cw=cw, hq=hq: e.dma_start(
                               out=self.hq_s[trow:trow + 128, c0:c0 + cw],
                               in_=hq[:, 0:cw]),
                           waits=[self.at("s_dve")], inc=hqsem, n=16)
                if ci == 2:
                    emit_hqtld(tc, "A", 0, 12)
            emit_hqtld(tc, "B", 12, 22)
            self.mark(f"gu_done_{tc}", "s_pe")

        def emit_dmm(tc):
            for ts_ in range(4):
                t = tc * 4 + ts_
                slot = t % 2
                trow = tc * 512 + ts_ * 128
                xr = self.f32v("XR", 1024, sub=slot * 4096)
                self.t("sync",
                       lambda e, trow=trow, xr=xr: e.dma_start(
                           out=xr, in_=self.x1_s[trow:trow + 128, :]),
                       waits=[self.xr_user[slot]], inc=f"s_xld{slot}", n=16)
                for part, i0, i1 in [("A", 0, 12), ("B", 12, 22)]:
                    for e2 in range(2):
                        ps = self.psA.ap()[:, e2 * 512:(e2 + 1) * 512]

                        def dmm(e, ts_=ts_, e2=e2, i0=i0, i1=i1, ps=ps):
                            last = None
                            for icc in range(i0, i1):
                                last = e.matmul(
                                    ps, self.bf("HQT", 128, sub=icc * 1024 + ts_ * 256),
                                    self.bf("QWD", 512, sub=icc * 2048 + e2 * 1024),
                                    start=(icc == 0), stop=(icc == 21))
                            return last
                        dw = [f"hqld{part}_{tc}"]
                        if part == "A" and e2 == 0:
                            if ts_ > 0:
                                dw.append(f"dev3_{tc}_{ts_ - 1}")
                            elif tc > 0:
                                dw.append(f"dev3_{tc - 1}_3")
                            else:
                                dw += ["stage2_act", "stage2_dve"]
                        self.t("tensor", dmm, waits=dw, inc="s_pe")
                    self.mark(f"dmm{part}_{tc}", "s_pe")
                outt = self.f32v("OUTT", 1024)

                def dev(v, outt=outt, xr=xr):
                    return v.tensor_tensor(outt, self.psA.ap()[:], xr, ALU.add)
                self.t("vector", dev,
                       waits=[self.at("s_pe"), self.at(f"s_xld{slot}"),
                              self.at("s_out")],
                       inc="s_dve")
                self.mark(f"dev3_{tc}_{ts_}", "s_dve")
                self.xr_user[slot] = f"dev3_{tc}_{ts_}"
                self.t("sync",
                       lambda e, trow=trow, outt=outt: e.dma_start(
                           out=self.out[trow:trow + 128, :], in_=outt),
                       waits=[self.at("s_dve")], inc="s_out", n=16)

        emit_ln2(0)
        emit_xqtld(0)
        for tc in range(TCN):
            if tc + 1 < TCN:
                emit_ln2(tc + 1)
            emit_gu(tc)
            if tc + 1 < TCN:
                emit_xqtld(tc + 1)
            emit_dmm(tc)

        self.t("sync", lambda e: None, waits=[self.at("s_out")])

    def finalize(self):
        nc = self.nc
        per_eng = {"sync": [], "tensor": [], "vector": [], "scalar": [], "gpsimd": []}
        for task in self.tasks:
            key = {"pool": "gpsimd", "scalardma": "scalar"}.get(task.eng, task.eng)
            per_eng[key].append(task)
        last_wait = {}

        class DmaProxy:
            def __init__(self, eng, sem):
                self._eng, self._sem = eng, sem
                self.count = 0

            def dma_start(self, **kw):
                self.count += 1
                return self._eng.dma_start(**kw).then_inc(self._sem, 16)

            def __getattr__(self, name):
                return getattr(self._eng, name)

        class SerialProxy:

            _EXEMPT = {"wait_ge", "then_inc", "dma_start"}

            def __init__(self, eng, selfsem):
                object.__setattr__(self, "_eng", eng)
                object.__setattr__(self, "_sem", selfsem)
                object.__setattr__(self, "_count", 0)
                object.__setattr__(self, "_prev", None)
                object.__setattr__(self, "_token", None)
                object.__setattr__(self, "_emitted", {})

            def _pre_op(self):
                if self._prev is not None:
                    object.__setattr__(self, "_count", self._count + 1)
                    self._prev.then_inc(self._sem, 1)
                    object.__setattr__(self, "_token", (self._sem, self._count))
                    object.__setattr__(self, "_prev", None)
                if self._token is not None:
                    self._eng.wait_ge(*self._token)
                    object.__setattr__(self, "_token", None)

            def wait_ge(self, sem, val):
                return self._eng.wait_ge(sem, val)

            def finish_task(self, inc_sem, inc_n, planned_total):
                ins = self._prev
                assert ins is not None
                ins.then_inc(inc_sem, inc_n)
                object.__setattr__(self, "_prev", None)
                object.__setattr__(self, "_token", (inc_sem, planned_total))

            def __getattr__(self, name):
                target = getattr(self._eng, name)
                if name.startswith("_") or not callable(target):
                    return target

                def wrapped(*a, **kw):
                    self._pre_op()
                    ins = target(*a, **kw)
                    object.__setattr__(self, "_prev", ins)
                    return ins
                return wrapped

        def runner(eng_name):
            def run(eng_obj):
                emitted = {}
                proxy = None
                if eng_name in ("vector", "scalar", "gpsimd"):
                    proxy = SerialProxy(eng_obj, self.sems[f"s_self_{eng_name}"])
                for task in per_eng[eng_name]:
                    for s, vv in task.waits:
                        if vv <= 0:
                            continue
                        key = (eng_name, s)
                        if last_wait.get(key, -1) >= vv:
                            continue
                        last_wait[key] = vv
                        eng_obj.wait_ge(self.sems[s], vv)
                    if task.inc is not None and task.eng in ("sync", "gpsimd",
                                                              "scalardma"):
                        dproxy = DmaProxy(eng_obj, self.sems[task.inc])
                        task.fn(dproxy)
                        assert dproxy.count * 16 == task.n, \
                            f"{eng_name} dma count {dproxy.count}*16 != {task.n}"
                    elif proxy is not None:
                        ret = task.fn(proxy)
                        if task.inc is not None:
                            assert ret is not None, f"{eng_name} needs instr ({task.inc})"
                            emitted[task.inc] = emitted.get(task.inc, 0) + task.n
                            proxy.finish_task(self.sems[task.inc], task.n,
                                              emitted[task.inc])
                    else:
                        ret = task.fn(eng_obj)
                        if task.inc is not None:
                            assert ret is not None, f"{eng_name} needs instr ({task.inc})"
                            ret.then_inc(self.sems[task.inc], task.n)
            return run

        self.sem("s_self_vector")
        self.sem("s_self_scalar")
        self.sem("s_self_gpsimd")
        blk = self._es.enter_context(nc.Block())
        blk.sync(runner("sync"))
        blk.tensor(runner("tensor"))
        blk.vector(runner("vector"))
        blk.scalar(runner("scalar"))
        blk.gpsimd(runner("gpsimd"))
        self._es.close()
        return nc


def build(debug=False):
    b = Builder(debug=debug)
    b.schedule()
    return b.finalize()


_cached = {}


def kernel(L0_attn_x, L0_attn_W_w, L0_attn_W_b, L0_attn_out_w, L0_attn_out_b,
           L0_ffn_w_gate, L0_ffn_w_up, L0_ffn_w_down):
    x = np.asarray(L0_attn_x, dtype=np.float32)
    wqkvT = np.ascontiguousarray(np.asarray(L0_attn_W_w, np.float32).T)
    woT = np.ascontiguousarray(np.asarray(L0_attn_out_w, np.float32).T)
    bqkv = np.ascontiguousarray(np.asarray(L0_attn_W_b, np.float32))
    bo = np.ascontiguousarray(np.asarray(L0_attn_out_b, np.float32))
    wg = np.ascontiguousarray(np.asarray(L0_ffn_w_gate, np.float32))
    wu = np.ascontiguousarray(np.asarray(L0_ffn_w_up, np.float32))
    wd = np.ascontiguousarray(np.asarray(L0_ffn_w_down, np.float32))
    ones_col = np.ones((128, 1), ml_dtypes.bfloat16)
    ones_row = np.ones((1, 128), ml_dtypes.bfloat16)

    if "nc" not in _cached:
        _cached["nc"] = build(debug=False)
    nc = _cached["nc"]
    in_maps = []
    for b_ in range(8):
        in_maps.append({
            "x": np.ascontiguousarray(x[b_]),
            "wqkvT": wqkvT, "woT": woT, "bqkv": bqkv, "bo": bo,
            "wg": wg, "wu": wu, "wd": wd,
            "ones_col": ones_col, "ones_row": ones_row,
        })
    res = run_bass_kernel_spmd(nc, in_maps, core_ids=list(range(8)))
    return np.stack([r["out"] for r in res.results], axis=0)


# version 15
# speedup vs baseline: 1.4190x; 1.0757x over previous
import contextlib
import numpy as np
import ml_dtypes
import concourse.bass as bass
import concourse.mybir as mybir
from concourse.bass_utils import run_bass_kernel_spmd

F32 = mybir.dt.float32
BF16 = mybir.dt.bfloat16
I32 = mybir.dt.int32
U8 = mybir.dt.uint8
ALU = mybir.AluOpType
AF = mybir.ActivationFunctionType
AX = mybir.AxisListType

T, D, H, DH, II = 4096, 1024, 16, 64, 2816
ICN, ICW = 8, 512
JCN = T // 128
TCN, TCW = 8, 512
NPAIR = H // 2
JG = 2


def fbits(x):
    return int(np.float32(x).view(np.int32))


B_0875, B_025, B_075 = fbits(0.875), fbits(0.25), fbits(0.75)
MAGIC_ADD = 0x00A00000
MASK_TOP = -4194304


class Task:
    __slots__ = ("eng", "fn", "waits", "inc", "n")

    def __init__(self, eng, fn, waits, inc, n):
        self.eng, self.fn, self.waits, self.inc, self.n = eng, fn, waits, inc, n


class Builder:
    def __init__(self, debug=False):
        self.debug = debug
        self.nc = bass.Bass("TRN2", target_bir_lowering=False, debug=False)
        self.tasks = []
        self.tot = {}
        self.sems = {}
        self.marks = {}
        self._q_tmp_user = None
        self._es = contextlib.ExitStack()
        self._declare()

    def sem(self, name):
        if name not in self.sems:
            self.sems[name] = self._es.enter_context(self.nc.semaphore(name))
            self.tot[name] = 0
        return self.sems[name]

    def at(self, semname):
        self.sem(semname)
        return (semname, self.tot[semname])

    def t(self, eng, fn, waits=(), inc=None, n=1):
        ws = [self.marks[w] if isinstance(w, str) else w for w in waits]
        for s, v in ws:
            assert v <= self.tot.get(s, 0), f"forward wait {s}:{v}>{self.tot.get(s, 0)}"
        if inc is not None:
            self.sem(inc)
            self.tot[inc] += n
        self.tasks.append(Task(eng, fn, ws, inc, n))

    def mark(self, name, semname):
        self.sem(semname)
        self.marks[name] = (semname, self.tot[semname])

    def _declare(self):
        nc = self.nc
        ein, eout = "ExternalInput", "ExternalOutput"
        scr = eout if self.debug else "Internal"
        d = lambda name, shape, dt, kind: nc.dram_tensor(name, shape, dt, kind=kind).ap()
        self.x = d("x", [T, D], F32, ein)
        self.wqkvT = d("wqkvT", [D, 3 * D], F32, ein)
        self.woT = d("woT", [D, D], F32, ein)
        self.bqkv = d("bqkv", [3 * D], F32, ein)
        self.bo = d("bo", [D], F32, ein)
        self.wg = d("wg", [II, D], F32, ein)
        self.wu = d("wu", [II, D], F32, ein)
        self.wd = d("wd", [D, II], F32, ein)
        self.ones_col_d = d("ones_col", [128, 1], BF16, ein)
        self.ones_row_d = d("ones_row", [1, 128], BF16, ein)
        self.out = d("out", [T, D], F32, eout)
        self.xln_s = d("xln_s", [T, D], BF16, scr)
        self.qT_s = d("qT_s", [D, T], BF16, scr)
        self.x1_s = d("x1_s", [T, D], F32, scr)
        self.xq_s = d("xq_s", [T, D], BF16, scr)
        self.hq_s = d("hq_s", [T, II], BF16, scr)
        self.qwg_s = d("qwg_s", [II, D], BF16, scr)
        self.qwu_s = d("qwu_s", [II, D], BF16, scr)
        self.qwd_s = d("qwd_s", [D, II], BF16, scr)

        AREN = 157696
        self.arena = nc.alloc_sbuf_tensor("arena", [128, AREN], U8)
        self.off = {}
        for name, off, nb in [
            ("KT", 0, 65536),
            ("V", 65536, 66560),
            ("WSL", 132096, 16384),
            ("OT", 148480, 8192),
            ("QWG", 0, 45056), ("QWU", 45056, 45056), ("QWD", 90112, 45056),
            ("HQT", 135168, 22528),
        ]:
            assert off + nb <= AREN
            self.off[name] = ("A", off, nb)

        SCRN = 49152
        self.scrb = nc.alloc_sbuf_tensor("scrb", [128, SCRN], U8)
        o = 0

        def sseg(name, nb):
            nonlocal o
            o = (o + 63) // 64 * 64
            assert o + nb <= SCRN, (name, o, nb)
            self.off[name] = ("S", o, nb)
            o += nb

        for n_, nb in [("WF", 2048), ("WQX", 2048), ("WQA", 2048), ("WQG", 2048),
                       ("WQT", 2048), ("WQC", 2048), ("WQM", 2048), ("WQO", 1024),
                       ("WQSC", 512), ("BNS", 128), ("BNV", 128),
                       ("XR", 8192), ("X2", 4096), ("XLN", 2048), ("ST", 192),
                       ("UN", 18432)]:
            sseg(n_, nb)
        u = self.off["UN"][1]
        for name, du, nb in [
            ("XLT", 0, 16384),
            ("ST2", 16384, 192),
            ("QS", 16384, 2048), ("PA", 0, 6144), ("PB", 6144, 6144),
            ("BOROW", 0, 4096),
            ("OE", 12288, 2048), ("T1", 12288, 2048), ("RSBF", 14336, 2048),
            ("XQT", 0, 8192), ("SGT", 8192, 2048), ("HT", 10240, 2048),
            ("HQ", 12288, 2048), ("OUTT", 14336, 4096),
        ]:
            assert du + nb <= 18432
            self.off[name] = ("S", u + du, nb)
        self.off["BVROW"] = ("S", self.off["XR"][1] + 4096, 4096)
        self.off["X1T"] = self.off["X2"]

        self.psA = self._es.enter_context(nc.psum_tensor("psA", [128, 1024], F32))
        self.psB = self._es.enter_context(nc.psum_tensor("psB", [128, 1024], F32))
        self.psC = self._es.enter_context(nc.psum_tensor("psC", [128, 1024], F32))
        self.psOA = self._es.enter_context(nc.psum_tensor("psOA", [128, 512], F32))
        self.psOB = self._es.enter_context(nc.psum_tensor("psOB", [128, 512], F32))

        sb = nc.alloc_sbuf_tensor
        self.ones_col = sb("ones_col_t", [128, 1], BF16).ap()
        self.ones_row = sb("ones_row_t", [1, 128], BF16).ap()
        self.ones_row_f = sb("ones_row_f", [1, 128], F32).ap()
        self.bK_col = sb("bK_col", [128, 8], F32).ap()
        self.bQ_col = sb("bQ_col", [128, 8], F32).ap()
        self.bias_tile = sb("bias_tile", [128, 1024], F32).ap()
        self.cneg = sb("cneg", [128, 16], F32).ap()
        self.mq2 = sb("mq2", [1, 16], F32).ap()
        self.mk2 = sb("mk2", [1, 16], F32).ap()
        self.sm1 = sb("sm1", [1, 64], F32).ap()
        self.mxc = sb("mxc", [1, 256], BF16).ap()
        rb_kind, rb_off, rb_nb = self.off["RSBF"]
        self.rs_bf = self.scrb.ap()[0:1, rb_off:rb_off + rb_nb].bitcast(BF16)
        self.cneg_bf = sb("cneg_bf", [1, 16], BF16).ap()

    def _view(self, name, esz, cols, sub):
        kind, off, nb = self.off[name]
        base = self.arena if kind == "A" else self.scrb
        assert sub + cols * esz <= nb, (name, sub, cols)
        return base.ap()[:, off + sub:off + sub + cols * esz]

    def bf(self, name, cols, sub=0):
        return self._view(name, 2, cols, sub).bitcast(BF16)

    def f32v(self, name, cols, sub=0):
        return self._view(name, 4, cols, sub).bitcast(F32)

    def i32v(self, name, cols, sub=0):
        return self._view(name, 4, cols, sub).bitcast(I32)

    def u8v(self, name, cols, sub=0):
        return self._view(name, 1, cols, sub)

    def emit_quant2(self, src_f32, dst_bf16, F, waits_first=(), waits_last=(),
                    inc_last=None, pool=True):
        nb = F // 16
        bm = self.f32v("WQSC", nb, sub=0)
        s_ = self.f32v("WQSC", nb, sub=128)
        rs = self.f32v("WQSC", nb, sub=256)
        sh = self.f32v("WQSC", nb, sub=384)
        xn, ai = self.f32v("WQX", F), self.i32v("WQA", F)
        sg = self.f32v("WQG", F)
        tt, cc = self.i32v("WQT", F), self.f32v("WQC", F)
        mm = self.f32v("WQM", F)
        sv = src_f32.rearrange("p (n b) -> p n b", b=16)
        rsb = rs.unsqueeze(2).broadcast_to([128, nb, 16])
        shb = sh.unsqueeze(2).broadcast_to([128, nb, 16])
        dv = dst_bf16.rearrange("p (n b) -> p n b", b=16)
        xnv = xn.rearrange("p (n b) -> p n b", b=16)
        mmv = mm.rearrange("p (n b) -> p n b", b=16)
        fin = inc_last if inc_last is not None else ("s_qp" if pool else "s_qv")
        tw = list(waits_first)
        if self._q_tmp_user is not None:
            tw.append(self._q_tmp_user)

        if pool:
            def A(v):
                v.tensor_reduce(bm, sv, AX.X, ALU.max, apply_absolute_value=True)
                v.tensor_scalar(s_, bm, 1.0 / 6.0, 1e-12, ALU.mult, ALU.max)
                v.reciprocal(rs, s_)
                return v.tensor_scalar(sh, s_, 0.5, None, ALU.mult)
            self.t("vector", A, waits=tw, inc="s_qv")

            def B(g):
                return g.tensor_tensor(xnv, sv, rsb, ALU.mult)
            self.t("pool", B, waits=[self.at("s_qv")], inc="s_qp")
            srcfree = self.at("s_qp")

            def C(v):
                v.tensor_scalar(ai, xn.bitcast(I32), 0x7FFFFFFF, None, ALU.bitwise_and)
                v.tensor_scalar(sg.bitcast(I32), xn.bitcast(I32), -2147483648,
                                1065353216, ALU.bitwise_and, ALU.bitwise_or)
                v.tensor_scalar(tt, ai, B_0875, MAGIC_ADD, ALU.max, ALU.add)
                return v.tensor_scalar(tt, tt, MASK_TOP, None, ALU.bitwise_and)
            self.t("vector", C, waits=[self.at("s_qp")], inc="s_qv")

            def D(g):
                g.tensor_scalar(cc, ai.bitcast(F32), 0.25, None, ALU.is_gt)
                g.tensor_scalar(mm, ai.bitcast(F32), 0.75, None, ALU.is_gt)
                g.tensor_scalar(xn, tt.bitcast(F32), -1.0, None, ALU.add)
                g.tensor_tensor(mm, mm, xn, ALU.mult)
                g.tensor_add(mm, mm, cc)
                g.tensor_tensor(mm, mm, sg, ALU.mult)
                return g.tensor_tensor(dv, mmv, shb, ALU.mult)
            self.t("pool", D, waits=[self.at("s_qv")] + list(waits_last), inc=fin)
            self._q_tmp_user = self.at(fin)
            return srcfree, self._q_tmp_user

        u8m = self.u8v("WQM", F)

        def E(v):
            v.tensor_reduce(bm, sv, AX.X, ALU.max, apply_absolute_value=True)
            v.tensor_scalar(s_, bm, 1.0 / 6.0, 1e-12, ALU.mult, ALU.max)
            v.reciprocal(rs, s_)
            v.tensor_scalar(sh, s_, 0.5, None, ALU.mult)
            v.tensor_tensor(xnv, sv, rsb, ALU.mult)
            v.tensor_scalar(ai, xn.bitcast(I32), 0x7FFFFFFF, None, ALU.bitwise_and)
            v.tensor_scalar(sg.bitcast(I32), xn.bitcast(I32), -2147483648,
                            1065353216, ALU.bitwise_and, ALU.bitwise_or)
            v.tensor_scalar(tt, ai, B_0875, MAGIC_ADD, ALU.max, ALU.add)
            v.tensor_scalar(tt, tt, MASK_TOP, None, ALU.bitwise_and)
            v.tensor_scalar(cc, ai, B_025, None, ALU.is_gt)
            v.tensor_scalar(u8m, ai, B_075, None, ALU.is_gt)
            v.select(xn, u8m, tt.bitcast(F32), cc)
            v.tensor_tensor(xn, xn, sg, ALU.mult)
            return v.tensor_tensor(dv, xnv, shb, ALU.mult)
        self.t("vector", E, waits=tw + list(waits_last), inc=fin)
        self._q_tmp_user = self.at(fin)
        return self._q_tmp_user, self._q_tmp_user

    def schedule(self):
        self.prologue()
        self.stage1a()
        self.stage1b()
        self.stage2()
        self.stage3()

    def prologue(self):
        def ld_consts(e):
            e.dma_start(out=self.ones_col, in_=self.ones_col_d)
            e.dma_start(out=self.ones_row, in_=self.ones_row_d)
            for cc in range(8):
                e.dma_start(out=self.bQ_col[:, cc:cc + 1],
                            in_=self.bqkv[cc * 128:(cc + 1) * 128].unsqueeze(1))
                e.dma_start(out=self.bK_col[:, cc:cc + 1],
                            in_=self.bqkv[1024 + cc * 128:1024 + (cc + 1) * 128].unsqueeze(1))
            return e.dma_start(out=self.f32v("BVROW", 1024)[0:1, :],
                               in_=self.bqkv[2048:3072].unsqueeze(0))
        self.t("sync", ld_consts, inc="s_ld", n=19 * 16)

        def consts_prep(v):
            vones = self.bf("V", 16 * 65 * 32).rearrange("p (jh e) -> p jh e", e=65)
            v.memset(vones[:, :, 64:65], 1.0)
            v.memset(self.f32v("ST", 1, sub=128), 1e-5)
            return v.tensor_copy(self.ones_row_f, self.ones_row)
        self.t("vector", consts_prep, waits=[self.at("s_ld")], inc="s_dve")

        def bv_mm(e):
            bv = self.f32v("BVROW", 1024)[0:1, :]
            e.matmul(self.psA.ap()[:, 0:512], self.ones_row_f, bv[:, 0:512],
                     start=True, stop=True)
            return e.matmul(self.psA.ap()[:, 512:1024], self.ones_row_f,
                            bv[:, 512:1024], start=True, stop=True)
        self.t("tensor", bv_mm, waits=[self.at("s_ld"), self.at("s_dve")], inc="s_pe")
        self.mark("bv_mm_done", "s_pe")
        self.t("vector", lambda v: v.tensor_copy(self.bias_tile, self.psA.ap()[:]),
               waits=[self.at("s_pe")], inc="s_dve")
        self.mark("bv_ready", "s_dve")

        def ld_wk(e):
            last = None
            for kc in range(8):
                last = e.dma_start(out=self.bf("WSL", 1024, sub=kc * 2048),
                                   in_=self.wqkvT[kc * 128:(kc + 1) * 128, 1024:2048])
            return last
        self.t("gpsimd", ld_wk, inc="s_wld", n=8 * 16)
        self.mark("wk_ready", "s_wld")

        jobs = []
        for (w_in, w_out, rows, cols) in [(self.wg, self.qwg_s, II, D),
                                          (self.wu, self.qwu_s, II, D),
                                          (self.wd, self.qwd_s, D, II)]:
            for r0 in range(0, rows, 128):
                for c0 in range(0, cols, 512):
                    jobs.append((w_in, w_out, r0, c0, min(512, cols - c0)))
        self.wq_jobs, self.wq_idx = jobs, 0
        self.wq_wf_free = [None, None]
        self.wq_out_free = [None, None]

    def emit_wquant(self, k=1, pool=False):
        for _ in range(k):
            if self.wq_idx >= len(self.wq_jobs):
                return
            w_in, w_out, r0, c0, cw = self.wq_jobs[self.wq_idx]
            slot = self.wq_idx % 2
            self.wq_idx += 1
            wf = self.f32v("WF", cw)
            wo = self.bf("WQO", cw)
            ldsem = "s_wl2"
            waits = []
            if self.wq_wf_free[0] is not None:
                waits.append(self.wq_wf_free[0])
            self.t("gpsimd",
                   lambda e, w_in=w_in, r0=r0, c0=c0, cw=cw, wf=wf:
                   e.dma_start(out=wf, in_=w_in[r0:r0 + 128, c0:c0 + cw]),
                   waits=waits, inc=ldsem, n=16)
            wl = []
            if self.wq_out_free[0] is not None:
                wl.append(self.wq_out_free[0])
            srcfree, qdone = self.emit_quant2(wf, wo, cw,
                                              waits_first=[self.at(ldsem)],
                                              waits_last=wl, pool=pool)
            self.wq_wf_free[0] = srcfree
            spsem = f"s_wsp{slot}"
            self.t("gpsimd",
                   lambda e, w_out=w_out, r0=r0, c0=c0, cw=cw, wo=wo:
                   e.dma_start(out=w_out[r0:r0 + 128, c0:c0 + cw], in_=wo),
                   waits=[qdone], inc=spsem, n=16)
            self.wq_out_free[0] = self.at(spsem)

    def stage1a(self):
        def ln_views(tt):
            slot = tt % 3
            xr = (self.f32v("XR", 1024, sub=slot * 4096) if slot < 2
                  else self.f32v("X2", 1024))
            if slot < 2:
                bns = self.f32v("BNS", 12, sub=slot * 64)
                bnv = self.f32v("BNV", 16, sub=slot * 64)
            else:
                bns = self.f32v("ST", 12, sub=0)
                bnv = self.f32v("ST", 16, sub=48)
            return dict(
                xr=xr,
                bns0=bns[:, 0:6], bns1=bns[:, 6:12], bns=bns,
                mean=bnv[:, 0:1], var=bnv[:, 1:2], mv=bnv[:, 0:2],
                rstd=bnv[:, 2:3], rstd2=bnv[:, 3:4],
                xln=self.bf("XLN", 1024))

        def emit_ln_stats(tt):
            vw = ln_views(tt)
            waits = []
            if tt >= 3:
                waits.append(f"norm1_{tt - 3}")
            if tt == 1:
                waits.append("bv_mm_done")
            self.t("gpsimd",
                   lambda e, tt=tt, xr=vw["xr"]: e.dma_start(
                       out=xr, in_=self.x[tt * 128:(tt + 1) * 128, :]),
                   waits=waits, inc=f"s_xld{tt % 3}", n=16)

            def ln(v, vw=vw):
                v.bn_stats(vw["bns0"], vw["xr"][:, 0:512])
                v.bn_stats(vw["bns1"], vw["xr"][:, 512:1024])
                return v.bn_aggr(vw["mv"], vw["bns"])
            self.t("vector", ln, waits=[self.at(f"s_xld{tt % 3}")], inc="s_dve")
            self.t("scalar",
                   lambda a, vw=vw: a.activation(vw["rstd"], vw["var"], AF.Sqrt,
                                                 bias=self.f32v("ST", 1, sub=128)),
                   waits=[self.at("s_dve")], inc="s_act")
            self.mark(f"sqrt1_{tt}", "s_act")

        def emit_ln_norm(tt):
            vw = ln_views(tt)

            def norm(v, vw=vw):
                v.reciprocal(vw["rstd2"], vw["rstd"])
                return v.tensor_scalar(vw["xln"], vw["xr"], vw["mean"], vw["rstd2"],
                                       ALU.subtract, ALU.mult)
            nwaits = [f"sqrt1_{tt}"]
            if tt >= 1:
                nwaits.append(self.at(f"s_xsp{(tt - 1) % 2}"))
            self.t("vector", norm, waits=nwaits, inc="s_dve")
            self.mark(f"norm1_{tt}", "s_dve")
            self.t("sync",
                   lambda e, tt=tt, xln=vw["xln"]: e.dma_start(
                       out=self.xln_s[tt * 128:(tt + 1) * 128, :], in_=xln),
                   waits=[self.at("s_dve")], inc=f"s_xsp{tt % 2}", n=16)
            self.mark(f"xspm_{tt}", f"s_xsp{tt % 2}")

        self.xlt_user = [None, None]

        def k_phase(ic):
            slot = ic % 2

            def tld(e, ic=ic, slot=slot):
                last = None
                for kc in range(8):
                    last = e.dma_start(
                        out=self.bf("XLT", 512, sub=slot * 8192 + kc * 1024),
                        in_=self.xln_s[ic * 512:(ic + 1) * 512, kc * 128:(kc + 1) * 128],
                        transpose=True)
                return last
            waits = [f"xspm_{4 * ic + 3}", f"xspm_{4 * ic + 2}"]
            if self.xlt_user[slot] is not None:
                waits.append(self.xlt_user[slot])
            self.t("sync", tld, waits=waits, inc=f"s_tp{slot}", n=8 * 16)
            tmark = self.at(f"s_tp{slot}")

            for cc in range(8):
                bank = cc % 2
                gidx = ic * 8 + cc
                ps = self.psA.ap()[:, bank * 512:(bank + 1) * 512]
                waits = [tmark, "wk_ready", "bv_ready"]
                if gidx >= 2:
                    waits.append(f"kev_{gidx - 2}")

                def kmm(e, cc=cc, ps=ps, slot=slot):
                    last = None
                    for kc in range(8):
                        last = e.matmul(
                            ps, self.bf("WSL", 128, sub=kc * 2048 + cc * 256),
                            self.bf("XLT", 512, sub=slot * 8192 + kc * 1024),
                            start=(kc == 0), stop=(kc == 7))
                    return last
                self.t("tensor", kmm, waits=waits, inc="s_pe")
                self.t("vector",
                       lambda v, cc=cc, ic=ic, ps=ps: v.tensor_scalar(
                           self.bf("KT", 512, sub=cc * 8192 + ic * 1024), ps,
                           self.bK_col[:, cc:cc + 1], None, ALU.add),
                       waits=[self.at("s_pe")], inc="s_dve")
                self.mark(f"kev_{gidx}", "s_dve")
            self.mark(f"kvmm_K_{ic}", "s_pe")
            self.xlt_user[slot] = f"kvmm_K_{ic}"
            self.emit_wquant(2)

        NT = T // 128
        emit_ln_stats(0)
        for tt in range(1, NT):
            emit_ln_stats(tt)
            emit_ln_norm(tt - 1)
            if tt % 4 == 0:
                k_phase(tt // 4 - 1)
        emit_ln_norm(NT - 1)
        k_phase(ICN - 1)
        self.ln1_done_waits = [self.at("s_xsp0"), self.at("s_xsp1")]

        for phase in ("V",):
            if phase == "V":
                def ld_wv(e):
                    last = None
                    for kc in range(8):
                        last = e.dma_start(out=self.bf("WSL", 1024, sub=kc * 2048),
                                           in_=self.wqkvT[kc * 128:(kc + 1) * 128, 2048:3072])
                    return last
                self.t("gpsimd", ld_wv, waits=[self.at("s_pe")], inc="s_wld", n=8 * 16)
                self.mark("wv_ready", "s_wld")
            for ic in range(ICN):
                slot = ic % 2
                def tld(e, ic=ic, slot=slot):
                    last = None
                    for kc in range(8):
                        last = e.dma_start(
                            out=self.bf("XLT", 512, sub=slot * 8192 + kc * 1024),
                            in_=self.xln_s[ic * 512:(ic + 1) * 512, kc * 128:(kc + 1) * 128],
                            transpose=True)
                    return last
                waits = list(self.ln1_done_waits)
                if self.xlt_user[slot] is not None:
                    waits.append(self.xlt_user[slot])
                self.t("sync", tld, waits=waits, inc=f"s_tp{slot}", n=8 * 16)
                tmark = self.at(f"s_tp{slot}")

                if True:
                    for ts_ in range(4):
                        for vc in range(2):
                            bank = (ts_ * 2 + vc) % 2
                            gidx = (ic * 4 + ts_) * 2 + vc
                            ps = self.psB.ap()[:, bank * 512:(bank + 1) * 512]
                            waits = [tmark, "wv_ready", "bv_ready"]
                            if gidx >= 2:
                                waits.append(f"vev_{gidx - 2}")

                            def vmm(e, ts_=ts_, vc=vc, ps=ps, slot=slot):
                                last = None
                                for kc in range(8):
                                    last = e.matmul(
                                        ps,
                                        self.bf("XLT", 128,
                                                sub=slot * 8192 + kc * 1024 + ts_ * 256),
                                        self.bf("WSL", 512, sub=kc * 2048 + vc * 1024),
                                        start=(kc == 0), stop=(kc == 7))
                                return last
                            self.t("tensor", vmm, waits=waits, inc="s_pe")
                            jc = ic * 4 + ts_

                            def vev(v, jc=jc, vc=vc, ps=ps):
                                dst = self.bf("V", 8 * 65, sub=(jc * 16 + vc * 8) * 130)
                                dst = dst.rearrange("p (h e) -> p h e", e=65)[:, :, 0:64]
                                src = ps.rearrange("p (h e) -> p h e", e=64)
                                bt = self.bias_tile[:, vc * 512:(vc + 1) * 512]
                                bt = bt.rearrange("p (h e) -> p h e", e=64)
                                return v.tensor_tensor(dst, src, bt, ALU.add)
                            self.t("vector", vev, waits=[self.at("s_pe")], inc="s_dve")
                            self.mark(f"vev_{gidx}", "s_dve")
                self.mark(f"kvmm_{phase}_{ic}", "s_pe")
                self.xlt_user[slot] = f"kvmm_{phase}_{ic}"
                self.emit_wquant(2)
        self.mark("kv_done", "s_dve")

    def stage1b(self):
        def ld_wq(e):
            last = None
            for kc in range(8):
                last = e.dma_start(out=self.bf("WSL", 1024, sub=kc * 2048),
                                   in_=self.wqkvT[kc * 128:(kc + 1) * 128, 0:1024])
            return last
        self.t("gpsimd", ld_wq, waits=["kv_done", self.at("s_pe")], inc="s_wld", n=8 * 16)
        self.mark("wq_ready", "s_wld")

        mxc = self.mxc
        sum_rows = [self.psB.ap()[0:1, 0:512], self.psB.ap()[0:1, 512:1024],
                    self.psC.ap()[0:1, 0:512], self.psC.ap()[0:1, 512:1024]]
        aidx = [0]
        sq_user = [None, None]
        qev_marks = []

        def emit_sumrow(src2, half, col, extra=()):
            i = aidx[0]
            aidx[0] += 1
            pss = sum_rows[i % 4]

            def ones_mm(e, half=half, src2=src2, pss=pss):
                return e.matmul(pss, self.ones_col[half * 64:(half + 1) * 64, :],
                                src2[half * 64:(half + 1) * 64, :],
                                start=True, stop=True)
            w = [self.at("s_dve")] + list(extra)
            if i >= 4:
                w.append(f"redm_{i - 4}")
            if i < 4:
                w.append("kv_done")
            self.t("tensor", ones_mm, waits=w, inc="s_pe")
            self.mark(f"sqmm_{i}", "s_pe")

            def redm(v, pss=pss, col=col):
                with self.nc.allow_low_precision(reason="bf16 collect of softmax bound"):
                    return v.reduce_max(mxc[:, col:col + 1], pss, AX.X)
            self.t("vector", redm, waits=[self.at("s_pe")], inc="s_dve2")
            self.mark(f"redm_{i}", "s_dve2")

        for ic in range(ICN):
            slot = ic % 2
            def tld(e, ic=ic, slot=slot):
                last = None
                for kc in range(8):
                    last = e.dma_start(
                        out=self.bf("XLT", 512, sub=slot * 8192 + kc * 1024),
                        in_=self.xln_s[ic * 512:(ic + 1) * 512, kc * 128:(kc + 1) * 128],
                        transpose=True)
                return last
            waits = []
            if self.xlt_user[slot] is not None:
                waits.append(self.xlt_user[slot])
            self.t("sync", tld, waits=waits, inc=f"s_tp{slot}", n=8 * 16)
            tmark = self.at(f"s_tp{slot}")

            for cc in range(8):
                bank = cc % 2
                qi = ic * 8 + cc
                ps = self.psA.ap()[:, bank * 512:(bank + 1) * 512]
                waits = [tmark, "wq_ready"]
                if len(qev_marks) >= 2:
                    waits.append(qev_marks[-2])

                def qmm(e, cc=cc, ps=ps, slot=slot):
                    last = None
                    for kc in range(8):
                        last = e.matmul(ps, self.bf("WSL", 128, sub=kc * 2048 + cc * 256),
                                        self.bf("XLT", 512, sub=slot * 8192 + kc * 1024),
                                        start=(kc == 0), stop=(kc == 7))
                    return last
                self.t("tensor", qmm, waits=waits, inc="s_pe")
                qtile = self.bf("XLN", 512, sub=bank * 1024)

                def qev(v, cc=cc, ps=ps, qtile=qtile):
                    return v.tensor_scalar(qtile, ps, self.bQ_col[:, cc:cc + 1], 0.125,
                                           ALU.add, ALU.mult)
                qpar = f"s_qsp{(ic * 8 + cc) % 2}"
                evw = [self.at("s_pe"), self.at(qpar)]
                self.t("vector", qev, waits=evw, inc="s_dve")
                self.mark(f"qev_{qi}", "s_dve")
                qev_marks.append(f"qev_{qi}")
                self.t("sync",
                       lambda e, cc=cc, ic=ic, qtile=qtile: e.dma_start(
                           out=self.qT_s[cc * 128:(cc + 1) * 128, ic * 512:(ic + 1) * 512],
                           in_=qtile),
                       waits=[self.at("s_dve")], inc=qpar, n=16)

                q2 = self.bf("X2", 512, sub=(qi % 2) * 1024)
                sqw = [self.at("s_dve")]
                if sq_user[qi % 2] is not None:
                    sqw.append(sq_user[qi % 2])
                self.t("vector",
                       lambda v, qtile=qtile, q2=q2: v.tensor_tensor(q2, qtile, qtile,
                                                                     ALU.mult),
                       waits=sqw, inc="s_dve")
                for half in range(2):
                    emit_sumrow(q2, half, (cc * 2 + half) * 8 + ic)
                sq_user[qi % 2] = f"sqmm_{aidx[0] - 1}"
            self.mark(f"qmm_{ic}", "s_pe")
            self.xlt_user[slot] = f"qmm_{ic}"
            self.emit_wquant(2)

        for cc in range(8):
            for js in range(ICN):
                ki = cc * 8 + js
                k2 = self.bf("X2", 512, sub=(ki % 2) * 1024)
                kw = []
                if sq_user[ki % 2] is not None:
                    kw.append(sq_user[ki % 2])

                def ksq(v, cc=cc, js=js, k2=k2):
                    sl = self.bf("KT", 512, sub=cc * 8192 + js * 1024)
                    return v.tensor_tensor(k2, sl, sl, ALU.mult)
                self.t("vector", ksq, waits=kw, inc="s_dve")
                for half in range(2):
                    emit_sumrow(k2, half, 128 + (cc * 2 + half) * 8 + js)
                sq_user[ki % 2] = f"sqmm_{aidx[0] - 1}"

        def fmax(v):
            qv = mxc[:, 0:128].rearrange("p (h i) -> p h i", i=8)
            kv = mxc[:, 128:256].rearrange("p (h i) -> p h i", i=8)
            v.tensor_reduce(self.mq2, qv, AX.X, ALU.max)
            return v.tensor_reduce(self.mk2, kv, AX.X, ALU.max)
        self.t("vector", fmax, waits=[self.at("s_dve2")], inc="s_dve")

        self.t("vector",
               lambda v: v.tensor_tensor(self.sm1[:, 16:32], self.mq2, self.mk2, ALU.mult),
               waits=[self.at("s_dve")], inc="s_dve")
        self.t("scalar",
               lambda a: a.activation(self.sm1[:, 32:48], self.sm1[:, 16:32], AF.Sqrt),
               waits=[self.at("s_dve")], inc="s_act")
        self.t("vector",
               lambda v: v.tensor_scalar(self.cneg_bf, self.sm1[:, 32:48], -1.0, None,
                                         ALU.mult),
               waits=[self.at("s_act")], inc="s_dve")

        def cbc(e):
            return e.matmul(self.psOB.ap()[:, 0:16], self.ones_row, self.cneg_bf,
                            start=True, stop=True)
        self.t("tensor", cbc, waits=[self.at("s_dve"), self.at("s_dve2")], inc="s_pe")
        self.t("vector",
               lambda v: v.tensor_copy(self.cneg, self.psOB.ap()[:, 0:16]),
               waits=[self.at("s_pe")], inc="s_dve2")
        self.mark("cneg_ready", "s_dve2")
        self.mark("s1b_evacs", "s_dve")
        self.mark("s1b_evacs2", "s_dve2")

    def stage2(self):
        def ld_wo(e):
            last = None
            for dc in range(8):
                last = e.dma_start(out=self.bf("WSL", 1024, sub=dc * 2048),
                                   in_=self.woT[dc * 128:(dc + 1) * 128, :])
            return last
        self.t("gpsimd", ld_wo, waits=["kv_done", self.at("s_pe")], inc="s_wld", n=8 * 16)
        self.mark("wo_ready", "s_wld")

        self.t("sync",
               lambda e: e.dma_start(out=self.f32v("BOROW", 1024)[0:1, :],
                                     in_=self.bo.unsqueeze(0)),
               waits=[self.at("s_pe")], inc="s_ld", n=16)

        def bo_mm(e):
            br = self.f32v("BOROW", 1024)[0:1, :]
            e.matmul(self.psC.ap()[:, 0:512], self.ones_row_f, br[:, 0:512],
                     start=True, stop=True)
            return e.matmul(self.psC.ap()[:, 512:1024], self.ones_row_f,
                            br[:, 512:1024], start=True, stop=True)
        self.t("tensor", bo_mm,
               waits=["s1b_evacs2", "s1b_evacs", self.at("s_ld")], inc="s_pe")

        def bo_ev(v):
            return v.tensor_copy(self.bias_tile, self.psC.ap()[:])
        self.t("vector", bo_ev, waits=[self.at("s_pe")], inc="s_dve")
        self.mark("bo_ready", "s_dve")

        ring = [self.psA, self.psB, self.psC]
        ring_consumer = ["s1b_evacs", "kv_done", "bo_ready"]
        ring_i = [0]

        def ring_alloc():
            idx = ring_i[0] % 3
            ring_i[0] += 1
            return idx, ring_consumer[idx]

        psO_free = ["bo_ready"]
        pslot_user = [None, None, None]
        pcount = [0]
        tail = [None]
        pending_op = [None]
        s2xr_user = ["cneg_ready", "cneg_ready"]

        def emit_tail_recips():
            td = tail[0]

            def rsf(a):
                lns = self.f32v("OE", 512)[0:1, :]
                with self.nc.allow_low_precision(reason="1/s as bf16 matmul operand"):
                    a.activation(lns, self.psOA.ap()[64:65, :], AF.Ln)
                    a.activation(self.rs_bf[:, 0:512], lns, AF.Exp, scale=-1.0)
                    a.activation(lns, self.psOB.ap()[64:65, :], AF.Ln)
                    return a.activation(self.rs_bf[:, 512:1024], lns, AF.Exp,
                                        scale=-1.0)
            self.t("scalar", rsf, waits=[td["avdone"]], inc="s_act")
            td["rsm"] = self.at("s_act")

        def emit_tail_rest():
            td = tail[0]
            gi, pair = td["gi"], td["pair"]
            iR, wR = ring_alloc()
            sR = ring[iR]

            def rsbc(e, sR=sR):
                e.matmul(sR.ap()[0:64, 0:512], self.ones_row[0:1, 0:64],
                         self.rs_bf[:, 0:512], start=True, stop=True,
                         tile_position=(0, 0), skip_group_check=True)
                return e.matmul(sR.ap()[64:128, 0:512], self.ones_row[0:1, 0:64],
                                self.rs_bf[:, 512:1024], start=True, stop=True,
                                tile_position=(0, 64), skip_group_check=True)
            self.t("tensor", rsbc, waits=[td["rsm"], wR], inc="s_pe")

            oe = self.f32v("OE", 512)

            def oev(v, pair=pair, oe=oe, sR=sR):
                v.tensor_copy(oe[0:64, :], self.psOA.ap()[0:64, :])
                v.tensor_copy(oe[64:128, :], self.psOB.ap()[0:64, :])
                return v.tensor_tensor(self.bf("OT", 512, sub=pair * 1024), oe,
                                       sR.ap()[:, 0:512], ALU.mult)
            ow = [self.at("s_pe")]
            if pair == 0 and td["ic"] > 0:
                ow.append(f"opdone_{td['ic'] - 1}")
            self.t("vector", oev, waits=ow, inc="s_dve")
            self.mark(f"oev_{gi}", "s_dve")
            ring_consumer[iR] = f"oev_{gi}"
            psO_free[0] = f"oev_{gi}"
            tail[0] = None

        def emit_op_chunk():
            ic_o, isub = pending_op[0]["ic"], pending_op[0]["isub"]
            pending_op[0]["isub"] += 1
            if pending_op[0]["isub"] >= 4:
                pending_op[0] = None
            trow = ic_o * 512 + isub * 128
            slot = isub % 2
            xr = self.f32v("XR", 1024, sub=slot * 4096)
            self.t("sync",
                   lambda e, trow=trow, xr=xr: e.dma_start(
                       out=xr, in_=self.x[trow:trow + 128, :]),
                   waits=[s2xr_user[slot]], inc=f"s_xld{slot}", n=16)
            iO, wO = ring_alloc()
            sO = ring[iO]

            def opmm(e, isub=isub, sO=sO):
                last = None
                for e2 in range(2):
                    for p in range(8):
                        last = e.matmul(
                            sO.ap()[:, e2 * 512:(e2 + 1) * 512],
                            self.bf("OT", 128, sub=p * 1024 + isub * 256),
                            self.bf("WSL", 512, sub=p * 2048 + e2 * 1024),
                            start=(p == 0), stop=(p == 7))
                return last
            self.t("tensor", opmm,
                   waits=[f"oev_{ic_o * NPAIR + NPAIR - 1}", "wo_ready", wO],
                   inc="s_pe")
            if isub == 3:
                self.mark(f"opdone_{ic_o}", "s_pe")
            opm = self.at("s_pe")
            x1t = self.f32v("X1T", 1024)

            def opev(v, x1t=x1t, xr=xr, sO=sO):
                t1 = self.f32v("T1", 512)
                last = None
                for e2 in range(2):
                    ps = sO.ap()[:, e2 * 512:(e2 + 1) * 512]
                    v.tensor_tensor(t1, ps, self.bias_tile[:, e2 * 512:(e2 + 1) * 512],
                                    ALU.add)
                    last = v.tensor_tensor(x1t[:, e2 * 512:(e2 + 1) * 512], t1,
                                           xr[:, e2 * 512:(e2 + 1) * 512], ALU.add)
                return last
            self.t("vector", opev,
                   waits=[opm, self.at(f"s_xld{slot}"), self.at("s_x1sp")],
                   inc="s_dve")
            self.mark(f"opev_{ic_o}_{isub}", "s_dve")
            ring_consumer[iO] = f"opev_{ic_o}_{isub}"
            s2xr_user[slot] = f"opev_{ic_o}_{isub}"
            self.t("sync",
                   lambda e, trow=trow, x1t=x1t: e.dma_start(
                       out=self.x1_s[trow:trow + 128, :], in_=x1t),
                   waits=[self.at("s_dve")], inc="s_x1sp", n=16)
            if isub == 3:
                self.mark(f"x1spm_{ic_o}", "s_x1sp")

        for ic in range(ICN):
            for pair in range(NPAIR):
                hA, hB = 2 * pair, 2 * pair + 1
                gi = ic * NPAIR + pair
                qslot = gi % 2
                qs = self.bf("QS", 512, sub=qslot * 1024)
                waits = [self.at("s_qsp0"), self.at("s_qsp1"), "cneg_ready", "bo_ready"]
                if gi >= 2:
                    waits.append(self.marks[f"qfree_{gi - 2}"])
                qldsem = f"s_tpq{gi % 2}"
                self.t("sync",
                       lambda e, pair=pair, ic=ic, qs=qs: e.dma_start(
                           out=qs, in_=self.qT_s[pair * 128:(pair + 1) * 128,
                                                 ic * 512:(ic + 1) * 512]),
                       waits=waits, inc=qldsem, n=16)
                qld = self.at(qldsem)

                ngrp = JCN // JG
                pend = []

                def emit_attnv(jg, pslot, gi=gi, hA=hA, hB=hB):
                    pa = self.bf("PA", 1024, sub=pslot * 2048)
                    pb = self.bf("PB", 1024, sub=pslot * 2048)

                    def avmm(e, jg=jg, pa=pa, pb=pb):
                        last = None
                        for u in range(JG):
                            jc = jg * JG + u
                            st, sp = (jc == 0), (jc == JCN - 1)
                            vA = self.bf("V", 65, sub=(jc * 16 + hA) * 130)
                            vB = self.bf("V", 65, sub=(jc * 16 + hB) * 130)
                            e.matmul(self.psOA.ap()[0:65, :], vA,
                                     pa[:, u * 512:(u + 1) * 512], start=st, stop=sp)
                            last = e.matmul(self.psOB.ap()[0:65, :], vB,
                                            pb[:, u * 512:(u + 1) * 512],
                                            start=st, stop=sp)
                        return last
                    awaits = [f"expA_{gi}_{jg}", f"expB_{gi}_{jg}"]
                    if jg == 0:
                        awaits.append(psO_free[0])
                    self.t("tensor", avmm, waits=awaits, inc="s_pe")
                    self.mark(f"av_{gi}_{jg}", "s_pe")

                for jg in range(ngrp):
                    pslot = pcount[0] % 3
                    pcount[0] += 1
                    iA, wA = ring_alloc()
                    iB, wB = ring_alloc()
                    sA, sB = ring[iA], ring[iB]

                    def smm(e, jg=jg, pair=pair, qs=qs, sA=sA, sB=sB):
                        last = None
                        for u in range(JG):
                            jc = jg * JG + u
                            kv = self.bf("KT", 128, sub=pair * 8192 + jc * 256)
                            last = e.matmul(sA.ap()[:, u * 512:(u + 1) * 512],
                                            kv[0:64, :], qs[0:64, :],
                                            start=True, stop=True,
                                            tile_position=(0, 0))
                            last = e.matmul(sB.ap()[:, u * 512:(u + 1) * 512],
                                            kv[64:128, :], qs[64:128, :],
                                            start=True, stop=True,
                                            tile_position=(64, 0))
                        return last
                    self.t("tensor", smm, waits=[qld, wA, wB], inc="s_pe")
                    smark = self.at("s_pe")

                    pa = self.bf("PA", 1024, sub=pslot * 2048)
                    pb = self.bf("PB", 1024, sub=pslot * 2048)
                    ewA = [smark]
                    ewB = []
                    if pslot_user[pslot] is not None:
                        ewA.append(pslot_user[pslot])
                    pslot_user[pslot] = f"av_{gi}_{jg}"
                    self.t("scalar",
                           lambda a, pa=pa, hA=hA, sA=sA: a.activation(
                               pa, sA.ap()[:], AF.Exp, bias=self.cneg[:, hA:hA + 1]),
                           waits=ewA, inc="s_act")
                    self.mark(f"expA_{gi}_{jg}", "s_act")
                    ring_consumer[iA] = f"expA_{gi}_{jg}"
                    self.t("scalar",
                           lambda a, pb=pb, hB=hB, sB=sB: a.activation(
                               pb, sB.ap()[:], AF.Exp, bias=self.cneg[:, hB:hB + 1]),
                           waits=ewB, inc="s_act")
                    self.mark(f"expB_{gi}_{jg}", "s_act")
                    ring_consumer[iB] = f"expB_{gi}_{jg}"

                    pend.append((jg, pslot))
                    if len(pend) > 2:
                        emit_attnv(*pend.pop(0))
                    if jg == 0 and tail[0] is not None:
                        emit_tail_recips()
                    elif jg == 1 and tail[0] is not None:
                        emit_tail_rest()
                    elif jg in (2, 4, 6, 8) and pending_op[0] is not None:
                        emit_op_chunk()
                    elif jg in (3, 7, 11):
                        self.emit_wquant(1)
                while pend:
                    emit_attnv(*pend.pop(0))
                self.mark(f"qfree_{gi}", "s_pe")
                tail[0] = {"gi": gi, "pair": pair, "ic": ic,
                           "avdone": self.at("s_pe")}

            pending_op[0] = {"ic": ic, "isub": 0}

        emit_tail_recips()
        emit_tail_rest()
        while pending_op[0] is not None:
            emit_op_chunk()
        self.mark("stage2_done", "s_x1sp")
        self.mark("attn_pe_done", "s_pe")
        self.mark("stage2_dve", "s_dve")
        self.mark("stage2_act", "s_act")

    def stage3(self):
        while self.wq_idx < len(self.wq_jobs):
            self.emit_wquant(1)

        def ld_qw(e):
            last = None
            for kc in range(8):
                last = e.dma_start(out=self.bf("QWG", 2816, sub=kc * 5632),
                                   in_=self.qwg_s[:, kc * 128:(kc + 1) * 128],
                                   transpose=True)
                last = e.dma_start(out=self.bf("QWU", 2816, sub=kc * 5632),
                                   in_=self.qwu_s[:, kc * 128:(kc + 1) * 128],
                                   transpose=True)
            for icc in range(22):
                last = e.dma_start(out=self.bf("QWD", 1024, sub=icc * 2048),
                                   in_=self.qwd_s[:, icc * 128:(icc + 1) * 128],
                                   transpose=True)
            return last
        self.t("sync", ld_qw,
               waits=[self.at("s_wsp0"), self.at("s_wsp1"), "attn_pe_done",
                      "stage2_done"],
               inc="s_tpw", n=38 * 16)
        self.mark("qw_ready", "s_tpw")

        CCH = [(i * 512, min(512, II - i * 512)) for i in range((II + 511) // 512)]
        self.xr_user = ["stage2_dve", "stage2_dve"]

        def emit_ln2(tc):
            for ts_ in range(4):
                trow = tc * 512 + ts_ * 128
                t = tc * 4 + ts_
                slot = t % 2
                xr = self.f32v("XR", 1024, sub=slot * 4096)
                lw = [self.xr_user[slot], f"x1spm_{tc}"]
                self.t("sync",
                       lambda e, trow=trow, xr=xr: e.dma_start(
                           out=xr, in_=self.x1_s[trow:trow + 128, :]),
                       waits=lw, inc=f"s_xld{slot}", n=16)
                bns0 = self.f32v("BNS", 6, sub=slot * 64)
                bns1 = self.f32v("BNS", 6, sub=slot * 64 + 24)
                bns = self.f32v("BNS", 12, sub=slot * 64)
                mean = self.f32v("BNV", 1, sub=slot * 64)
                var = self.f32v("BNV", 1, sub=slot * 64 + 4)
                mv = self.f32v("BNV", 2, sub=slot * 64)
                rstd = self.f32v("BNV", 1, sub=slot * 64 + 8)
                rstd2 = self.f32v("BNV", 1, sub=slot * 64 + 12)

                def ln(v, xr=xr, bns0=bns0, bns1=bns1, bns=bns, mv=mv):
                    v.bn_stats(bns0, xr[:, 0:512])
                    v.bn_stats(bns1, xr[:, 512:1024])
                    return v.bn_aggr(mv, bns)
                self.t("vector", ln, waits=[self.at(f"s_xld{slot}")], inc="s_dve")
                self.t("scalar",
                       lambda a, var=var, rstd=rstd: a.activation(
                           rstd, var, AF.Sqrt, bias=self.f32v("ST", 1, sub=128)),
                       waits=[self.at("s_dve")], inc="s_act")
                self.mark(f"sqrt3_{t}", "s_act")
                x2 = self.f32v("X2", 1024)

                def norm(v, xr=xr, mean=mean, rstd=rstd, rstd2=rstd2, x2=x2):
                    v.reciprocal(rstd2, rstd)
                    return v.tensor_scalar(x2, xr, mean, rstd2,
                                           ALU.subtract, ALU.mult)
                nw = [f"sqrt3_{t}"]
                if t == 0:
                    nw.append("stage2_done")
                self.t("vector", norm, waits=nw, inc="s_dve")
                self.mark(f"ln3n_{t}", "s_dve")
                self.xr_user[slot] = f"ln3n_{t}"

                xq = self.bf("XLN", 1024)
                qw0 = [self.at("s_xqsp")]
                for half in range(2):
                    self.emit_quant2(x2[:, half * 512:(half + 1) * 512],
                                     xq[:, half * 512:(half + 1) * 512], 512,
                                     waits_first=qw0 if half == 0 else (),
                                     inc_last="s_dve", pool=False)
                self.t("sync",
                       lambda e, trow=trow, xq=xq: e.dma_start(
                           out=self.xq_s[trow:trow + 128, :], in_=xq),
                       waits=[self.at("s_dve")], inc="s_xqsp", n=16)

        def emit_xqtld(tc):
            def xqtld(e, tc=tc):
                last = None
                for kc in range(8):
                    last = e.dma_start(
                        out=self.bf("XQT", 512, sub=kc * 1024),
                        in_=self.xq_s[tc * 512:(tc + 1) * 512, kc * 128:(kc + 1) * 128],
                        transpose=True)
                return last
            waits = [self.at("s_xqsp")]
            if tc == 0:
                waits += ["attn_pe_done", "qw_ready"]
            else:
                waits.append(f"gu_done_{tc - 1}")
            self.t("sync", xqtld, waits=waits, inc="s_tpx", n=8 * 16)
            self.mark(f"xqm_{tc}", "s_tpx")

        def emit_hqtld(tc, part, i0, i1, extra=()):
            def hqtld(e, tc=tc, i0=i0, i1=i1):
                last = None
                for icc in range(i0, i1):
                    last = e.dma_start(
                        out=self.bf("HQT", 512, sub=icc * 1024),
                        in_=self.hq_s[tc * 512:(tc + 1) * 512, icc * 128:(icc + 1) * 128],
                        transpose=True)
                return last
            waits = [self.at("s_hqsp0"), self.at("s_hqsp1")] + list(extra)
            self.t("sync", hqtld, waits=waits, inc="s_tph", n=(i1 - i0) * 16)
            self.mark(f"hqld{part}_{tc}", "s_tph")

        DPS = [self.psA, self.psB, self.psC, self.psA]

        def emit_dchain(tc, ts_, part, i0, i1):
            pst = DPS[ts_]

            def dmm(e, ts_=ts_, i0=i0, i1=i1, pst=pst):
                last = None
                for e2 in range(2):
                    for icc in range(i0, i1):
                        last = e.matmul(
                            pst.ap()[:, e2 * 512:(e2 + 1) * 512],
                            self.bf("HQT", 128, sub=icc * 1024 + ts_ * 256),
                            self.bf("QWD", 512, sub=icc * 2048 + e2 * 1024),
                            start=(icc == 0), stop=(icc == 21))
                return last
            if part == "A":
                dw = [f"hqldA_{tc}"]
                if ts_ == 3:
                    dw.append(f"dev3_{tc}_0")
                elif tc == 0:
                    dw += ["stage2_act", "stage2_dve"]
                elif ts_ == 0:
                    dw.append(f"dev3_{tc - 1}_3")
                else:
                    dw.append(f"dev3_{tc - 1}_{ts_}")
            else:
                dw = [f"hqldB_{tc}", f"hqldC_{tc}"]
            self.t("tensor", dmm, waits=dw, inc="s_pe")
            self.mark(f"dmm{part}_{tc}_{ts_}", "s_pe")

        def emit_gu(tc):
            for ci, (c0, cw) in enumerate(CCH):
                for ts_ in range(4):
                    ch = tc * 24 + ci * 4 + ts_

                    def gmm(e, ts_=ts_, c0=c0, cw=cw):
                        last = None
                        for kc in range(8):
                            last = e.matmul(
                                self.psOA.ap()[:, 0:cw],
                                self.bf("XQT", 128, sub=kc * 1024 + ts_ * 256),
                                self.bf("QWG", cw, sub=kc * 5632 + c0 * 2),
                                start=(kc == 0), stop=(kc == 7))
                        return last
                    gw = [f"xqm_{tc}"]
                    if ch == 0:
                        gw.append("stage2_dve")
                    else:
                        gw.append(f"silu_{ch - 1}")
                    self.t("tensor", gmm, waits=gw, inc="s_pe")
                    gm = self.at("s_pe")
                    sgt = self.f32v("SGT", 512)
                    sw = [gm]
                    if ch > 0:
                        sw.append(f"hm_{ch - 1}")
                    self.t("scalar",
                           lambda a, cw=cw, sgt=sgt: a.activation(
                               sgt[:, 0:cw], self.psOA.ap()[:, 0:cw], AF.Silu),
                           waits=sw, inc="s_act")
                    self.mark(f"silu_{ch}", "s_act")

                    def umm(e, ts_=ts_, c0=c0, cw=cw):
                        last = None
                        for kc in range(8):
                            last = e.matmul(
                                self.psOB.ap()[:, 0:cw],
                                self.bf("XQT", 128, sub=kc * 1024 + ts_ * 256),
                                self.bf("QWU", cw, sub=kc * 5632 + c0 * 2),
                                start=(kc == 0), stop=(kc == 7))
                        return last
                    uw = [f"xqm_{tc}"]
                    if ch == 0:
                        uw.append("stage2_dve")
                    else:
                        uw.append(f"hm_{ch - 1}")
                    self.t("tensor", umm, waits=uw, inc="s_pe")
                    um = self.at("s_pe")
                    ht = self.f32v("HT", 512)

                    def hmul(v, cw=cw, sgt=sgt, ht=ht):
                        return v.tensor_tensor(ht[:, 0:cw], sgt[:, 0:cw],
                                               self.psOB.ap()[:, 0:cw], ALU.mult)
                    self.t("vector", hmul, waits=[um, f"silu_{ch}"], inc="s_dve")
                    self.mark(f"hm_{ch}", "s_dve")
                    hq = self.bf("HQ", 512, sub=(ch % 2) * 1024)
                    hqsem = f"s_hqsp{ch % 2}"
                    self.emit_quant2(ht[:, 0:cw], hq[:, 0:cw], cw,
                                     waits_first=[self.at(hqsem)],
                                     inc_last="s_dve", pool=False)
                    trow = tc * 512 + ts_ * 128
                    self.t("sync",
                           lambda e, trow=trow, c0=c0, cw=cw, hq=hq: e.dma_start(
                               out=self.hq_s[trow:trow + 128, c0:c0 + cw],
                               in_=hq[:, 0:cw]),
                           waits=[self.at("s_dve")], inc=hqsem, n=16)
                if ci == 2:
                    emit_hqtld(tc, "A", 0, 12,
                               [f"dmmA_{tc - 1}_3"] if tc else [])
                elif ci == 3:
                    emit_dchain(tc, 0, "A", 0, 12)
                elif ci == 4:
                    emit_hqtld(tc, "B", 12, 18,
                               [f"dmmB_{tc - 1}_3"] if tc else [])
                    emit_dchain(tc, 1, "A", 0, 12)
                    emit_dchain(tc, 2, "A", 0, 12)
            emit_hqtld(tc, "C", 18, 22,
                       [f"dmmB_{tc - 1}_3"] if tc else [])
            self.mark(f"gu_done_{tc}", "s_pe")

        def emit_dmm(tc):
            for ts_ in range(4):
                t = tc * 4 + ts_
                slot = t % 2
                trow = tc * 512 + ts_ * 128
                xr = self.f32v("XR", 1024, sub=slot * 4096)
                self.t("sync",
                       lambda e, trow=trow, xr=xr: e.dma_start(
                           out=xr, in_=self.x1_s[trow:trow + 128, :]),
                       waits=[self.xr_user[slot]], inc=f"s_xld{slot}", n=16)
                if ts_ == 3:
                    emit_dchain(tc, 3, "A", 0, 12)
                emit_dchain(tc, ts_, "B", 12, 22)
                outt = self.f32v("OUTT", 1024)

                def dev(v, outt=outt, xr=xr, pst=DPS[ts_]):
                    return v.tensor_tensor(outt, pst.ap()[:], xr, ALU.add)
                self.t("vector", dev,
                       waits=[self.at("s_pe"), self.at(f"s_xld{slot}"),
                              self.at("s_out")],
                       inc="s_dve")
                self.mark(f"dev3_{tc}_{ts_}", "s_dve")
                self.xr_user[slot] = f"dev3_{tc}_{ts_}"
                self.t("sync",
                       lambda e, trow=trow, outt=outt: e.dma_start(
                           out=self.out[trow:trow + 128, :], in_=outt),
                       waits=[self.at("s_dve")], inc="s_out", n=16)

        emit_ln2(0)
        emit_xqtld(0)
        for tc in range(TCN):
            if tc + 1 < TCN:
                emit_ln2(tc + 1)
            emit_gu(tc)
            if tc + 1 < TCN:
                emit_xqtld(tc + 1)
            emit_dmm(tc)

        self.t("sync", lambda e: None, waits=[self.at("s_out")])

    def finalize(self):
        nc = self.nc
        per_eng = {"sync": [], "tensor": [], "vector": [], "scalar": [], "gpsimd": []}
        for task in self.tasks:
            key = {"pool": "gpsimd", "scalardma": "scalar"}.get(task.eng, task.eng)
            per_eng[key].append(task)
        last_wait = {}

        class DmaProxy:
            def __init__(self, eng, sem):
                self._eng, self._sem = eng, sem
                self.count = 0

            def dma_start(self, **kw):
                self.count += 1
                return self._eng.dma_start(**kw).then_inc(self._sem, 16)

            def __getattr__(self, name):
                return getattr(self._eng, name)

        class SerialProxy:

            _EXEMPT = {"wait_ge", "then_inc", "dma_start"}

            def __init__(self, eng, selfsem):
                object.__setattr__(self, "_eng", eng)
                object.__setattr__(self, "_sem", selfsem)
                object.__setattr__(self, "_count", 0)
                object.__setattr__(self, "_prev", None)
                object.__setattr__(self, "_token", None)
                object.__setattr__(self, "_emitted", {})

            def _pre_op(self):
                if self._prev is not None:
                    object.__setattr__(self, "_count", self._count + 1)
                    self._prev.then_inc(self._sem, 1)
                    object.__setattr__(self, "_token", (self._sem, self._count))
                    object.__setattr__(self, "_prev", None)
                if self._token is not None:
                    self._eng.wait_ge(*self._token)
                    object.__setattr__(self, "_token", None)

            def wait_ge(self, sem, val):
                return self._eng.wait_ge(sem, val)

            def finish_task(self, inc_sem, inc_n, planned_total):
                ins = self._prev
                assert ins is not None
                ins.then_inc(inc_sem, inc_n)
                object.__setattr__(self, "_prev", None)
                object.__setattr__(self, "_token", (inc_sem, planned_total))

            def __getattr__(self, name):
                target = getattr(self._eng, name)
                if name.startswith("_") or not callable(target):
                    return target

                def wrapped(*a, **kw):
                    self._pre_op()
                    ins = target(*a, **kw)
                    object.__setattr__(self, "_prev", ins)
                    return ins
                return wrapped

        def runner(eng_name):
            def run(eng_obj):
                emitted = {}
                proxy = None
                if eng_name in ("vector", "scalar", "gpsimd"):
                    proxy = SerialProxy(eng_obj, self.sems[f"s_self_{eng_name}"])
                for task in per_eng[eng_name]:
                    for s, vv in task.waits:
                        if vv <= 0:
                            continue
                        key = (eng_name, s)
                        if last_wait.get(key, -1) >= vv:
                            continue
                        last_wait[key] = vv
                        eng_obj.wait_ge(self.sems[s], vv)
                    if task.inc is not None and task.eng in ("sync", "gpsimd",
                                                              "scalardma"):
                        dproxy = DmaProxy(eng_obj, self.sems[task.inc])
                        task.fn(dproxy)
                        assert dproxy.count * 16 == task.n, \
                            f"{eng_name} dma count {dproxy.count}*16 != {task.n}"
                    elif proxy is not None:
                        ret = task.fn(proxy)
                        if task.inc is not None:
                            assert ret is not None, f"{eng_name} needs instr ({task.inc})"
                            emitted[task.inc] = emitted.get(task.inc, 0) + task.n
                            proxy.finish_task(self.sems[task.inc], task.n,
                                              emitted[task.inc])
                    else:
                        ret = task.fn(eng_obj)
                        if task.inc is not None:
                            assert ret is not None, f"{eng_name} needs instr ({task.inc})"
                            ret.then_inc(self.sems[task.inc], task.n)
            return run

        self.sem("s_self_vector")
        self.sem("s_self_scalar")
        self.sem("s_self_gpsimd")
        blk = self._es.enter_context(nc.Block())
        blk.sync(runner("sync"))
        blk.tensor(runner("tensor"))
        blk.vector(runner("vector"))
        blk.scalar(runner("scalar"))
        blk.gpsimd(runner("gpsimd"))
        self._es.close()
        return nc


def build(debug=False):
    b = Builder(debug=debug)
    b.schedule()
    return b.finalize()


_cached = {}


def kernel(L0_attn_x, L0_attn_W_w, L0_attn_W_b, L0_attn_out_w, L0_attn_out_b,
           L0_ffn_w_gate, L0_ffn_w_up, L0_ffn_w_down):
    x = np.asarray(L0_attn_x, dtype=np.float32)
    wqkvT = np.ascontiguousarray(np.asarray(L0_attn_W_w, np.float32).T)
    woT = np.ascontiguousarray(np.asarray(L0_attn_out_w, np.float32).T)
    bqkv = np.ascontiguousarray(np.asarray(L0_attn_W_b, np.float32))
    bo = np.ascontiguousarray(np.asarray(L0_attn_out_b, np.float32))
    wg = np.ascontiguousarray(np.asarray(L0_ffn_w_gate, np.float32))
    wu = np.ascontiguousarray(np.asarray(L0_ffn_w_up, np.float32))
    wd = np.ascontiguousarray(np.asarray(L0_ffn_w_down, np.float32))
    ones_col = np.ones((128, 1), ml_dtypes.bfloat16)
    ones_row = np.ones((1, 128), ml_dtypes.bfloat16)

    if "nc" not in _cached:
        _cached["nc"] = build(debug=False)
    nc = _cached["nc"]
    in_maps = []
    for b_ in range(8):
        in_maps.append({
            "x": np.ascontiguousarray(x[b_]),
            "wqkvT": wqkvT, "woT": woT, "bqkv": bqkv, "bo": bo,
            "wg": wg, "wu": wu, "wd": wd,
            "ones_col": ones_col, "ones_row": ones_row,
        })
    res = run_bass_kernel_spmd(nc, in_maps, core_ids=list(range(8)))
    return np.stack([r["out"] for r in res.results], axis=0)


# version 16
# speedup vs baseline: 1.4726x; 1.0378x over previous
import contextlib
import numpy as np
import ml_dtypes
import concourse.bass as bass
import concourse.mybir as mybir
from concourse.bass_utils import run_bass_kernel_spmd

F32 = mybir.dt.float32
BF16 = mybir.dt.bfloat16
I32 = mybir.dt.int32
U8 = mybir.dt.uint8
ALU = mybir.AluOpType
AF = mybir.ActivationFunctionType
AX = mybir.AxisListType

T, D, H, DH, II = 4096, 1024, 16, 64, 2816
ICN, ICW = 8, 512
JCN = T // 128
TCN, TCW = 8, 512
NPAIR = H // 2
JG = 2


def fbits(x):
    return int(np.float32(x).view(np.int32))


B_0875, B_025, B_075 = fbits(0.875), fbits(0.25), fbits(0.75)
MAGIC_ADD = 0x00A00000
MASK_TOP = -4194304


class Task:
    __slots__ = ("eng", "fn", "waits", "inc", "n")

    def __init__(self, eng, fn, waits, inc, n):
        self.eng, self.fn, self.waits, self.inc, self.n = eng, fn, waits, inc, n


class Builder:
    def __init__(self, debug=False):
        self.debug = debug
        self.nc = bass.Bass("TRN2", target_bir_lowering=False, debug=False)
        self.tasks = []
        self.tot = {}
        self.sems = {}
        self.marks = {}
        self._q_tmp_user = None
        self._es = contextlib.ExitStack()
        self._declare()

    def sem(self, name):
        if name not in self.sems:
            self.sems[name] = self._es.enter_context(self.nc.semaphore(name))
            self.tot[name] = 0
        return self.sems[name]

    def at(self, semname):
        self.sem(semname)
        return (semname, self.tot[semname])

    def t(self, eng, fn, waits=(), inc=None, n=1):
        ws = [self.marks[w] if isinstance(w, str) else w for w in waits]
        for s, v in ws:
            assert v <= self.tot.get(s, 0), f"forward wait {s}:{v}>{self.tot.get(s, 0)}"
        if inc is not None:
            self.sem(inc)
            self.tot[inc] += n
        self.tasks.append(Task(eng, fn, ws, inc, n))

    def mark(self, name, semname):
        self.sem(semname)
        self.marks[name] = (semname, self.tot[semname])

    def _declare(self):
        nc = self.nc
        ein, eout = "ExternalInput", "ExternalOutput"
        scr = eout if self.debug else "Internal"
        d = lambda name, shape, dt, kind: nc.dram_tensor(name, shape, dt, kind=kind).ap()
        self.x = d("x", [T, D], F32, ein)
        self.wqkvT = d("wqkvT", [D, 3 * D], F32, ein)
        self.woT = d("woT", [D, D], F32, ein)
        self.bqkv = d("bqkv", [3 * D], F32, ein)
        self.bo = d("bo", [D], F32, ein)
        self.wg = d("wg", [II, D], F32, ein)
        self.wu = d("wu", [II, D], F32, ein)
        self.wd = d("wd", [D, II], F32, ein)
        self.ones_col_d = d("ones_col", [128, 1], BF16, ein)
        self.ones_row_d = d("ones_row", [1, 128], BF16, ein)
        self.out = d("out", [T, D], F32, eout)
        self.xln_s = d("xln_s", [T, D], BF16, scr)
        self.qT_s = d("qT_s", [D, T], BF16, scr)
        self.x1_s = d("x1_s", [T, D], F32, scr)
        self.xq_s = d("xq_s", [T, D], BF16, scr)
        self.hq_s = d("hq_s", [T, II], BF16, scr)
        self.qwg_s = d("qwg_s", [II, D], BF16, scr)
        self.qwu_s = d("qwu_s", [II, D], BF16, scr)
        self.qwd_s = d("qwd_s", [D, II], BF16, scr)

        AREN = 157696
        self.arena = nc.alloc_sbuf_tensor("arena", [128, AREN], U8)
        self.off = {}
        for name, off, nb in [
            ("KT", 0, 65536),
            ("V", 65536, 66560),
            ("WSL", 132096, 16384),
            ("OT", 148480, 8192),
            ("QWG", 0, 45056), ("QWU", 45056, 45056), ("QWD", 90112, 45056),
            ("HQT", 135168, 22528),
        ]:
            assert off + nb <= AREN
            self.off[name] = ("A", off, nb)

        SCRN = 49152
        self.scrb = nc.alloc_sbuf_tensor("scrb", [128, SCRN], U8)
        o = 0

        def sseg(name, nb):
            nonlocal o
            o = (o + 63) // 64 * 64
            assert o + nb <= SCRN, (name, o, nb)
            self.off[name] = ("S", o, nb)
            o += nb

        for n_, nb in [("WF", 2048), ("WQX", 2048), ("WQA", 2048), ("WQG", 2048),
                       ("WQT", 2048), ("WQC", 2048), ("WQM", 2048), ("WQO", 1024),
                       ("WQSC", 512), ("BNS", 128), ("BNV", 128),
                       ("XR", 8192), ("X2", 4096), ("XLN", 2048), ("ST", 192),
                       ("UN", 18432)]:
            sseg(n_, nb)
        u = self.off["UN"][1]
        for name, du, nb in [
            ("XLT", 0, 16384),
            ("ST2", 16384, 192),
            ("QS", 16384, 2048), ("PA", 0, 6144), ("PB", 6144, 6144),
            ("BOROW", 0, 4096),
            ("OE", 12288, 2048), ("T1", 12288, 2048), ("RSBF", 14336, 2048),
            ("XQT", 0, 8192), ("SGT", 8192, 2048), ("HT", 10240, 2048),
            ("HQ", 12288, 2048), ("OUTT", 14336, 4096),
        ]:
            assert du + nb <= 18432
            self.off[name] = ("S", u + du, nb)
        self.off["BVROW"] = ("S", self.off["XR"][1] + 4096, 4096)
        self.off["X1T"] = self.off["X2"]

        self.psA = self._es.enter_context(nc.psum_tensor("psA", [128, 1024], F32))
        self.psB = self._es.enter_context(nc.psum_tensor("psB", [128, 1024], F32))
        self.psC = self._es.enter_context(nc.psum_tensor("psC", [128, 1024], F32))
        self.psOA = self._es.enter_context(nc.psum_tensor("psOA", [128, 512], F32))
        self.psOB = self._es.enter_context(nc.psum_tensor("psOB", [128, 512], F32))

        sb = nc.alloc_sbuf_tensor
        self.ones_col = sb("ones_col_t", [128, 1], BF16).ap()
        self.ones_row = sb("ones_row_t", [1, 128], BF16).ap()
        self.ones_row_f = sb("ones_row_f", [1, 128], F32).ap()
        self.bK_col = sb("bK_col", [128, 8], F32).ap()
        self.bQ_col = sb("bQ_col", [128, 8], F32).ap()
        self.bias_tile = sb("bias_tile", [128, 1024], F32).ap()
        self.cneg = sb("cneg", [128, 16], F32).ap()
        self.mq2 = sb("mq2", [1, 16], F32).ap()
        self.mk2 = sb("mk2", [1, 16], F32).ap()
        self.sm1 = sb("sm1", [1, 64], F32).ap()
        self.mxc = sb("mxc", [1, 256], BF16).ap()
        rb_kind, rb_off, rb_nb = self.off["RSBF"]
        self.rs_bf = self.scrb.ap()[0:1, rb_off:rb_off + rb_nb].bitcast(BF16)
        self.cneg_bf = sb("cneg_bf", [1, 16], BF16).ap()

    def _view(self, name, esz, cols, sub):
        kind, off, nb = self.off[name]
        base = self.arena if kind == "A" else self.scrb
        assert sub + cols * esz <= nb, (name, sub, cols)
        return base.ap()[:, off + sub:off + sub + cols * esz]

    def bf(self, name, cols, sub=0):
        return self._view(name, 2, cols, sub).bitcast(BF16)

    def f32v(self, name, cols, sub=0):
        return self._view(name, 4, cols, sub).bitcast(F32)

    def i32v(self, name, cols, sub=0):
        return self._view(name, 4, cols, sub).bitcast(I32)

    def u8v(self, name, cols, sub=0):
        return self._view(name, 1, cols, sub)

    def emit_quant2(self, src_f32, dst_bf16, F, waits_first=(), waits_last=(),
                    inc_last=None, pool=True):
        nb = F // 16
        bm = self.f32v("WQSC", nb, sub=0)
        s_ = self.f32v("WQSC", nb, sub=128)
        rs = self.f32v("WQSC", nb, sub=256)
        sh = self.f32v("WQSC", nb, sub=384)
        xn, ai = self.f32v("WQX", F), self.i32v("WQA", F)
        sg = self.f32v("WQG", F)
        tt, cc = self.i32v("WQT", F), self.f32v("WQC", F)
        mm = self.f32v("WQM", F)
        sv = src_f32.rearrange("p (n b) -> p n b", b=16)
        rsb = rs.unsqueeze(2).broadcast_to([128, nb, 16])
        shb = sh.unsqueeze(2).broadcast_to([128, nb, 16])
        dv = dst_bf16.rearrange("p (n b) -> p n b", b=16)
        xnv = xn.rearrange("p (n b) -> p n b", b=16)
        mmv = mm.rearrange("p (n b) -> p n b", b=16)
        fin = inc_last if inc_last is not None else ("s_qp" if pool else "s_qv")
        tw = list(waits_first)
        if self._q_tmp_user is not None:
            tw.append(self._q_tmp_user)

        if pool:
            def A(v):
                v.tensor_reduce(bm, sv, AX.X, ALU.max, apply_absolute_value=True)
                v.tensor_scalar(s_, bm, 1.0 / 6.0, 1e-12, ALU.mult, ALU.max)
                v.reciprocal(rs, s_)
                return v.tensor_scalar(sh, s_, 0.5, None, ALU.mult)
            self.t("vector", A, waits=tw, inc="s_qv")

            def B(g):
                return g.tensor_tensor(xnv, sv, rsb, ALU.mult)
            self.t("pool", B, waits=[self.at("s_qv")], inc="s_qp")
            srcfree = self.at("s_qp")

            def C(v):
                v.tensor_scalar(ai, xn.bitcast(I32), 0x7FFFFFFF, None, ALU.bitwise_and)
                v.tensor_scalar(sg.bitcast(I32), xn.bitcast(I32), -2147483648,
                                1065353216, ALU.bitwise_and, ALU.bitwise_or)
                v.tensor_scalar(tt, ai, B_0875, MAGIC_ADD, ALU.max, ALU.add)
                return v.tensor_scalar(tt, tt, MASK_TOP, None, ALU.bitwise_and)
            self.t("vector", C, waits=[self.at("s_qp")], inc="s_qv")

            def D(g):
                g.tensor_scalar(cc, ai.bitcast(F32), 0.25, None, ALU.is_gt)
                g.tensor_scalar(mm, ai.bitcast(F32), 0.75, None, ALU.is_gt)
                g.tensor_scalar(xn, tt.bitcast(F32), -1.0, None, ALU.add)
                g.tensor_tensor(mm, mm, xn, ALU.mult)
                g.tensor_add(mm, mm, cc)
                g.tensor_tensor(mm, mm, sg, ALU.mult)
                return g.tensor_tensor(dv, mmv, shb, ALU.mult)
            self.t("pool", D, waits=[self.at("s_qv")] + list(waits_last), inc=fin)
            self._q_tmp_user = self.at(fin)
            return srcfree, self._q_tmp_user

        u8m = self.u8v("WQM", F)

        def E(v):
            v.tensor_reduce(bm, sv, AX.X, ALU.max, apply_absolute_value=True)
            v.tensor_scalar(s_, bm, 1.0 / 6.0, 1e-12, ALU.mult, ALU.max)
            v.reciprocal(rs, s_)
            v.tensor_scalar(sh, s_, 0.5, None, ALU.mult)
            v.tensor_tensor(xnv, sv, rsb, ALU.mult)
            v.tensor_scalar(ai, xn.bitcast(I32), 0x7FFFFFFF, None, ALU.bitwise_and)
            v.tensor_scalar(sg.bitcast(I32), xn.bitcast(I32), -2147483648,
                            1065353216, ALU.bitwise_and, ALU.bitwise_or)
            v.tensor_scalar(tt, ai, B_0875, MAGIC_ADD, ALU.max, ALU.add)
            v.tensor_scalar(tt, tt, MASK_TOP, None, ALU.bitwise_and)
            v.tensor_scalar(cc, ai, B_025, None, ALU.is_gt)
            v.tensor_scalar(u8m, ai, B_075, None, ALU.is_gt)
            v.select(xn, u8m, tt.bitcast(F32), cc)
            v.tensor_tensor(xn, xn, sg, ALU.mult)
            return v.tensor_tensor(dv, xnv, shb, ALU.mult)
        self.t("vector", E, waits=tw + list(waits_last), inc=fin)
        self._q_tmp_user = self.at(fin)
        return self._q_tmp_user, self._q_tmp_user

    def schedule(self):
        self.prologue()
        self.stage1a()
        self.stage1b()
        self.stage2()
        self.stage3()

    def prologue(self):
        def ld_consts(e):
            e.dma_start(out=self.ones_col, in_=self.ones_col_d)
            e.dma_start(out=self.ones_row, in_=self.ones_row_d)
            for cc in range(8):
                e.dma_start(out=self.bQ_col[:, cc:cc + 1],
                            in_=self.bqkv[cc * 128:(cc + 1) * 128].unsqueeze(1))
                e.dma_start(out=self.bK_col[:, cc:cc + 1],
                            in_=self.bqkv[1024 + cc * 128:1024 + (cc + 1) * 128].unsqueeze(1))
            return e.dma_start(out=self.f32v("BVROW", 1024)[0:1, :],
                               in_=self.bqkv[2048:3072].unsqueeze(0))
        self.t("sync", ld_consts, inc="s_ld", n=19 * 16)

        def consts_prep(v):
            vones = self.bf("V", 16 * 65 * 32).rearrange("p (jh e) -> p jh e", e=65)
            v.memset(vones[:, :, 64:65], 1.0)
            v.memset(self.f32v("ST", 1, sub=128), 1e-5)
            return v.tensor_copy(self.ones_row_f, self.ones_row)
        self.t("vector", consts_prep, waits=[self.at("s_ld")], inc="s_dve")

        def bv_mm(e):
            bv = self.f32v("BVROW", 1024)[0:1, :]
            e.matmul(self.psA.ap()[:, 0:512], self.ones_row_f, bv[:, 0:512],
                     start=True, stop=True)
            return e.matmul(self.psA.ap()[:, 512:1024], self.ones_row_f,
                            bv[:, 512:1024], start=True, stop=True)
        self.t("tensor", bv_mm, waits=[self.at("s_ld"), self.at("s_dve")], inc="s_pe")
        self.mark("bv_mm_done", "s_pe")
        self.t("vector", lambda v: v.tensor_copy(self.bias_tile, self.psA.ap()[:]),
               waits=[self.at("s_pe")], inc="s_dve")
        self.mark("bv_ready", "s_dve")

        def ld_wk(e):
            last = None
            for kc in range(8):
                last = e.dma_start(out=self.bf("WSL", 1024, sub=kc * 2048),
                                   in_=self.wqkvT[kc * 128:(kc + 1) * 128, 1024:2048])
            return last
        self.t("gpsimd", ld_wk, inc="s_wld", n=8 * 16)
        self.mark("wk_ready", "s_wld")

        jobs = []
        for (w_in, w_out, rows, cols) in [(self.wg, self.qwg_s, II, D),
                                          (self.wu, self.qwu_s, II, D),
                                          (self.wd, self.qwd_s, D, II)]:
            for r0 in range(0, rows, 128):
                for c0 in range(0, cols, 512):
                    jobs.append((w_in, w_out, r0, c0, min(512, cols - c0)))
        self.wq_jobs, self.wq_idx = jobs, 0
        self.wq_wf_free = [None, None]
        self.wq_out_free = [None, None]

    def emit_wquant(self, k=1, pool=False):
        for _ in range(k):
            if self.wq_idx >= len(self.wq_jobs):
                return
            w_in, w_out, r0, c0, cw = self.wq_jobs[self.wq_idx]
            slot = self.wq_idx % 2
            self.wq_idx += 1
            wf = self.f32v("WF", cw)
            wo = self.bf("WQO", cw)
            ldsem = "s_wl2"
            waits = []
            if self.wq_wf_free[0] is not None:
                waits.append(self.wq_wf_free[0])
            self.t("gpsimd",
                   lambda e, w_in=w_in, r0=r0, c0=c0, cw=cw, wf=wf:
                   e.dma_start(out=wf, in_=w_in[r0:r0 + 128, c0:c0 + cw]),
                   waits=waits, inc=ldsem, n=16)
            wl = []
            if self.wq_out_free[0] is not None:
                wl.append(self.wq_out_free[0])
            srcfree, qdone = self.emit_quant2(wf, wo, cw,
                                              waits_first=[self.at(ldsem)],
                                              waits_last=wl, pool=pool)
            self.wq_wf_free[0] = srcfree
            spsem = f"s_wsp{slot}"
            self.t("gpsimd",
                   lambda e, w_out=w_out, r0=r0, c0=c0, cw=cw, wo=wo:
                   e.dma_start(out=w_out[r0:r0 + 128, c0:c0 + cw], in_=wo),
                   waits=[qdone], inc=spsem, n=16)
            self.wq_out_free[0] = self.at(spsem)

    def stage1a(self):
        def ln_views(tt):
            slot = tt % 3
            xr = (self.f32v("XR", 1024, sub=slot * 4096) if slot < 2
                  else self.f32v("X2", 1024))
            if slot < 2:
                bns = self.f32v("BNS", 12, sub=slot * 64)
                bnv = self.f32v("BNV", 16, sub=slot * 64)
            else:
                bns = self.f32v("ST", 12, sub=0)
                bnv = self.f32v("ST", 16, sub=48)
            return dict(
                xr=xr,
                bns0=bns[:, 0:6], bns1=bns[:, 6:12], bns=bns,
                mean=bnv[:, 0:1], var=bnv[:, 1:2], mv=bnv[:, 0:2],
                rstd=bnv[:, 2:3], rstd2=bnv[:, 3:4],
                xln=self.bf("XLN", 1024))

        def emit_ln_stats(tt):
            vw = ln_views(tt)
            waits = []
            if tt >= 3:
                waits.append(f"norm1_{tt - 3}")
            if tt == 1:
                waits.append("bv_mm_done")
            self.t("gpsimd",
                   lambda e, tt=tt, xr=vw["xr"]: e.dma_start(
                       out=xr, in_=self.x[tt * 128:(tt + 1) * 128, :]),
                   waits=waits, inc=f"s_xld{tt % 3}", n=16)

            def ln(v, vw=vw):
                v.bn_stats(vw["bns0"], vw["xr"][:, 0:512])
                v.bn_stats(vw["bns1"], vw["xr"][:, 512:1024])
                return v.bn_aggr(vw["mv"], vw["bns"])
            self.t("vector", ln, waits=[self.at(f"s_xld{tt % 3}")], inc="s_dve")
            self.t("scalar",
                   lambda a, vw=vw: a.activation(vw["rstd"], vw["var"], AF.Sqrt,
                                                 bias=self.f32v("ST", 1, sub=128)),
                   waits=[self.at("s_dve")], inc="s_act")
            self.mark(f"sqrt1_{tt}", "s_act")

        def emit_ln_norm(tt):
            vw = ln_views(tt)

            def norm(v, vw=vw):
                v.reciprocal(vw["rstd2"], vw["rstd"])
                return v.tensor_scalar(vw["xln"], vw["xr"], vw["mean"], vw["rstd2"],
                                       ALU.subtract, ALU.mult)
            nwaits = [f"sqrt1_{tt}"]
            if tt >= 1:
                nwaits.append(self.at(f"s_xsp{(tt - 1) % 2}"))
            self.t("vector", norm, waits=nwaits, inc="s_dve")
            self.mark(f"norm1_{tt}", "s_dve")
            self.t("sync",
                   lambda e, tt=tt, xln=vw["xln"]: e.dma_start(
                       out=self.xln_s[tt * 128:(tt + 1) * 128, :], in_=xln),
                   waits=[self.at("s_dve")], inc=f"s_xsp{tt % 2}", n=16)
            self.mark(f"xspm_{tt}", f"s_xsp{tt % 2}")

        self.xlt_user = [None, None]

        def k_phase(ic):
            slot = ic % 2

            def tld(e, ic=ic, slot=slot):
                last = None
                for kc in range(8):
                    last = e.dma_start(
                        out=self.bf("XLT", 512, sub=slot * 8192 + kc * 1024),
                        in_=self.xln_s[ic * 512:(ic + 1) * 512, kc * 128:(kc + 1) * 128],
                        transpose=True)
                return last
            waits = [f"xspm_{4 * ic + 3}", f"xspm_{4 * ic + 2}"]
            if self.xlt_user[slot] is not None:
                waits.append(self.xlt_user[slot])
            self.t("sync", tld, waits=waits, inc=f"s_tp{slot}", n=8 * 16)
            tmark = self.at(f"s_tp{slot}")

            for cc in range(8):
                bank = cc % 2
                gidx = ic * 8 + cc
                ps = self.psA.ap()[:, bank * 512:(bank + 1) * 512]
                waits = [tmark, "wk_ready", "bv_ready"]
                if gidx >= 2:
                    waits.append(f"kev_{gidx - 2}")

                def kmm(e, cc=cc, ps=ps, slot=slot):
                    last = None
                    for kc in range(8):
                        last = e.matmul(
                            ps, self.bf("WSL", 128, sub=kc * 2048 + cc * 256),
                            self.bf("XLT", 512, sub=slot * 8192 + kc * 1024),
                            start=(kc == 0), stop=(kc == 7))
                    return last
                self.t("tensor", kmm, waits=waits, inc="s_pe")
                self.t("vector",
                       lambda v, cc=cc, ic=ic, ps=ps: v.tensor_scalar(
                           self.bf("KT", 512, sub=cc * 8192 + ic * 1024), ps,
                           self.bK_col[:, cc:cc + 1], None, ALU.add),
                       waits=[self.at("s_pe")], inc="s_dve")
                self.mark(f"kev_{gidx}", "s_dve")
            self.mark(f"kvmm_K_{ic}", "s_pe")
            self.xlt_user[slot] = f"kvmm_K_{ic}"
            self.emit_wquant(2)

        NT = T // 128
        emit_ln_stats(0)
        for tt in range(1, NT):
            emit_ln_stats(tt)
            emit_ln_norm(tt - 1)
            if tt % 4 == 0:
                k_phase(tt // 4 - 1)
        emit_ln_norm(NT - 1)
        k_phase(ICN - 1)
        self.ln1_done_waits = [self.at("s_xsp0"), self.at("s_xsp1")]

        for phase in ("V",):
            if phase == "V":
                def ld_wv(e):
                    last = None
                    for kc in range(8):
                        last = e.dma_start(out=self.bf("WSL", 1024, sub=kc * 2048),
                                           in_=self.wqkvT[kc * 128:(kc + 1) * 128, 2048:3072])
                    return last
                self.t("gpsimd", ld_wv, waits=[self.at("s_pe")], inc="s_wld", n=8 * 16)
                self.mark("wv_ready", "s_wld")
            for ic in range(ICN):
                slot = ic % 2
                def tld(e, ic=ic, slot=slot):
                    last = None
                    for kc in range(8):
                        last = e.dma_start(
                            out=self.bf("XLT", 512, sub=slot * 8192 + kc * 1024),
                            in_=self.xln_s[ic * 512:(ic + 1) * 512, kc * 128:(kc + 1) * 128],
                            transpose=True)
                    return last
                waits = list(self.ln1_done_waits)
                if self.xlt_user[slot] is not None:
                    waits.append(self.xlt_user[slot])
                self.t("sync", tld, waits=waits, inc=f"s_tp{slot}", n=8 * 16)
                tmark = self.at(f"s_tp{slot}")

                if True:
                    for ts_ in range(4):
                        for vc in range(2):
                            bank = (ts_ * 2 + vc) % 2
                            gidx = (ic * 4 + ts_) * 2 + vc
                            ps = self.psB.ap()[:, bank * 512:(bank + 1) * 512]
                            waits = [tmark, "wv_ready", "bv_ready"]
                            if gidx >= 2:
                                waits.append(f"vev_{gidx - 2}")

                            def vmm(e, ts_=ts_, vc=vc, ps=ps, slot=slot):
                                last = None
                                for kc in range(8):
                                    last = e.matmul(
                                        ps,
                                        self.bf("XLT", 128,
                                                sub=slot * 8192 + kc * 1024 + ts_ * 256),
                                        self.bf("WSL", 512, sub=kc * 2048 + vc * 1024),
                                        start=(kc == 0), stop=(kc == 7))
                                return last
                            self.t("tensor", vmm, waits=waits, inc="s_pe")
                            jc = ic * 4 + ts_

                            def vev(v, jc=jc, vc=vc, ps=ps):
                                dst = self.bf("V", 8 * 65, sub=(jc * 16 + vc * 8) * 130)
                                dst = dst.rearrange("p (h e) -> p h e", e=65)[:, :, 0:64]
                                src = ps.rearrange("p (h e) -> p h e", e=64)
                                bt = self.bias_tile[:, vc * 512:(vc + 1) * 512]
                                bt = bt.rearrange("p (h e) -> p h e", e=64)
                                return v.tensor_tensor(dst, src, bt, ALU.add)
                            self.t("vector", vev, waits=[self.at("s_pe")], inc="s_dve")
                            self.mark(f"vev_{gidx}", "s_dve")
                self.mark(f"kvmm_{phase}_{ic}", "s_pe")
                self.xlt_user[slot] = f"kvmm_{phase}_{ic}"
                self.emit_wquant(2)
        self.mark("kv_done", "s_dve")

    def stage1b(self):
        def ld_wq(e):
            last = None
            for kc in range(8):
                last = e.dma_start(out=self.bf("WSL", 1024, sub=kc * 2048),
                                   in_=self.wqkvT[kc * 128:(kc + 1) * 128, 0:1024])
            return last
        self.t("gpsimd", ld_wq, waits=["kv_done", self.at("s_pe")], inc="s_wld", n=8 * 16)
        self.mark("wq_ready", "s_wld")

        mxc = self.mxc
        sum_rows = [self.psB.ap()[0:1, 0:512], self.psB.ap()[0:1, 512:1024],
                    self.psC.ap()[0:1, 0:512], self.psC.ap()[0:1, 512:1024]]
        aidx = [0]
        sq_user = [None, None]
        qev_marks = []

        def emit_sumrow(src2, half, col, extra=()):
            i = aidx[0]
            aidx[0] += 1
            pss = sum_rows[i % 4]

            def ones_mm(e, half=half, src2=src2, pss=pss):
                return e.matmul(pss, self.ones_col[half * 64:(half + 1) * 64, :],
                                src2[half * 64:(half + 1) * 64, :],
                                start=True, stop=True)
            w = [self.at("s_dve")] + list(extra)
            if i >= 4:
                w.append(f"redm_{i - 4}")
            if i < 4:
                w.append("kv_done")
            self.t("tensor", ones_mm, waits=w, inc="s_pe")
            self.mark(f"sqmm_{i}", "s_pe")

            def redm(v, pss=pss, col=col):
                with self.nc.allow_low_precision(reason="bf16 collect of softmax bound"):
                    return v.reduce_max(mxc[:, col:col + 1], pss, AX.X)
            self.t("vector", redm, waits=[self.at("s_pe")], inc="s_dve2")
            self.mark(f"redm_{i}", "s_dve2")

        for ic in range(ICN):
            slot = ic % 2
            def tld(e, ic=ic, slot=slot):
                last = None
                for kc in range(8):
                    last = e.dma_start(
                        out=self.bf("XLT", 512, sub=slot * 8192 + kc * 1024),
                        in_=self.xln_s[ic * 512:(ic + 1) * 512, kc * 128:(kc + 1) * 128],
                        transpose=True)
                return last
            waits = []
            if self.xlt_user[slot] is not None:
                waits.append(self.xlt_user[slot])
            self.t("sync", tld, waits=waits, inc=f"s_tp{slot}", n=8 * 16)
            tmark = self.at(f"s_tp{slot}")

            for cc in range(8):
                bank = cc % 2
                qi = ic * 8 + cc
                ps = self.psA.ap()[:, bank * 512:(bank + 1) * 512]
                waits = [tmark, "wq_ready"]
                if len(qev_marks) >= 2:
                    waits.append(qev_marks[-2])

                def qmm(e, cc=cc, ps=ps, slot=slot):
                    last = None
                    for kc in range(8):
                        last = e.matmul(ps, self.bf("WSL", 128, sub=kc * 2048 + cc * 256),
                                        self.bf("XLT", 512, sub=slot * 8192 + kc * 1024),
                                        start=(kc == 0), stop=(kc == 7))
                    return last
                self.t("tensor", qmm, waits=waits, inc="s_pe")
                qtile = self.bf("XLN", 512, sub=bank * 1024)

                def qev(v, cc=cc, ps=ps, qtile=qtile):
                    return v.tensor_scalar(qtile, ps, self.bQ_col[:, cc:cc + 1], 0.125,
                                           ALU.add, ALU.mult)
                qpar = f"s_qsp{(ic * 8 + cc) % 2}"
                evw = [self.at("s_pe"), self.at(qpar)]
                self.t("vector", qev, waits=evw, inc="s_dve")
                self.mark(f"qev_{qi}", "s_dve")
                qev_marks.append(f"qev_{qi}")
                self.t("sync",
                       lambda e, cc=cc, ic=ic, qtile=qtile: e.dma_start(
                           out=self.qT_s[cc * 128:(cc + 1) * 128, ic * 512:(ic + 1) * 512],
                           in_=qtile),
                       waits=[self.at("s_dve")], inc=qpar, n=16)

                q2 = self.bf("X2", 512, sub=(qi % 2) * 1024)
                sqw = [self.at("s_dve")]
                if sq_user[qi % 2] is not None:
                    sqw.append(sq_user[qi % 2])
                self.t("vector",
                       lambda v, qtile=qtile, q2=q2: v.tensor_tensor(q2, qtile, qtile,
                                                                     ALU.mult),
                       waits=sqw, inc="s_dve")
                for half in range(2):
                    emit_sumrow(q2, half, (cc * 2 + half) * 8 + ic)
                sq_user[qi % 2] = f"sqmm_{aidx[0] - 1}"
            self.mark(f"qmm_{ic}", "s_pe")
            self.xlt_user[slot] = f"qmm_{ic}"
            self.emit_wquant(2)

        for cc in range(8):
            for js in range(ICN):
                ki = cc * 8 + js
                k2 = self.bf("X2", 512, sub=(ki % 2) * 1024)
                kw = []
                if sq_user[ki % 2] is not None:
                    kw.append(sq_user[ki % 2])

                def ksq(v, cc=cc, js=js, k2=k2):
                    sl = self.bf("KT", 512, sub=cc * 8192 + js * 1024)
                    return v.tensor_tensor(k2, sl, sl, ALU.mult)
                self.t("vector", ksq, waits=kw, inc="s_dve")
                for half in range(2):
                    emit_sumrow(k2, half, 128 + (cc * 2 + half) * 8 + js)
                sq_user[ki % 2] = f"sqmm_{aidx[0] - 1}"

        def fmax(v):
            qv = mxc[:, 0:128].rearrange("p (h i) -> p h i", i=8)
            kv = mxc[:, 128:256].rearrange("p (h i) -> p h i", i=8)
            v.tensor_reduce(self.mq2, qv, AX.X, ALU.max)
            return v.tensor_reduce(self.mk2, kv, AX.X, ALU.max)
        self.t("vector", fmax, waits=[self.at("s_dve2")], inc="s_dve")

        self.t("vector",
               lambda v: v.tensor_tensor(self.sm1[:, 16:32], self.mq2, self.mk2, ALU.mult),
               waits=[self.at("s_dve")], inc="s_dve")
        self.t("scalar",
               lambda a: a.activation(self.sm1[:, 32:48], self.sm1[:, 16:32], AF.Sqrt),
               waits=[self.at("s_dve")], inc="s_act")
        self.t("vector",
               lambda v: v.tensor_scalar(self.cneg_bf, self.sm1[:, 32:48], -1.0, None,
                                         ALU.mult),
               waits=[self.at("s_act")], inc="s_dve")

        def cbc(e):
            return e.matmul(self.psOB.ap()[:, 0:16], self.ones_row, self.cneg_bf,
                            start=True, stop=True)
        self.t("tensor", cbc, waits=[self.at("s_dve"), self.at("s_dve2")], inc="s_pe")
        self.t("vector",
               lambda v: v.tensor_copy(self.cneg, self.psOB.ap()[:, 0:16]),
               waits=[self.at("s_pe")], inc="s_dve2")
        self.mark("cneg_ready", "s_dve2")
        self.mark("s1b_evacs", "s_dve")
        self.mark("s1b_evacs2", "s_dve2")

    def stage2(self):
        def ld_wo(e):
            last = None
            for dc in range(8):
                last = e.dma_start(out=self.bf("WSL", 1024, sub=dc * 2048),
                                   in_=self.woT[dc * 128:(dc + 1) * 128, :])
            return last
        self.t("gpsimd", ld_wo, waits=["kv_done", self.at("s_pe")], inc="s_wld", n=8 * 16)
        self.mark("wo_ready", "s_wld")

        self.t("sync",
               lambda e: e.dma_start(out=self.f32v("BOROW", 1024)[0:1, :],
                                     in_=self.bo.unsqueeze(0)),
               waits=[self.at("s_pe")], inc="s_ld", n=16)

        def bo_mm(e):
            br = self.f32v("BOROW", 1024)[0:1, :]
            e.matmul(self.psC.ap()[:, 0:512], self.ones_row_f, br[:, 0:512],
                     start=True, stop=True)
            return e.matmul(self.psC.ap()[:, 512:1024], self.ones_row_f,
                            br[:, 512:1024], start=True, stop=True)
        self.t("tensor", bo_mm,
               waits=["s1b_evacs2", "s1b_evacs", self.at("s_ld")], inc="s_pe")

        def bo_ev(v):
            return v.tensor_copy(self.bias_tile, self.psC.ap()[:])
        self.t("vector", bo_ev, waits=[self.at("s_pe")], inc="s_dve")
        self.mark("bo_ready", "s_dve")

        ring = [self.psA, self.psB, self.psC]
        ring_consumer = ["s1b_evacs", "kv_done", "bo_ready"]
        ring_i = [0]

        def ring_alloc():
            idx = ring_i[0] % 3
            ring_i[0] += 1
            return idx, ring_consumer[idx]

        psO_free = ["bo_ready"]
        pslot_user = [None, None, None]
        pcount = [0]
        tail = [None]
        pending_op = [None]
        s2xr_user = ["cneg_ready", "cneg_ready"]

        def emit_tail_recips():
            td = tail[0]

            lns = self.f32v("OE", 512)[0:1, :]

            def mk_ln(src):
                def f(a, src=src):
                    return a.activation(lns, src, AF.Ln)
                return f

            def mk_ex(dst):
                def f(a, dst=dst):
                    with self.nc.allow_low_precision(
                            reason="1/s as bf16 matmul operand"):
                        return a.activation(dst, lns, AF.Exp, scale=-1.0)
                return f
            self.t("scalar", mk_ln(self.psOA.ap()[64:65, :]),
                   waits=[td["avdone"]], inc="s_act")
            self.t("scalar", mk_ex(self.rs_bf[:, 0:512]),
                   waits=[self.at("s_act")], inc="s_act")
            self.t("scalar", mk_ln(self.psOB.ap()[64:65, :]),
                   waits=[self.at("s_act")], inc="s_act")
            self.t("scalar", mk_ex(self.rs_bf[:, 512:1024]),
                   waits=[self.at("s_act")], inc="s_act")
            td["rsm"] = self.at("s_act")

        def emit_tail_rest():
            td = tail[0]
            gi, pair = td["gi"], td["pair"]
            iR, wR = ring_alloc()
            sR = ring[iR]

            def rsbc(e, sR=sR):
                e.matmul(sR.ap()[0:64, 0:512], self.ones_row[0:1, 0:64],
                         self.rs_bf[:, 0:512], start=True, stop=True,
                         tile_position=(0, 0), skip_group_check=True)
                return e.matmul(sR.ap()[64:128, 0:512], self.ones_row[0:1, 0:64],
                                self.rs_bf[:, 512:1024], start=True, stop=True,
                                tile_position=(0, 64), skip_group_check=True)
            self.t("tensor", rsbc, waits=[td["rsm"], wR], inc="s_pe")

            oe = self.f32v("OE", 512)

            def oev(v, pair=pair, oe=oe, sR=sR):
                v.tensor_copy(oe[0:64, :], self.psOA.ap()[0:64, :])
                v.tensor_copy(oe[64:128, :], self.psOB.ap()[0:64, :])
                return v.tensor_tensor(self.bf("OT", 512, sub=pair * 1024), oe,
                                       sR.ap()[:, 0:512], ALU.mult)
            ow = [self.at("s_pe")]
            if pair == 0 and td["ic"] > 0:
                ow.append(f"opdone_{td['ic'] - 1}")
            self.t("vector", oev, waits=ow, inc="s_dve")
            self.mark(f"oev_{gi}", "s_dve")
            ring_consumer[iR] = f"oev_{gi}"
            psO_free[0] = f"oev_{gi}"
            tail[0] = None

        def emit_op_chunk():
            ic_o, isub = pending_op[0]["ic"], pending_op[0]["isub"]
            pending_op[0]["isub"] += 1
            if pending_op[0]["isub"] >= 4:
                pending_op[0] = None
            trow = ic_o * 512 + isub * 128
            slot = isub % 2
            xr = self.f32v("XR", 1024, sub=slot * 4096)
            self.t("sync",
                   lambda e, trow=trow, xr=xr: e.dma_start(
                       out=xr, in_=self.x[trow:trow + 128, :]),
                   waits=[s2xr_user[slot]], inc=f"s_xld{slot}", n=16)
            iO, wO = ring_alloc()
            sO = ring[iO]

            def opmm(e, isub=isub, sO=sO):
                last = None
                for e2 in range(2):
                    for p in range(8):
                        last = e.matmul(
                            sO.ap()[:, e2 * 512:(e2 + 1) * 512],
                            self.bf("OT", 128, sub=p * 1024 + isub * 256),
                            self.bf("WSL", 512, sub=p * 2048 + e2 * 1024),
                            start=(p == 0), stop=(p == 7))
                return last
            self.t("tensor", opmm,
                   waits=[f"oev_{ic_o * NPAIR + NPAIR - 1}", "wo_ready", wO],
                   inc="s_pe")
            if isub == 3:
                self.mark(f"opdone_{ic_o}", "s_pe")
            opm = self.at("s_pe")
            x1t = self.f32v("X1T", 1024)

            def opev(v, x1t=x1t, xr=xr, sO=sO):
                t1 = self.f32v("T1", 512)
                last = None
                for e2 in range(2):
                    ps = sO.ap()[:, e2 * 512:(e2 + 1) * 512]
                    v.tensor_tensor(t1, ps, self.bias_tile[:, e2 * 512:(e2 + 1) * 512],
                                    ALU.add)
                    last = v.tensor_tensor(x1t[:, e2 * 512:(e2 + 1) * 512], t1,
                                           xr[:, e2 * 512:(e2 + 1) * 512], ALU.add)
                return last
            self.t("vector", opev,
                   waits=[opm, self.at(f"s_xld{slot}"), self.at("s_x1sp")],
                   inc="s_dve")
            self.mark(f"opev_{ic_o}_{isub}", "s_dve")
            ring_consumer[iO] = f"opev_{ic_o}_{isub}"
            s2xr_user[slot] = f"opev_{ic_o}_{isub}"
            self.t("sync",
                   lambda e, trow=trow, x1t=x1t: e.dma_start(
                       out=self.x1_s[trow:trow + 128, :], in_=x1t),
                   waits=[self.at("s_dve")], inc="s_x1sp", n=16)
            if isub == 3:
                self.mark(f"x1spm_{ic_o}", "s_x1sp")

        for ic in range(ICN):
            for pair in range(NPAIR):
                hA, hB = 2 * pair, 2 * pair + 1
                gi = ic * NPAIR + pair
                qslot = gi % 2
                qs = self.bf("QS", 512, sub=qslot * 1024)
                waits = [self.at("s_qsp0"), self.at("s_qsp1"), "cneg_ready", "bo_ready"]
                if gi >= 2:
                    waits.append(self.marks[f"qfree_{gi - 2}"])
                qldsem = f"s_tpq{gi % 2}"
                self.t("sync",
                       lambda e, pair=pair, ic=ic, qs=qs: e.dma_start(
                           out=qs, in_=self.qT_s[pair * 128:(pair + 1) * 128,
                                                 ic * 512:(ic + 1) * 512]),
                       waits=waits, inc=qldsem, n=16)
                qld = self.at(qldsem)

                ngrp = JCN // JG
                pend = []

                def emit_attnv(jg, pslot, gi=gi, hA=hA, hB=hB):
                    pa = self.bf("PA", 1024, sub=pslot * 2048)
                    pb = self.bf("PB", 1024, sub=pslot * 2048)

                    def avmm(e, jg=jg, pa=pa, pb=pb):
                        last = None
                        for u in range(JG):
                            jc = jg * JG + u
                            st, sp = (jc == 0), (jc == JCN - 1)
                            vA = self.bf("V", 65, sub=(jc * 16 + hA) * 130)
                            vB = self.bf("V", 65, sub=(jc * 16 + hB) * 130)
                            e.matmul(self.psOA.ap()[0:65, :], vA,
                                     pa[:, u * 512:(u + 1) * 512], start=st, stop=sp)
                            last = e.matmul(self.psOB.ap()[0:65, :], vB,
                                            pb[:, u * 512:(u + 1) * 512],
                                            start=st, stop=sp)
                        return last
                    awaits = [f"expA_{gi}_{jg}", f"expB_{gi}_{jg}"]
                    if jg == 0:
                        awaits.append(psO_free[0])
                    self.t("tensor", avmm, waits=awaits, inc="s_pe")
                    self.mark(f"av_{gi}_{jg}", "s_pe")

                for jg in range(ngrp):
                    pslot = pcount[0] % 3
                    pcount[0] += 1
                    iA, wA = ring_alloc()
                    iB, wB = ring_alloc()
                    sA, sB = ring[iA], ring[iB]

                    def smm(e, jg=jg, pair=pair, qs=qs, sA=sA, sB=sB):
                        last = None
                        for u in range(JG):
                            jc = jg * JG + u
                            kv = self.bf("KT", 128, sub=pair * 8192 + jc * 256)
                            last = e.matmul(sA.ap()[:, u * 512:(u + 1) * 512],
                                            kv[0:64, :], qs[0:64, :],
                                            start=True, stop=True,
                                            tile_position=(0, 0))
                            last = e.matmul(sB.ap()[:, u * 512:(u + 1) * 512],
                                            kv[64:128, :], qs[64:128, :],
                                            start=True, stop=True,
                                            tile_position=(64, 0))
                        return last
                    self.t("tensor", smm, waits=[qld, wA, wB], inc="s_pe")
                    smark = self.at("s_pe")

                    pa = self.bf("PA", 1024, sub=pslot * 2048)
                    pb = self.bf("PB", 1024, sub=pslot * 2048)
                    ewA = [smark]
                    if pslot_user[pslot] is not None:
                        ewA.append(pslot_user[pslot])
                    ewB = list(ewA)
                    pslot_user[pslot] = f"av_{gi}_{jg}"
                    self.t("scalar",
                           lambda a, pa=pa, hA=hA, sA=sA: a.activation(
                               pa, sA.ap()[:], AF.Exp, bias=self.cneg[:, hA:hA + 1]),
                           waits=ewA, inc="s_act")
                    self.mark(f"expA_{gi}_{jg}", "s_act")
                    ring_consumer[iA] = f"expA_{gi}_{jg}"
                    self.t("scalar",
                           lambda a, pb=pb, hB=hB, sB=sB: a.activation(
                               pb, sB.ap()[:], AF.Exp, bias=self.cneg[:, hB:hB + 1]),
                           waits=ewB, inc="s_act")
                    self.mark(f"expB_{gi}_{jg}", "s_act")
                    ring_consumer[iB] = f"expB_{gi}_{jg}"

                    pend.append((jg, pslot))
                    if len(pend) > 2:
                        emit_attnv(*pend.pop(0))
                    if jg == 0 and tail[0] is not None:
                        emit_tail_recips()
                    elif jg == 1 and tail[0] is not None:
                        emit_tail_rest()
                    elif jg in (2, 4, 6, 8) and pending_op[0] is not None:
                        emit_op_chunk()
                    elif jg in (3, 7, 11):
                        self.emit_wquant(1)
                while pend:
                    emit_attnv(*pend.pop(0))
                self.mark(f"qfree_{gi}", "s_pe")
                tail[0] = {"gi": gi, "pair": pair, "ic": ic,
                           "avdone": self.at("s_pe")}

            pending_op[0] = {"ic": ic, "isub": 0}

        emit_tail_recips()
        emit_tail_rest()
        while pending_op[0] is not None:
            emit_op_chunk()
        self.mark("stage2_done", "s_x1sp")
        self.mark("attn_pe_done", "s_pe")
        self.mark("stage2_dve", "s_dve")
        self.mark("stage2_act", "s_act")

    def stage3(self):
        while self.wq_idx < len(self.wq_jobs):
            self.emit_wquant(1)

        def ld_qw(e):
            last = None
            for kc in range(8):
                last = e.dma_start(out=self.bf("QWG", 2816, sub=kc * 5632),
                                   in_=self.qwg_s[:, kc * 128:(kc + 1) * 128],
                                   transpose=True)
                last = e.dma_start(out=self.bf("QWU", 2816, sub=kc * 5632),
                                   in_=self.qwu_s[:, kc * 128:(kc + 1) * 128],
                                   transpose=True)
            for icc in range(22):
                last = e.dma_start(out=self.bf("QWD", 1024, sub=icc * 2048),
                                   in_=self.qwd_s[:, icc * 128:(icc + 1) * 128],
                                   transpose=True)
            return last
        self.t("sync", ld_qw,
               waits=[self.at("s_wsp0"), self.at("s_wsp1"), "attn_pe_done",
                      "stage2_done"],
               inc="s_tpw", n=38 * 16)
        self.mark("qw_ready", "s_tpw")

        CCH = [(i * 512, min(512, II - i * 512)) for i in range((II + 511) // 512)]
        self.xr_user = ["stage2_dve", "stage2_dve"]

        def emit_ln2(tc):
            for ts_ in range(4):
                trow = tc * 512 + ts_ * 128
                t = tc * 4 + ts_
                slot = t % 2
                xr = self.f32v("XR", 1024, sub=slot * 4096)
                lw = [self.xr_user[slot], f"x1spm_{tc}"]
                self.t("sync",
                       lambda e, trow=trow, xr=xr: e.dma_start(
                           out=xr, in_=self.x1_s[trow:trow + 128, :]),
                       waits=lw, inc=f"s_xld{slot}", n=16)
                bns0 = self.f32v("BNS", 6, sub=slot * 64)
                bns1 = self.f32v("BNS", 6, sub=slot * 64 + 24)
                bns = self.f32v("BNS", 12, sub=slot * 64)
                mean = self.f32v("BNV", 1, sub=slot * 64)
                var = self.f32v("BNV", 1, sub=slot * 64 + 4)
                mv = self.f32v("BNV", 2, sub=slot * 64)
                rstd = self.f32v("BNV", 1, sub=slot * 64 + 8)
                rstd2 = self.f32v("BNV", 1, sub=slot * 64 + 12)

                def ln(v, xr=xr, bns0=bns0, bns1=bns1, bns=bns, mv=mv):
                    v.bn_stats(bns0, xr[:, 0:512])
                    v.bn_stats(bns1, xr[:, 512:1024])
                    return v.bn_aggr(mv, bns)
                self.t("vector", ln, waits=[self.at(f"s_xld{slot}")], inc="s_dve")
                self.t("scalar",
                       lambda a, var=var, rstd=rstd: a.activation(
                           rstd, var, AF.Sqrt, bias=self.f32v("ST", 1, sub=128)),
                       waits=[self.at("s_dve")], inc="s_act")
                self.mark(f"sqrt3_{t}", "s_act")
                x2 = self.f32v("X2", 1024)

                def norm(v, xr=xr, mean=mean, rstd=rstd, rstd2=rstd2, x2=x2):
                    v.reciprocal(rstd2, rstd)
                    return v.tensor_scalar(x2, xr, mean, rstd2,
                                           ALU.subtract, ALU.mult)
                nw = [f"sqrt3_{t}"]
                if t == 0:
                    nw.append("stage2_done")
                self.t("vector", norm, waits=nw, inc="s_dve")
                self.mark(f"ln3n_{t}", "s_dve")
                self.xr_user[slot] = f"ln3n_{t}"

                xq = self.bf("XLN", 1024)
                qw0 = [self.at("s_xqsp")]
                for half in range(2):
                    self.emit_quant2(x2[:, half * 512:(half + 1) * 512],
                                     xq[:, half * 512:(half + 1) * 512], 512,
                                     waits_first=qw0 if half == 0 else (),
                                     inc_last="s_dve", pool=False)
                self.t("sync",
                       lambda e, trow=trow, xq=xq: e.dma_start(
                           out=self.xq_s[trow:trow + 128, :], in_=xq),
                       waits=[self.at("s_dve")], inc="s_xqsp", n=16)

        def emit_xqtld(tc):
            def xqtld(e, tc=tc):
                last = None
                for kc in range(8):
                    last = e.dma_start(
                        out=self.bf("XQT", 512, sub=kc * 1024),
                        in_=self.xq_s[tc * 512:(tc + 1) * 512, kc * 128:(kc + 1) * 128],
                        transpose=True)
                return last
            waits = [self.at("s_xqsp")]
            if tc == 0:
                waits += ["attn_pe_done", "qw_ready"]
            else:
                waits.append(f"gu_done_{tc - 1}")
            self.t("sync", xqtld, waits=waits, inc="s_tpx", n=8 * 16)
            self.mark(f"xqm_{tc}", "s_tpx")

        def emit_hqtld(tc, part, i0, i1, extra=()):
            def hqtld(e, tc=tc, i0=i0, i1=i1):
                last = None
                for icc in range(i0, i1):
                    last = e.dma_start(
                        out=self.bf("HQT", 512, sub=icc * 1024),
                        in_=self.hq_s[tc * 512:(tc + 1) * 512, icc * 128:(icc + 1) * 128],
                        transpose=True)
                return last
            waits = [self.at("s_hqsp0"), self.at("s_hqsp1")] + list(extra)
            self.t("sync", hqtld, waits=waits, inc="s_tph", n=(i1 - i0) * 16)
            self.mark(f"hqld{part}_{tc}", "s_tph")

        DPS = [self.psA, self.psB, self.psC, self.psA]

        def emit_dchain(tc, ts_, part, i0, i1):
            pst = DPS[ts_]

            def dmm(e, ts_=ts_, i0=i0, i1=i1, pst=pst):
                last = None
                for e2 in range(2):
                    for icc in range(i0, i1):
                        last = e.matmul(
                            pst.ap()[:, e2 * 512:(e2 + 1) * 512],
                            self.bf("HQT", 128, sub=icc * 1024 + ts_ * 256),
                            self.bf("QWD", 512, sub=icc * 2048 + e2 * 1024),
                            start=(icc == 0), stop=(icc == 21))
                return last
            if part == "A":
                dw = [f"hqldA_{tc}"]
                if ts_ == 3:
                    dw.append(f"dev3_{tc}_0")
                elif tc == 0:
                    dw += ["stage2_act", "stage2_dve"]
                elif ts_ == 0:
                    dw.append(f"dev3_{tc - 1}_3")
                else:
                    dw.append(f"dev3_{tc - 1}_{ts_}")
            else:
                dw = [f"hqldB_{tc}", f"hqldC_{tc}"]
            self.t("tensor", dmm, waits=dw, inc="s_pe")
            self.mark(f"dmm{part}_{tc}_{ts_}", "s_pe")

        def emit_gu(tc):
            for ci, (c0, cw) in enumerate(CCH):
                for ts_ in range(4):
                    ch = tc * 24 + ci * 4 + ts_

                    def gmm(e, ts_=ts_, c0=c0, cw=cw):
                        last = None
                        for kc in range(8):
                            last = e.matmul(
                                self.psOA.ap()[:, 0:cw],
                                self.bf("XQT", 128, sub=kc * 1024 + ts_ * 256),
                                self.bf("QWG", cw, sub=kc * 5632 + c0 * 2),
                                start=(kc == 0), stop=(kc == 7))
                        return last
                    gw = [f"xqm_{tc}"]
                    if ch == 0:
                        gw.append("stage2_dve")
                    else:
                        gw.append(f"silu_{ch - 1}")
                    self.t("tensor", gmm, waits=gw, inc="s_pe")
                    gm = self.at("s_pe")
                    sgt = self.f32v("SGT", 512)
                    sw = [gm]
                    if ch > 0:
                        sw.append(f"hm_{ch - 1}")
                    self.t("scalar",
                           lambda a, cw=cw, sgt=sgt: a.activation(
                               sgt[:, 0:cw], self.psOA.ap()[:, 0:cw], AF.Silu),
                           waits=sw, inc="s_act")
                    self.mark(f"silu_{ch}", "s_act")

                    def umm(e, ts_=ts_, c0=c0, cw=cw):
                        last = None
                        for kc in range(8):
                            last = e.matmul(
                                self.psOB.ap()[:, 0:cw],
                                self.bf("XQT", 128, sub=kc * 1024 + ts_ * 256),
                                self.bf("QWU", cw, sub=kc * 5632 + c0 * 2),
                                start=(kc == 0), stop=(kc == 7))
                        return last
                    uw = [f"xqm_{tc}"]
                    if ch == 0:
                        uw.append("stage2_dve")
                    else:
                        uw.append(f"hm_{ch - 1}")
                    self.t("tensor", umm, waits=uw, inc="s_pe")
                    um = self.at("s_pe")
                    ht = self.f32v("HT", 512)

                    def hmul(v, cw=cw, sgt=sgt, ht=ht):
                        return v.tensor_tensor(ht[:, 0:cw], sgt[:, 0:cw],
                                               self.psOB.ap()[:, 0:cw], ALU.mult)
                    self.t("vector", hmul, waits=[um, f"silu_{ch}"], inc="s_dve")
                    self.mark(f"hm_{ch}", "s_dve")
                    hq = self.bf("HQ", 512, sub=(ch % 2) * 1024)
                    hqsem = f"s_hqsp{ch % 2}"
                    self.emit_quant2(ht[:, 0:cw], hq[:, 0:cw], cw,
                                     waits_first=[self.at(hqsem)],
                                     inc_last="s_dve", pool=False)
                    trow = tc * 512 + ts_ * 128
                    self.t("sync",
                           lambda e, trow=trow, c0=c0, cw=cw, hq=hq: e.dma_start(
                               out=self.hq_s[trow:trow + 128, c0:c0 + cw],
                               in_=hq[:, 0:cw]),
                           waits=[self.at("s_dve")], inc=hqsem, n=16)
                if ci == 2:
                    emit_hqtld(tc, "A", 0, 12,
                               [f"dmmA_{tc - 1}_3"] if tc else [])
                elif ci == 3:
                    emit_dchain(tc, 0, "A", 0, 12)
                elif ci == 4:
                    emit_hqtld(tc, "B", 12, 18,
                               [f"dmmB_{tc - 1}_3"] if tc else [])
                    emit_dchain(tc, 1, "A", 0, 12)
                    emit_dchain(tc, 2, "A", 0, 12)
            emit_hqtld(tc, "C", 18, 22,
                       [f"dmmB_{tc - 1}_3"] if tc else [])
            self.mark(f"gu_done_{tc}", "s_pe")

        def emit_dmm(tc):
            for ts_ in range(4):
                t = tc * 4 + ts_
                slot = t % 2
                trow = tc * 512 + ts_ * 128
                xr = self.f32v("XR", 1024, sub=slot * 4096)
                self.t("sync",
                       lambda e, trow=trow, xr=xr: e.dma_start(
                           out=xr, in_=self.x1_s[trow:trow + 128, :]),
                       waits=[self.xr_user[slot]], inc=f"s_xld{slot}", n=16)
                if ts_ == 3:
                    emit_dchain(tc, 3, "A", 0, 12)
                emit_dchain(tc, ts_, "B", 12, 22)
                outt = self.f32v("OUTT", 1024)

                def dev(v, outt=outt, xr=xr, pst=DPS[ts_]):
                    return v.tensor_tensor(outt, pst.ap()[:], xr, ALU.add)
                self.t("vector", dev,
                       waits=[self.at("s_pe"), self.at(f"s_xld{slot}"),
                              self.at("s_out")],
                       inc="s_dve")
                self.mark(f"dev3_{tc}_{ts_}", "s_dve")
                self.xr_user[slot] = f"dev3_{tc}_{ts_}"
                self.t("sync",
                       lambda e, trow=trow, outt=outt: e.dma_start(
                           out=self.out[trow:trow + 128, :], in_=outt),
                       waits=[self.at("s_dve")], inc="s_out", n=16)

        emit_ln2(0)
        emit_xqtld(0)
        for tc in range(TCN):
            if tc + 1 < TCN:
                emit_ln2(tc + 1)
            emit_gu(tc)
            if tc + 1 < TCN:
                emit_xqtld(tc + 1)
            emit_dmm(tc)

        self.t("sync", lambda e: None, waits=[self.at("s_out")])

    def finalize(self):
        nc = self.nc
        per_eng = {"sync": [], "tensor": [], "vector": [], "scalar": [], "gpsimd": []}
        for task in self.tasks:
            key = {"pool": "gpsimd", "scalardma": "scalar"}.get(task.eng, task.eng)
            per_eng[key].append(task)
        last_wait = {}

        class DmaProxy:
            def __init__(self, eng, sem):
                self._eng, self._sem = eng, sem
                self.count = 0

            def dma_start(self, **kw):
                self.count += 1
                return self._eng.dma_start(**kw).then_inc(self._sem, 16)

            def __getattr__(self, name):
                return getattr(self._eng, name)

        class SerialProxy:

            _EXEMPT = {"wait_ge", "then_inc", "dma_start"}

            def __init__(self, eng, selfsem):
                object.__setattr__(self, "_eng", eng)
                object.__setattr__(self, "_sem", selfsem)
                object.__setattr__(self, "_count", 0)
                object.__setattr__(self, "_prev", None)
                object.__setattr__(self, "_token", None)
                object.__setattr__(self, "_emitted", {})

            def _pre_op(self):
                if self._prev is not None:
                    object.__setattr__(self, "_count", self._count + 1)
                    self._prev.then_inc(self._sem, 1)
                    object.__setattr__(self, "_token", (self._sem, self._count))
                    object.__setattr__(self, "_prev", None)
                if self._token is not None:
                    self._eng.wait_ge(*self._token)
                    object.__setattr__(self, "_token", None)

            def wait_ge(self, sem, val):
                return self._eng.wait_ge(sem, val)

            def finish_task(self, inc_sem, inc_n, planned_total):
                ins = self._prev
                assert ins is not None
                ins.then_inc(inc_sem, inc_n)
                object.__setattr__(self, "_prev", None)
                object.__setattr__(self, "_token", (inc_sem, planned_total))

            def __getattr__(self, name):
                target = getattr(self._eng, name)
                if name.startswith("_") or not callable(target):
                    return target

                def wrapped(*a, **kw):
                    self._pre_op()
                    ins = target(*a, **kw)
                    object.__setattr__(self, "_prev", ins)
                    return ins
                return wrapped

        def runner(eng_name):
            def run(eng_obj):
                emitted = {}
                proxy = None
                if eng_name in ("vector", "gpsimd"):
                    proxy = SerialProxy(eng_obj, self.sems[f"s_self_{eng_name}"])
                for task in per_eng[eng_name]:
                    for s, vv in task.waits:
                        if vv <= 0:
                            continue
                        key = (eng_name, s)
                        if last_wait.get(key, -1) >= vv:
                            continue
                        last_wait[key] = vv
                        eng_obj.wait_ge(self.sems[s], vv)
                    if task.inc is not None and task.eng in ("sync", "gpsimd",
                                                              "scalardma"):
                        dproxy = DmaProxy(eng_obj, self.sems[task.inc])
                        task.fn(dproxy)
                        assert dproxy.count * 16 == task.n, \
                            f"{eng_name} dma count {dproxy.count}*16 != {task.n}"
                    elif proxy is not None:
                        ret = task.fn(proxy)
                        if task.inc is not None:
                            assert ret is not None, f"{eng_name} needs instr ({task.inc})"
                            emitted[task.inc] = emitted.get(task.inc, 0) + task.n
                            proxy.finish_task(self.sems[task.inc], task.n,
                                              emitted[task.inc])
                    else:
                        ret = task.fn(eng_obj)
                        if task.inc is not None:
                            assert ret is not None, f"{eng_name} needs instr ({task.inc})"
                            ret.then_inc(self.sems[task.inc], task.n)
            return run

        self.sem("s_self_vector")
        self.sem("s_self_scalar")
        self.sem("s_self_gpsimd")
        blk = self._es.enter_context(nc.Block())
        blk.sync(runner("sync"))
        blk.tensor(runner("tensor"))
        blk.vector(runner("vector"))
        blk.scalar(runner("scalar"))
        blk.gpsimd(runner("gpsimd"))
        self._es.close()
        return nc


def build(debug=False):
    b = Builder(debug=debug)
    b.schedule()
    return b.finalize()


_cached = {}


def kernel(L0_attn_x, L0_attn_W_w, L0_attn_W_b, L0_attn_out_w, L0_attn_out_b,
           L0_ffn_w_gate, L0_ffn_w_up, L0_ffn_w_down):
    x = np.asarray(L0_attn_x, dtype=np.float32)
    wqkvT = np.ascontiguousarray(np.asarray(L0_attn_W_w, np.float32).T)
    woT = np.ascontiguousarray(np.asarray(L0_attn_out_w, np.float32).T)
    bqkv = np.ascontiguousarray(np.asarray(L0_attn_W_b, np.float32))
    bo = np.ascontiguousarray(np.asarray(L0_attn_out_b, np.float32))
    wg = np.ascontiguousarray(np.asarray(L0_ffn_w_gate, np.float32))
    wu = np.ascontiguousarray(np.asarray(L0_ffn_w_up, np.float32))
    wd = np.ascontiguousarray(np.asarray(L0_ffn_w_down, np.float32))
    ones_col = np.ones((128, 1), ml_dtypes.bfloat16)
    ones_row = np.ones((1, 128), ml_dtypes.bfloat16)

    if "nc" not in _cached:
        _cached["nc"] = build(debug=False)
    nc = _cached["nc"]
    in_maps = []
    for b_ in range(8):
        in_maps.append({
            "x": np.ascontiguousarray(x[b_]),
            "wqkvT": wqkvT, "woT": woT, "bqkv": bqkv, "bo": bo,
            "wg": wg, "wu": wu, "wd": wd,
            "ones_col": ones_col, "ones_row": ones_row,
        })
    res = run_bass_kernel_spmd(nc, in_maps, core_ids=list(range(8)))
    return np.stack([r["out"] for r in res.results], axis=0)


# version 17
# speedup vs baseline: 1.4801x; 1.0051x over previous
import contextlib
import numpy as np
import ml_dtypes
import concourse.bass as bass
import concourse.mybir as mybir
from concourse.bass_utils import run_bass_kernel_spmd

F32 = mybir.dt.float32
BF16 = mybir.dt.bfloat16
I32 = mybir.dt.int32
U8 = mybir.dt.uint8
ALU = mybir.AluOpType
AF = mybir.ActivationFunctionType
AX = mybir.AxisListType

T, D, H, DH, II = 4096, 1024, 16, 64, 2816
ICN, ICW = 8, 512
JCN = T // 128
TCN, TCW = 8, 512
NPAIR = H // 2
JG = 2


def fbits(x):
    return int(np.float32(x).view(np.int32))


B_0875, B_025, B_075 = fbits(0.875), fbits(0.25), fbits(0.75)
MAGIC_ADD = 0x00A00000
MASK_TOP = -4194304


class Task:
    __slots__ = ("eng", "fn", "waits", "inc", "n")

    def __init__(self, eng, fn, waits, inc, n):
        self.eng, self.fn, self.waits, self.inc, self.n = eng, fn, waits, inc, n


class Builder:
    def __init__(self, debug=False):
        self.debug = debug
        self.nc = bass.Bass("TRN2", target_bir_lowering=False, debug=False)
        self.tasks = []
        self.tot = {}
        self.sems = {}
        self.marks = {}
        self._q_tmp_user = None
        self._es = contextlib.ExitStack()
        self._declare()

    def sem(self, name):
        if name not in self.sems:
            self.sems[name] = self._es.enter_context(self.nc.semaphore(name))
            self.tot[name] = 0
        return self.sems[name]

    def at(self, semname):
        self.sem(semname)
        return (semname, self.tot[semname])

    def t(self, eng, fn, waits=(), inc=None, n=1):
        ws = [self.marks[w] if isinstance(w, str) else w for w in waits]
        for s, v in ws:
            assert v <= self.tot.get(s, 0), f"forward wait {s}:{v}>{self.tot.get(s, 0)}"
        if inc is not None:
            self.sem(inc)
            self.tot[inc] += n
        self.tasks.append(Task(eng, fn, ws, inc, n))

    def mark(self, name, semname):
        self.sem(semname)
        self.marks[name] = (semname, self.tot[semname])

    def _declare(self):
        nc = self.nc
        ein, eout = "ExternalInput", "ExternalOutput"
        scr = eout if self.debug else "Internal"
        d = lambda name, shape, dt, kind: nc.dram_tensor(name, shape, dt, kind=kind).ap()
        self.x = d("x", [T, D], F32, ein)
        self.wqkvT = d("wqkvT", [D, 3 * D], F32, ein)
        self.woT = d("woT", [D, D], F32, ein)
        self.bqkv = d("bqkv", [3 * D], F32, ein)
        self.bo = d("bo", [D], F32, ein)
        self.wg = d("wg", [II, D], F32, ein)
        self.wu = d("wu", [II, D], F32, ein)
        self.wd = d("wd", [D, II], F32, ein)
        self.ones_col_d = d("ones_col", [128, 1], BF16, ein)
        self.ones_row_d = d("ones_row", [1, 128], BF16, ein)
        self.out = d("out", [T, D], F32, eout)
        self.xln_s = d("xln_s", [T, D], BF16, scr)
        self.qT_s = d("qT_s", [D, T], BF16, scr)
        self.x1_s = d("x1_s", [T, D], F32, scr)
        self.xq_s = d("xq_s", [T, D], BF16, scr)
        self.hq_s = d("hq_s", [T, II], BF16, scr)
        self.qwg_s = d("qwg_s", [II, D], BF16, scr)
        self.qwu_s = d("qwu_s", [II, D], BF16, scr)
        self.qwd_s = d("qwd_s", [D, II], BF16, scr)

        AREN = 157696
        self.arena = nc.alloc_sbuf_tensor("arena", [128, AREN], U8)
        self.off = {}
        for name, off, nb in [
            ("KT", 0, 65536),
            ("V", 65536, 66560),
            ("WSL", 132096, 16384),
            ("OT", 148480, 8192),
            ("QWG", 0, 45056), ("QWU", 45056, 45056), ("QWD", 90112, 45056),
            ("HQT", 135168, 22528),
        ]:
            assert off + nb <= AREN
            self.off[name] = ("A", off, nb)

        SCRN = 49152
        self.scrb = nc.alloc_sbuf_tensor("scrb", [128, SCRN], U8)
        o = 0

        def sseg(name, nb):
            nonlocal o
            o = (o + 63) // 64 * 64
            assert o + nb <= SCRN, (name, o, nb)
            self.off[name] = ("S", o, nb)
            o += nb

        for n_, nb in [("WF", 2048), ("WQX", 2048), ("WQA", 2048), ("WQG", 2048),
                       ("WQT", 2048), ("WQC", 2048), ("WQM", 2048), ("WQO", 1024),
                       ("WQSC", 512), ("BNS", 128), ("BNV", 128),
                       ("XR", 8192), ("X2", 4096), ("XLN", 2048), ("ST", 192),
                       ("UN", 18432)]:
            sseg(n_, nb)
        u = self.off["UN"][1]
        for name, du, nb in [
            ("XLT", 0, 16384),
            ("ST2", 16384, 192),
            ("QS", 16384, 2048), ("PA", 0, 6144), ("PB", 6144, 6144),
            ("BOROW", 0, 4096),
            ("OE", 12288, 2048), ("T1", 12288, 2048), ("RSBF", 14336, 2048),
            ("XQT", 0, 8192), ("SGT", 8192, 2048), ("HT", 10240, 2048),
            ("HQ", 12288, 2048), ("OUTT", 14336, 4096),
        ]:
            assert du + nb <= 18432
            self.off[name] = ("S", u + du, nb)
        self.off["BVROW"] = ("S", self.off["XR"][1] + 4096, 4096)
        self.off["X1T"] = self.off["X2"]

        self.psA = self._es.enter_context(nc.psum_tensor("psA", [128, 1024], F32))
        self.psB = self._es.enter_context(nc.psum_tensor("psB", [128, 1024], F32))
        self.psC = self._es.enter_context(nc.psum_tensor("psC", [128, 1024], F32))
        self.psOA = self._es.enter_context(nc.psum_tensor("psOA", [128, 512], F32))
        self.psOB = self._es.enter_context(nc.psum_tensor("psOB", [128, 512], F32))

        sb = nc.alloc_sbuf_tensor
        self.ones_col = sb("ones_col_t", [128, 1], BF16).ap()
        self.ones_row = sb("ones_row_t", [1, 128], BF16).ap()
        self.ones_row_f = sb("ones_row_f", [1, 128], F32).ap()
        self.bK_col = sb("bK_col", [128, 8], F32).ap()
        self.bQ_col = sb("bQ_col", [128, 8], F32).ap()
        self.bias_tile = sb("bias_tile", [128, 1024], F32).ap()
        self.cneg = sb("cneg", [128, 16], F32).ap()
        self.mq2 = sb("mq2", [1, 16], F32).ap()
        self.mk2 = sb("mk2", [1, 16], F32).ap()
        self.sm1 = sb("sm1", [1, 64], F32).ap()
        self.mxc = sb("mxc", [1, 256], BF16).ap()
        rb_kind, rb_off, rb_nb = self.off["RSBF"]
        self.rs_bf = self.scrb.ap()[0:1, rb_off:rb_off + rb_nb].bitcast(BF16)
        self.cneg_bf = sb("cneg_bf", [1, 16], BF16).ap()

    def _view(self, name, esz, cols, sub):
        kind, off, nb = self.off[name]
        base = self.arena if kind == "A" else self.scrb
        assert sub + cols * esz <= nb, (name, sub, cols)
        return base.ap()[:, off + sub:off + sub + cols * esz]

    def bf(self, name, cols, sub=0):
        return self._view(name, 2, cols, sub).bitcast(BF16)

    def f32v(self, name, cols, sub=0):
        return self._view(name, 4, cols, sub).bitcast(F32)

    def i32v(self, name, cols, sub=0):
        return self._view(name, 4, cols, sub).bitcast(I32)

    def u8v(self, name, cols, sub=0):
        return self._view(name, 1, cols, sub)

    def emit_quant2(self, src_f32, dst_bf16, F, waits_first=(), waits_last=(),
                    inc_last=None, pool=True):
        nb = F // 16
        bm = self.f32v("WQSC", nb, sub=0)
        s_ = self.f32v("WQSC", nb, sub=128)
        rs = self.f32v("WQSC", nb, sub=256)
        sh = self.f32v("WQSC", nb, sub=384)
        xn, ai = self.f32v("WQX", F), self.i32v("WQA", F)
        sg = self.f32v("WQG", F)
        tt, cc = self.i32v("WQT", F), self.f32v("WQC", F)
        mm = self.f32v("WQM", F)
        sv = src_f32.rearrange("p (n b) -> p n b", b=16)
        rsb = rs.unsqueeze(2).broadcast_to([128, nb, 16])
        shb = sh.unsqueeze(2).broadcast_to([128, nb, 16])
        dv = dst_bf16.rearrange("p (n b) -> p n b", b=16)
        xnv = xn.rearrange("p (n b) -> p n b", b=16)
        mmv = mm.rearrange("p (n b) -> p n b", b=16)
        fin = inc_last if inc_last is not None else ("s_qp" if pool else "s_qv")
        tw = list(waits_first)
        if self._q_tmp_user is not None:
            tw.append(self._q_tmp_user)

        if pool:
            def A(v):
                v.tensor_reduce(bm, sv, AX.X, ALU.max, apply_absolute_value=True)
                v.tensor_scalar(s_, bm, 1.0 / 6.0, 1e-12, ALU.mult, ALU.max)
                v.reciprocal(rs, s_)
                return v.tensor_scalar(sh, s_, 0.5, None, ALU.mult)
            self.t("vector", A, waits=tw, inc="s_qv")

            def B(g):
                return g.tensor_tensor(xnv, sv, rsb, ALU.mult)
            self.t("pool", B, waits=[self.at("s_qv")], inc="s_qp")
            srcfree = self.at("s_qp")

            def C(v):
                v.tensor_scalar(ai, xn.bitcast(I32), 0x7FFFFFFF, None, ALU.bitwise_and)
                v.tensor_scalar(sg.bitcast(I32), xn.bitcast(I32), -2147483648,
                                1065353216, ALU.bitwise_and, ALU.bitwise_or)
                v.tensor_scalar(tt, ai, B_0875, MAGIC_ADD, ALU.max, ALU.add)
                return v.tensor_scalar(tt, tt, MASK_TOP, None, ALU.bitwise_and)
            self.t("vector", C, waits=[self.at("s_qp")], inc="s_qv")

            def D(g):
                g.tensor_scalar(cc, ai.bitcast(F32), 0.25, None, ALU.is_gt)
                g.tensor_scalar(mm, ai.bitcast(F32), 0.75, None, ALU.is_gt)
                g.tensor_scalar(xn, tt.bitcast(F32), -1.0, None, ALU.add)
                g.tensor_tensor(mm, mm, xn, ALU.mult)
                g.tensor_add(mm, mm, cc)
                g.tensor_tensor(mm, mm, sg, ALU.mult)
                return g.tensor_tensor(dv, mmv, shb, ALU.mult)
            self.t("pool", D, waits=[self.at("s_qv")] + list(waits_last), inc=fin)
            self._q_tmp_user = self.at(fin)
            return srcfree, self._q_tmp_user

        u8m = self.u8v("WQM", F)

        def E(v):
            v.tensor_reduce(bm, sv, AX.X, ALU.max, apply_absolute_value=True)
            v.tensor_scalar(s_, bm, 1.0 / 6.0, 1e-12, ALU.mult, ALU.max)
            v.reciprocal(rs, s_)
            v.tensor_scalar(sh, s_, 0.5, None, ALU.mult)
            v.tensor_tensor(xnv, sv, rsb, ALU.mult)
            v.tensor_scalar(ai, xn.bitcast(I32), 0x7FFFFFFF, None, ALU.bitwise_and)
            v.tensor_scalar(sg.bitcast(I32), xn.bitcast(I32), -2147483648,
                            1065353216, ALU.bitwise_and, ALU.bitwise_or)
            v.tensor_scalar(tt, ai, B_0875, MAGIC_ADD, ALU.max, ALU.add)
            v.tensor_scalar(tt, tt, MASK_TOP, None, ALU.bitwise_and)
            v.tensor_scalar(cc, ai, B_025, None, ALU.is_gt)
            v.tensor_scalar(u8m, ai, B_075, None, ALU.is_gt)
            v.select(xn, u8m, tt.bitcast(F32), cc)
            v.tensor_tensor(xn, xn, sg, ALU.mult)
            return v.tensor_tensor(dv, xnv, shb, ALU.mult)
        self.t("vector", E, waits=tw + list(waits_last), inc=fin)
        self._q_tmp_user = self.at(fin)
        return self._q_tmp_user, self._q_tmp_user

    def schedule(self):
        self.prologue()
        self.stage1a()
        self.stage1b()
        self.stage2()
        self.stage3()

    def prologue(self):
        def ld_consts(e):
            e.dma_start(out=self.ones_col, in_=self.ones_col_d)
            e.dma_start(out=self.ones_row, in_=self.ones_row_d)
            for cc in range(8):
                e.dma_start(out=self.bQ_col[:, cc:cc + 1],
                            in_=self.bqkv[cc * 128:(cc + 1) * 128].unsqueeze(1))
                e.dma_start(out=self.bK_col[:, cc:cc + 1],
                            in_=self.bqkv[1024 + cc * 128:1024 + (cc + 1) * 128].unsqueeze(1))
            return e.dma_start(out=self.f32v("BVROW", 1024)[0:1, :],
                               in_=self.bqkv[2048:3072].unsqueeze(0))
        self.t("sync", ld_consts, inc="s_ld", n=19 * 16)

        def consts_prep(v):
            vones = self.bf("V", 16 * 65 * 32).rearrange("p (jh e) -> p jh e", e=65)
            v.memset(vones[:, :, 64:65], 1.0)
            v.memset(self.f32v("ST", 1, sub=128), 1e-5)
            return v.tensor_copy(self.ones_row_f, self.ones_row)
        self.t("vector", consts_prep, waits=[self.at("s_ld")], inc="s_dve")

        def bv_mm(e):
            bv = self.f32v("BVROW", 1024)[0:1, :]
            e.matmul(self.psA.ap()[:, 0:512], self.ones_row_f, bv[:, 0:512],
                     start=True, stop=True)
            return e.matmul(self.psA.ap()[:, 512:1024], self.ones_row_f,
                            bv[:, 512:1024], start=True, stop=True)
        self.t("tensor", bv_mm, waits=[self.at("s_ld"), self.at("s_dve")], inc="s_pe")
        self.mark("bv_mm_done", "s_pe")
        self.t("vector", lambda v: v.tensor_copy(self.bias_tile, self.psA.ap()[:]),
               waits=[self.at("s_pe")], inc="s_dve")
        self.mark("bv_ready", "s_dve")

        def ld_wk(e):
            last = None
            for kc in range(8):
                last = e.dma_start(out=self.bf("WSL", 1024, sub=kc * 2048),
                                   in_=self.wqkvT[kc * 128:(kc + 1) * 128, 1024:2048])
            return last
        self.t("gpsimd", ld_wk, inc="s_wld", n=8 * 16)
        self.mark("wk_ready", "s_wld")

        jobs = []
        for (w_in, w_out, rows, cols) in [(self.wg, self.qwg_s, II, D),
                                          (self.wu, self.qwu_s, II, D),
                                          (self.wd, self.qwd_s, D, II)]:
            for r0 in range(0, rows, 128):
                for c0 in range(0, cols, 512):
                    jobs.append((w_in, w_out, r0, c0, min(512, cols - c0)))
        self.wq_jobs, self.wq_idx = jobs, 0
        self.wq_wf_free = [None, None]
        self.wq_out_free = [None, None]

    def emit_wquant(self, k=1, pool=False):
        for _ in range(k):
            if self.wq_idx >= len(self.wq_jobs):
                return
            w_in, w_out, r0, c0, cw = self.wq_jobs[self.wq_idx]
            slot = self.wq_idx % 2
            self.wq_idx += 1
            wf = self.f32v("WF", cw)
            wo = self.bf("WQO", cw)
            ldsem = "s_wl2"
            waits = []
            if self.wq_wf_free[0] is not None:
                waits.append(self.wq_wf_free[0])
            self.t("gpsimd",
                   lambda e, w_in=w_in, r0=r0, c0=c0, cw=cw, wf=wf:
                   e.dma_start(out=wf, in_=w_in[r0:r0 + 128, c0:c0 + cw]),
                   waits=waits, inc=ldsem, n=16)
            wl = []
            if self.wq_out_free[0] is not None:
                wl.append(self.wq_out_free[0])
            srcfree, qdone = self.emit_quant2(wf, wo, cw,
                                              waits_first=[self.at(ldsem)],
                                              waits_last=wl, pool=pool)
            self.wq_wf_free[0] = srcfree
            spsem = f"s_wsp{slot}"
            self.t("gpsimd",
                   lambda e, w_out=w_out, r0=r0, c0=c0, cw=cw, wo=wo:
                   e.dma_start(out=w_out[r0:r0 + 128, c0:c0 + cw], in_=wo),
                   waits=[qdone], inc=spsem, n=16)
            self.wq_out_free[0] = self.at(spsem)

    def stage1a(self):
        def ln_views(tt):
            slot = tt % 3
            xr = (self.f32v("XR", 1024, sub=slot * 4096) if slot < 2
                  else self.f32v("X2", 1024))
            if slot < 2:
                bns = self.f32v("BNS", 12, sub=slot * 64)
                bnv = self.f32v("BNV", 16, sub=slot * 64)
            else:
                bns = self.f32v("ST", 12, sub=0)
                bnv = self.f32v("ST", 16, sub=48)
            xln = (self.bf("XLN", 1024) if tt % 2 == 0
                   else self.bf("OT", 1024))
            return dict(
                xr=xr,
                bns0=bns[:, 0:6], bns1=bns[:, 6:12], bns=bns,
                mean=bnv[:, 0:1], var=bnv[:, 1:2], mv=bnv[:, 0:2],
                rstd=bnv[:, 2:3], rstd2=bnv[:, 3:4],
                xln=xln)

        def emit_ln_stats(tt):
            vw = ln_views(tt)
            waits = []
            if tt >= 3:
                waits.append(f"norm1_{tt - 3}")
            if tt == 1:
                waits.append("bv_mm_done")
            self.t("gpsimd",
                   lambda e, tt=tt, xr=vw["xr"]: e.dma_start(
                       out=xr, in_=self.x[tt * 128:(tt + 1) * 128, :]),
                   waits=waits, inc=f"s_xld{tt % 3}", n=16)

            def ln(v, vw=vw):
                v.bn_stats(vw["bns0"], vw["xr"][:, 0:512])
                v.bn_stats(vw["bns1"], vw["xr"][:, 512:1024])
                return v.bn_aggr(vw["mv"], vw["bns"])
            self.t("vector", ln, waits=[self.at(f"s_xld{tt % 3}")], inc="s_dve")
            self.t("scalar",
                   lambda a, vw=vw: a.activation(vw["rstd"], vw["var"], AF.Sqrt,
                                                 bias=self.f32v("ST", 1, sub=128)),
                   waits=[self.at("s_dve")], inc="s_act")
            self.mark(f"sqrt1_{tt}", "s_act")

        def emit_ln_norm(tt):
            vw = ln_views(tt)

            def norm(v, vw=vw):
                v.reciprocal(vw["rstd2"], vw["rstd"])
                return v.tensor_scalar(vw["xln"], vw["xr"], vw["mean"], vw["rstd2"],
                                       ALU.subtract, ALU.mult)
            nwaits = [f"sqrt1_{tt}"]
            if tt >= 2:
                nwaits.append(self.at(f"s_xsp{tt % 2}"))
            self.t("vector", norm, waits=nwaits, inc="s_dve")
            self.mark(f"norm1_{tt}", "s_dve")
            self.t("sync",
                   lambda e, tt=tt, xln=vw["xln"]: e.dma_start(
                       out=self.xln_s[tt * 128:(tt + 1) * 128, :], in_=xln),
                   waits=[self.at("s_dve")], inc=f"s_xsp{tt % 2}", n=16)
            self.mark(f"xspm_{tt}", f"s_xsp{tt % 2}")

        self.xlt_user = [None, None]

        def k_phase(ic):
            slot = ic % 2

            def tld(e, ic=ic, slot=slot):
                last = None
                for kc in range(8):
                    last = e.dma_start(
                        out=self.bf("XLT", 512, sub=slot * 8192 + kc * 1024),
                        in_=self.xln_s[ic * 512:(ic + 1) * 512, kc * 128:(kc + 1) * 128],
                        transpose=True)
                return last
            waits = [f"xspm_{4 * ic + 3}", f"xspm_{4 * ic + 2}"]
            if self.xlt_user[slot] is not None:
                waits.append(self.xlt_user[slot])
            self.t("sync", tld, waits=waits, inc=f"s_tp{slot}", n=8 * 16)
            tmark = self.at(f"s_tp{slot}")

            for cc in range(8):
                bank = cc % 2
                gidx = ic * 8 + cc
                ps = self.psA.ap()[:, bank * 512:(bank + 1) * 512]
                waits = [tmark, "wk_ready", "bv_ready"]
                if gidx >= 2:
                    waits.append(f"kev_{gidx - 2}")

                def kmm(e, cc=cc, ps=ps, slot=slot):
                    last = None
                    for kc in range(8):
                        last = e.matmul(
                            ps, self.bf("WSL", 128, sub=kc * 2048 + cc * 256),
                            self.bf("XLT", 512, sub=slot * 8192 + kc * 1024),
                            start=(kc == 0), stop=(kc == 7))
                    return last
                self.t("tensor", kmm, waits=waits, inc="s_pe")
                self.t("vector",
                       lambda v, cc=cc, ic=ic, ps=ps: v.tensor_scalar(
                           self.bf("KT", 512, sub=cc * 8192 + ic * 1024), ps,
                           self.bK_col[:, cc:cc + 1], None, ALU.add),
                       waits=[self.at("s_pe")], inc="s_dve")
                self.mark(f"kev_{gidx}", "s_dve")
            self.mark(f"kvmm_K_{ic}", "s_pe")
            self.xlt_user[slot] = f"kvmm_K_{ic}"
            self.emit_wquant(2)

        NT = T // 128
        emit_ln_stats(0)
        for tt in range(1, NT):
            emit_ln_stats(tt)
            emit_ln_norm(tt - 1)
            if tt % 4 == 0:
                k_phase(tt // 4 - 1)
        emit_ln_norm(NT - 1)
        k_phase(ICN - 1)
        self.ln1_done_waits = [self.at("s_xsp0"), self.at("s_xsp1")]

        for phase in ("V",):
            if phase == "V":
                def ld_wv(e):
                    last = None
                    for kc in range(8):
                        last = e.dma_start(out=self.bf("WSL", 1024, sub=kc * 2048),
                                           in_=self.wqkvT[kc * 128:(kc + 1) * 128, 2048:3072])
                    return last
                self.t("gpsimd", ld_wv, waits=[self.at("s_pe")], inc="s_wld", n=8 * 16)
                self.mark("wv_ready", "s_wld")
            for ic in range(ICN):
                slot = ic % 2
                def tld(e, ic=ic, slot=slot):
                    last = None
                    for kc in range(8):
                        last = e.dma_start(
                            out=self.bf("XLT", 512, sub=slot * 8192 + kc * 1024),
                            in_=self.xln_s[ic * 512:(ic + 1) * 512, kc * 128:(kc + 1) * 128],
                            transpose=True)
                    return last
                waits = list(self.ln1_done_waits)
                if self.xlt_user[slot] is not None:
                    waits.append(self.xlt_user[slot])
                self.t("sync", tld, waits=waits, inc=f"s_tp{slot}", n=8 * 16)
                tmark = self.at(f"s_tp{slot}")

                if True:
                    for ts_ in range(4):
                        for vc in range(2):
                            bank = (ts_ * 2 + vc) % 2
                            gidx = (ic * 4 + ts_) * 2 + vc
                            ps = self.psB.ap()[:, bank * 512:(bank + 1) * 512]
                            waits = [tmark, "wv_ready", "bv_ready"]
                            if gidx >= 2:
                                waits.append(f"vev_{gidx - 2}")

                            def vmm(e, ts_=ts_, vc=vc, ps=ps, slot=slot):
                                last = None
                                for kc in range(8):
                                    last = e.matmul(
                                        ps,
                                        self.bf("XLT", 128,
                                                sub=slot * 8192 + kc * 1024 + ts_ * 256),
                                        self.bf("WSL", 512, sub=kc * 2048 + vc * 1024),
                                        start=(kc == 0), stop=(kc == 7))
                                return last
                            self.t("tensor", vmm, waits=waits, inc="s_pe")
                            jc = ic * 4 + ts_

                            def vev(v, jc=jc, vc=vc, ps=ps):
                                dst = self.bf("V", 8 * 65, sub=(jc * 16 + vc * 8) * 130)
                                dst = dst.rearrange("p (h e) -> p h e", e=65)[:, :, 0:64]
                                src = ps.rearrange("p (h e) -> p h e", e=64)
                                bt = self.bias_tile[:, vc * 512:(vc + 1) * 512]
                                bt = bt.rearrange("p (h e) -> p h e", e=64)
                                return v.tensor_tensor(dst, src, bt, ALU.add)
                            self.t("vector", vev, waits=[self.at("s_pe")], inc="s_dve")
                            self.mark(f"vev_{gidx}", "s_dve")
                self.mark(f"kvmm_{phase}_{ic}", "s_pe")
                self.xlt_user[slot] = f"kvmm_{phase}_{ic}"
                self.emit_wquant(2)
        self.mark("kv_done", "s_dve")

    def stage1b(self):
        def ld_wq(e):
            last = None
            for kc in range(8):
                last = e.dma_start(out=self.bf("WSL", 1024, sub=kc * 2048),
                                   in_=self.wqkvT[kc * 128:(kc + 1) * 128, 0:1024])
            return last
        self.t("gpsimd", ld_wq, waits=["kv_done", self.at("s_pe")], inc="s_wld", n=8 * 16)
        self.mark("wq_ready", "s_wld")

        mxc = self.mxc
        sum_rows = [self.psB.ap()[0:1, 0:512], self.psB.ap()[0:1, 512:1024],
                    self.psC.ap()[0:1, 0:512], self.psC.ap()[0:1, 512:1024]]
        aidx = [0]
        sq_user = [None, None]
        qev_marks = []

        def emit_sumrow(src2, half, col, extra=()):
            i = aidx[0]
            aidx[0] += 1
            pss = sum_rows[i % 4]

            def ones_mm(e, half=half, src2=src2, pss=pss):
                return e.matmul(pss, self.ones_col[half * 64:(half + 1) * 64, :],
                                src2[half * 64:(half + 1) * 64, :],
                                start=True, stop=True)
            w = [self.at("s_dve")] + list(extra)
            if i >= 4:
                w.append(f"redm_{i - 4}")
            if i < 4:
                w.append("kv_done")
            self.t("tensor", ones_mm, waits=w, inc="s_pe")
            self.mark(f"sqmm_{i}", "s_pe")

            def redm(v, pss=pss, col=col):
                with self.nc.allow_low_precision(reason="bf16 collect of softmax bound"):
                    return v.reduce_max(mxc[:, col:col + 1], pss, AX.X)
            self.t("vector", redm, waits=[self.at("s_pe")], inc="s_dve2")
            self.mark(f"redm_{i}", "s_dve2")

        for ic in range(ICN):
            slot = ic % 2
            def tld(e, ic=ic, slot=slot):
                last = None
                for kc in range(8):
                    last = e.dma_start(
                        out=self.bf("XLT", 512, sub=slot * 8192 + kc * 1024),
                        in_=self.xln_s[ic * 512:(ic + 1) * 512, kc * 128:(kc + 1) * 128],
                        transpose=True)
                return last
            waits = []
            if self.xlt_user[slot] is not None:
                waits.append(self.xlt_user[slot])
            self.t("sync", tld, waits=waits, inc=f"s_tp{slot}", n=8 * 16)
            tmark = self.at(f"s_tp{slot}")

            for cc in range(8):
                bank = cc % 2
                qi = ic * 8 + cc
                ps = self.psA.ap()[:, bank * 512:(bank + 1) * 512]
                waits = [tmark, "wq_ready"]
                if len(qev_marks) >= 2:
                    waits.append(qev_marks[-2])

                def qmm(e, cc=cc, ps=ps, slot=slot):
                    last = None
                    for kc in range(8):
                        last = e.matmul(ps, self.bf("WSL", 128, sub=kc * 2048 + cc * 256),
                                        self.bf("XLT", 512, sub=slot * 8192 + kc * 1024),
                                        start=(kc == 0), stop=(kc == 7))
                    return last
                self.t("tensor", qmm, waits=waits, inc="s_pe")
                qtile = self.bf("XLN", 512, sub=bank * 1024)

                def qev(v, cc=cc, ps=ps, qtile=qtile):
                    return v.tensor_scalar(qtile, ps, self.bQ_col[:, cc:cc + 1], 0.125,
                                           ALU.add, ALU.mult)
                qpar = f"s_qsp{(ic * 8 + cc) % 2}"
                evw = [self.at("s_pe"), self.at(qpar)]
                self.t("vector", qev, waits=evw, inc="s_dve")
                self.mark(f"qev_{qi}", "s_dve")
                qev_marks.append(f"qev_{qi}")
                self.t("sync",
                       lambda e, cc=cc, ic=ic, qtile=qtile: e.dma_start(
                           out=self.qT_s[cc * 128:(cc + 1) * 128, ic * 512:(ic + 1) * 512],
                           in_=qtile),
                       waits=[self.at("s_dve")], inc=qpar, n=16)

                q2 = self.bf("X2", 512, sub=(qi % 2) * 1024)
                sqw = [self.at("s_dve")]
                if sq_user[qi % 2] is not None:
                    sqw.append(sq_user[qi % 2])
                self.t("vector",
                       lambda v, qtile=qtile, q2=q2: v.tensor_tensor(q2, qtile, qtile,
                                                                     ALU.mult),
                       waits=sqw, inc="s_dve")
                for half in range(2):
                    emit_sumrow(q2, half, (cc * 2 + half) * 8 + ic)
                sq_user[qi % 2] = f"sqmm_{aidx[0] - 1}"
            self.mark(f"qmm_{ic}", "s_pe")
            self.xlt_user[slot] = f"qmm_{ic}"
            self.emit_wquant(2)

        for cc in range(8):
            for js in range(ICN):
                ki = cc * 8 + js
                k2 = self.bf("X2", 512, sub=(ki % 2) * 1024)
                kw = []
                if sq_user[ki % 2] is not None:
                    kw.append(sq_user[ki % 2])

                def ksq(v, cc=cc, js=js, k2=k2):
                    sl = self.bf("KT", 512, sub=cc * 8192 + js * 1024)
                    return v.tensor_tensor(k2, sl, sl, ALU.mult)
                self.t("vector", ksq, waits=kw, inc="s_dve")
                for half in range(2):
                    emit_sumrow(k2, half, 128 + (cc * 2 + half) * 8 + js)
                sq_user[ki % 2] = f"sqmm_{aidx[0] - 1}"

        def fmax(v):
            qv = mxc[:, 0:128].rearrange("p (h i) -> p h i", i=8)
            kv = mxc[:, 128:256].rearrange("p (h i) -> p h i", i=8)
            v.tensor_reduce(self.mq2, qv, AX.X, ALU.max)
            return v.tensor_reduce(self.mk2, kv, AX.X, ALU.max)
        self.t("vector", fmax, waits=[self.at("s_dve2")], inc="s_dve")

        self.t("vector",
               lambda v: v.tensor_tensor(self.sm1[:, 16:32], self.mq2, self.mk2, ALU.mult),
               waits=[self.at("s_dve")], inc="s_dve")
        self.t("scalar",
               lambda a: a.activation(self.sm1[:, 32:48], self.sm1[:, 16:32], AF.Sqrt),
               waits=[self.at("s_dve")], inc="s_act")
        self.t("vector",
               lambda v: v.tensor_scalar(self.cneg_bf, self.sm1[:, 32:48], -1.0, None,
                                         ALU.mult),
               waits=[self.at("s_act")], inc="s_dve")

        def cbc(e):
            return e.matmul(self.psOB.ap()[:, 0:16], self.ones_row, self.cneg_bf,
                            start=True, stop=True)
        self.t("tensor", cbc, waits=[self.at("s_dve"), self.at("s_dve2")], inc="s_pe")
        self.t("vector",
               lambda v: v.tensor_copy(self.cneg, self.psOB.ap()[:, 0:16]),
               waits=[self.at("s_pe")], inc="s_dve2")
        self.mark("cneg_ready", "s_dve2")
        self.mark("s1b_evacs", "s_dve")
        self.mark("s1b_evacs2", "s_dve2")

    def stage2(self):
        def ld_wo(e):
            last = None
            for dc in range(8):
                last = e.dma_start(out=self.bf("WSL", 1024, sub=dc * 2048),
                                   in_=self.woT[dc * 128:(dc + 1) * 128, :])
            return last
        self.t("gpsimd", ld_wo, waits=["kv_done", self.at("s_pe")], inc="s_wld", n=8 * 16)
        self.mark("wo_ready", "s_wld")

        self.t("sync",
               lambda e: e.dma_start(out=self.f32v("BOROW", 1024)[0:1, :],
                                     in_=self.bo.unsqueeze(0)),
               waits=[self.at("s_pe")], inc="s_ld", n=16)

        def bo_mm(e):
            br = self.f32v("BOROW", 1024)[0:1, :]
            e.matmul(self.psC.ap()[:, 0:512], self.ones_row_f, br[:, 0:512],
                     start=True, stop=True)
            return e.matmul(self.psC.ap()[:, 512:1024], self.ones_row_f,
                            br[:, 512:1024], start=True, stop=True)
        self.t("tensor", bo_mm,
               waits=["s1b_evacs2", "s1b_evacs", self.at("s_ld")], inc="s_pe")

        def bo_ev(v):
            return v.tensor_copy(self.bias_tile, self.psC.ap()[:])
        self.t("vector", bo_ev, waits=[self.at("s_pe")], inc="s_dve")
        self.mark("bo_ready", "s_dve")

        ring = [self.psA, self.psB, self.psC]
        ring_consumer = ["s1b_evacs", "kv_done", "bo_ready"]
        ring_i = [0]

        def ring_alloc():
            idx = ring_i[0] % 3
            ring_i[0] += 1
            return idx, ring_consumer[idx]

        psO_free = ["bo_ready"]
        pslot_user = [None, None, None]
        pcount = [0]
        tail = [None]
        pending_op = [None]
        s2xr_user = ["cneg_ready", "cneg_ready"]

        def emit_tail_recips():
            td = tail[0]

            lns = self.f32v("OE", 512)[0:1, :]

            def mk_ln(src):
                def f(a, src=src):
                    return a.activation(lns, src, AF.Ln)
                return f

            def mk_ex(dst):
                def f(a, dst=dst):
                    with self.nc.allow_low_precision(
                            reason="1/s as bf16 matmul operand"):
                        return a.activation(dst, lns, AF.Exp, scale=-1.0)
                return f
            self.t("scalar", mk_ln(self.psOA.ap()[64:65, :]),
                   waits=[td["avdone"]], inc="s_act")
            self.t("scalar", mk_ex(self.rs_bf[:, 0:512]),
                   waits=[self.at("s_act")], inc="s_act")
            self.t("scalar", mk_ln(self.psOB.ap()[64:65, :]),
                   waits=[self.at("s_act")], inc="s_act")
            self.t("scalar", mk_ex(self.rs_bf[:, 512:1024]),
                   waits=[self.at("s_act")], inc="s_act")
            td["rsm"] = self.at("s_act")

        def emit_tail_rest():
            td = tail[0]
            gi, pair = td["gi"], td["pair"]
            iR, wR = ring_alloc()
            sR = ring[iR]

            def rsbc(e, sR=sR):
                e.matmul(sR.ap()[0:64, 0:512], self.ones_row[0:1, 0:64],
                         self.rs_bf[:, 0:512], start=True, stop=True,
                         tile_position=(0, 0), skip_group_check=True)
                return e.matmul(sR.ap()[64:128, 0:512], self.ones_row[0:1, 0:64],
                                self.rs_bf[:, 512:1024], start=True, stop=True,
                                tile_position=(0, 64), skip_group_check=True)
            self.t("tensor", rsbc, waits=[td["rsm"], wR], inc="s_pe")

            oe = self.f32v("OE", 512)

            def oev(v, pair=pair, oe=oe, sR=sR):
                v.tensor_copy(oe[0:64, :], self.psOA.ap()[0:64, :])
                v.tensor_copy(oe[64:128, :], self.psOB.ap()[0:64, :])
                return v.tensor_tensor(self.bf("OT", 512, sub=pair * 1024), oe,
                                       sR.ap()[:, 0:512], ALU.mult)
            ow = [self.at("s_pe")]
            if pair == 0 and td["ic"] > 0:
                ow.append(f"opdone_{td['ic'] - 1}")
            self.t("vector", oev, waits=ow, inc="s_dve")
            self.mark(f"oev_{gi}", "s_dve")
            ring_consumer[iR] = f"oev_{gi}"
            psO_free[0] = f"oev_{gi}"
            tail[0] = None

        def emit_op_chunk():
            ic_o, isub = pending_op[0]["ic"], pending_op[0]["isub"]
            pending_op[0]["isub"] += 1
            if pending_op[0]["isub"] >= 4:
                pending_op[0] = None
            trow = ic_o * 512 + isub * 128
            slot = isub % 2
            xr = self.f32v("XR", 1024, sub=slot * 4096)
            self.t("sync",
                   lambda e, trow=trow, xr=xr: e.dma_start(
                       out=xr, in_=self.x[trow:trow + 128, :]),
                   waits=[s2xr_user[slot]], inc=f"s_xld{slot}", n=16)
            iO, wO = ring_alloc()
            sO = ring[iO]

            def opmm(e, isub=isub, sO=sO):
                last = None
                for e2 in range(2):
                    for p in range(8):
                        last = e.matmul(
                            sO.ap()[:, e2 * 512:(e2 + 1) * 512],
                            self.bf("OT", 128, sub=p * 1024 + isub * 256),
                            self.bf("WSL", 512, sub=p * 2048 + e2 * 1024),
                            start=(p == 0), stop=(p == 7))
                return last
            self.t("tensor", opmm,
                   waits=[f"oev_{ic_o * NPAIR + NPAIR - 1}", "wo_ready", wO],
                   inc="s_pe")
            if isub == 3:
                self.mark(f"opdone_{ic_o}", "s_pe")
            opm = self.at("s_pe")
            x1t = self.f32v("X1T", 1024)

            def opev(v, x1t=x1t, xr=xr, sO=sO):
                t1 = self.f32v("T1", 512)
                last = None
                for e2 in range(2):
                    ps = sO.ap()[:, e2 * 512:(e2 + 1) * 512]
                    v.tensor_tensor(t1, ps, self.bias_tile[:, e2 * 512:(e2 + 1) * 512],
                                    ALU.add)
                    last = v.tensor_tensor(x1t[:, e2 * 512:(e2 + 1) * 512], t1,
                                           xr[:, e2 * 512:(e2 + 1) * 512], ALU.add)
                return last
            self.t("vector", opev,
                   waits=[opm, self.at(f"s_xld{slot}"), self.at("s_x1sp")],
                   inc="s_dve")
            self.mark(f"opev_{ic_o}_{isub}", "s_dve")
            ring_consumer[iO] = f"opev_{ic_o}_{isub}"
            s2xr_user[slot] = f"opev_{ic_o}_{isub}"
            self.t("sync",
                   lambda e, trow=trow, x1t=x1t: e.dma_start(
                       out=self.x1_s[trow:trow + 128, :], in_=x1t),
                   waits=[self.at("s_dve")], inc="s_x1sp", n=16)
            if isub == 3:
                self.mark(f"x1spm_{ic_o}", "s_x1sp")

        for ic in range(ICN):
            for pair in range(NPAIR):
                hA, hB = 2 * pair, 2 * pair + 1
                gi = ic * NPAIR + pair
                qslot = gi % 2
                qs = self.bf("QS", 512, sub=qslot * 1024)
                waits = [self.at("s_qsp0"), self.at("s_qsp1"), "cneg_ready", "bo_ready"]
                if gi >= 2:
                    waits.append(self.marks[f"qfree_{gi - 2}"])
                qldsem = f"s_tpq{gi % 2}"
                self.t("sync",
                       lambda e, pair=pair, ic=ic, qs=qs: e.dma_start(
                           out=qs, in_=self.qT_s[pair * 128:(pair + 1) * 128,
                                                 ic * 512:(ic + 1) * 512]),
                       waits=waits, inc=qldsem, n=16)
                qld = self.at(qldsem)

                ngrp = JCN // JG
                pend = []

                def emit_attnv(jg, pslot, gi=gi, hA=hA, hB=hB):
                    pa = self.bf("PA", 1024, sub=pslot * 2048)
                    pb = self.bf("PB", 1024, sub=pslot * 2048)

                    def avmm(e, jg=jg, pa=pa, pb=pb):
                        last = None
                        for u in range(JG):
                            jc = jg * JG + u
                            st, sp = (jc == 0), (jc == JCN - 1)
                            vA = self.bf("V", 65, sub=(jc * 16 + hA) * 130)
                            vB = self.bf("V", 65, sub=(jc * 16 + hB) * 130)
                            e.matmul(self.psOA.ap()[0:65, :], vA,
                                     pa[:, u * 512:(u + 1) * 512], start=st, stop=sp)
                            last = e.matmul(self.psOB.ap()[0:65, :], vB,
                                            pb[:, u * 512:(u + 1) * 512],
                                            start=st, stop=sp)
                        return last
                    awaits = [f"expA_{gi}_{jg}", f"expB_{gi}_{jg}"]
                    if jg == 0:
                        awaits.append(psO_free[0])
                    self.t("tensor", avmm, waits=awaits, inc="s_pe")
                    self.mark(f"av_{gi}_{jg}", "s_pe")

                for jg in range(ngrp):
                    pslot = pcount[0] % 3
                    pcount[0] += 1
                    iA, wA = ring_alloc()
                    iB, wB = ring_alloc()
                    sA, sB = ring[iA], ring[iB]

                    def smm(e, jg=jg, pair=pair, qs=qs, sA=sA, sB=sB):
                        last = None
                        for u in range(JG):
                            jc = jg * JG + u
                            kv = self.bf("KT", 128, sub=pair * 8192 + jc * 256)
                            last = e.matmul(sA.ap()[:, u * 512:(u + 1) * 512],
                                            kv[0:64, :], qs[0:64, :],
                                            start=True, stop=True,
                                            tile_position=(0, 0))
                            last = e.matmul(sB.ap()[:, u * 512:(u + 1) * 512],
                                            kv[64:128, :], qs[64:128, :],
                                            start=True, stop=True,
                                            tile_position=(64, 0))
                        return last
                    self.t("tensor", smm, waits=[qld, wA, wB], inc="s_pe")
                    smark = self.at("s_pe")

                    pa = self.bf("PA", 1024, sub=pslot * 2048)
                    pb = self.bf("PB", 1024, sub=pslot * 2048)
                    ewA = [smark]
                    if pslot_user[pslot] is not None:
                        ewA.append(pslot_user[pslot])
                    ewB = list(ewA)
                    pslot_user[pslot] = f"av_{gi}_{jg}"
                    self.t("scalar",
                           lambda a, pa=pa, hA=hA, sA=sA: a.activation(
                               pa, sA.ap()[:], AF.Exp, bias=self.cneg[:, hA:hA + 1]),
                           waits=ewA, inc="s_act")
                    self.mark(f"expA_{gi}_{jg}", "s_act")
                    ring_consumer[iA] = f"expA_{gi}_{jg}"
                    self.t("scalar",
                           lambda a, pb=pb, hB=hB, sB=sB: a.activation(
                               pb, sB.ap()[:], AF.Exp, bias=self.cneg[:, hB:hB + 1]),
                           waits=ewB, inc="s_act")
                    self.mark(f"expB_{gi}_{jg}", "s_act")
                    ring_consumer[iB] = f"expB_{gi}_{jg}"

                    pend.append((jg, pslot))
                    if len(pend) > 2:
                        emit_attnv(*pend.pop(0))
                    if jg == 0 and tail[0] is not None:
                        emit_tail_recips()
                    elif jg == 1 and tail[0] is not None:
                        emit_tail_rest()
                    elif jg in (2, 4, 6, 8) and pending_op[0] is not None:
                        emit_op_chunk()
                    elif jg in (3, 7, 11):
                        self.emit_wquant(1)
                while pend:
                    emit_attnv(*pend.pop(0))
                self.mark(f"qfree_{gi}", "s_pe")
                tail[0] = {"gi": gi, "pair": pair, "ic": ic,
                           "avdone": self.at("s_pe")}

            pending_op[0] = {"ic": ic, "isub": 0}

        emit_tail_recips()
        emit_tail_rest()
        while pending_op[0] is not None:
            emit_op_chunk()
        self.mark("stage2_done", "s_x1sp")
        self.mark("attn_pe_done", "s_pe")
        self.mark("stage2_dve", "s_dve")
        self.mark("stage2_act", "s_act")

    def stage3(self):
        while self.wq_idx < len(self.wq_jobs):
            self.emit_wquant(1)

        def ld_qw(e):
            last = None
            for kc in range(8):
                last = e.dma_start(out=self.bf("QWG", 2816, sub=kc * 5632),
                                   in_=self.qwg_s[:, kc * 128:(kc + 1) * 128],
                                   transpose=True)
                last = e.dma_start(out=self.bf("QWU", 2816, sub=kc * 5632),
                                   in_=self.qwu_s[:, kc * 128:(kc + 1) * 128],
                                   transpose=True)
            for icc in range(22):
                last = e.dma_start(out=self.bf("QWD", 1024, sub=icc * 2048),
                                   in_=self.qwd_s[:, icc * 128:(icc + 1) * 128],
                                   transpose=True)
            return last
        self.t("sync", ld_qw,
               waits=[self.at("s_wsp0"), self.at("s_wsp1"), "attn_pe_done",
                      "stage2_done"],
               inc="s_tpw", n=38 * 16)
        self.mark("qw_ready", "s_tpw")

        CCH = [(i * 512, min(512, II - i * 512)) for i in range((II + 511) // 512)]
        self.xr_user = ["stage2_dve", "stage2_dve"]

        def emit_ln2(tc):
            for ts_ in range(4):
                trow = tc * 512 + ts_ * 128
                t = tc * 4 + ts_
                slot = t % 2
                xr = self.f32v("XR", 1024, sub=slot * 4096)
                lw = [self.xr_user[slot], f"x1spm_{tc}"]
                self.t("sync",
                       lambda e, trow=trow, xr=xr: e.dma_start(
                           out=xr, in_=self.x1_s[trow:trow + 128, :]),
                       waits=lw, inc=f"s_xld{slot}", n=16)
                bns0 = self.f32v("BNS", 6, sub=slot * 64)
                bns1 = self.f32v("BNS", 6, sub=slot * 64 + 24)
                bns = self.f32v("BNS", 12, sub=slot * 64)
                mean = self.f32v("BNV", 1, sub=slot * 64)
                var = self.f32v("BNV", 1, sub=slot * 64 + 4)
                mv = self.f32v("BNV", 2, sub=slot * 64)
                rstd = self.f32v("BNV", 1, sub=slot * 64 + 8)
                rstd2 = self.f32v("BNV", 1, sub=slot * 64 + 12)

                def ln(v, xr=xr, bns0=bns0, bns1=bns1, bns=bns, mv=mv):
                    v.bn_stats(bns0, xr[:, 0:512])
                    v.bn_stats(bns1, xr[:, 512:1024])
                    return v.bn_aggr(mv, bns)
                self.t("vector", ln, waits=[self.at(f"s_xld{slot}")], inc="s_dve")
                self.t("scalar",
                       lambda a, var=var, rstd=rstd: a.activation(
                           rstd, var, AF.Sqrt, bias=self.f32v("ST", 1, sub=128)),
                       waits=[self.at("s_dve")], inc="s_act")
                self.mark(f"sqrt3_{t}", "s_act")
                x2 = self.f32v("X2", 1024)

                def norm(v, xr=xr, mean=mean, rstd=rstd, rstd2=rstd2, x2=x2):
                    v.reciprocal(rstd2, rstd)
                    return v.tensor_scalar(x2, xr, mean, rstd2,
                                           ALU.subtract, ALU.mult)
                nw = [f"sqrt3_{t}"]
                if t == 0:
                    nw.append("stage2_done")
                self.t("vector", norm, waits=nw, inc="s_dve")
                self.mark(f"ln3n_{t}", "s_dve")
                self.xr_user[slot] = f"ln3n_{t}"

                xq = self.bf("XLN", 1024)
                qw0 = [self.at("s_xqsp")]
                for half in range(2):
                    self.emit_quant2(x2[:, half * 512:(half + 1) * 512],
                                     xq[:, half * 512:(half + 1) * 512], 512,
                                     waits_first=qw0 if half == 0 else (),
                                     inc_last="s_dve", pool=False)
                self.t("sync",
                       lambda e, trow=trow, xq=xq: e.dma_start(
                           out=self.xq_s[trow:trow + 128, :], in_=xq),
                       waits=[self.at("s_dve")], inc="s_xqsp", n=16)

        def emit_xqtld(tc):
            def xqtld(e, tc=tc):
                last = None
                for kc in range(8):
                    last = e.dma_start(
                        out=self.bf("XQT", 512, sub=kc * 1024),
                        in_=self.xq_s[tc * 512:(tc + 1) * 512, kc * 128:(kc + 1) * 128],
                        transpose=True)
                return last
            waits = [self.at("s_xqsp")]
            if tc == 0:
                waits += ["attn_pe_done", "qw_ready"]
            else:
                waits.append(f"gu_done_{tc - 1}")
            self.t("sync", xqtld, waits=waits, inc="s_tpx", n=8 * 16)
            self.mark(f"xqm_{tc}", "s_tpx")

        def emit_hqtld(tc, part, i0, i1, extra=()):
            def hqtld(e, tc=tc, i0=i0, i1=i1):
                last = None
                for icc in range(i0, i1):
                    last = e.dma_start(
                        out=self.bf("HQT", 512, sub=icc * 1024),
                        in_=self.hq_s[tc * 512:(tc + 1) * 512, icc * 128:(icc + 1) * 128],
                        transpose=True)
                return last
            waits = [self.at("s_hqsp0"), self.at("s_hqsp1")] + list(extra)
            self.t("sync", hqtld, waits=waits, inc="s_tph", n=(i1 - i0) * 16)
            self.mark(f"hqld{part}_{tc}", "s_tph")

        DPS = [self.psA, self.psB, self.psC, self.psA]

        def emit_dchain(tc, ts_, part, i0, i1):
            pst = DPS[ts_]

            def dmm(e, ts_=ts_, i0=i0, i1=i1, pst=pst):
                last = None
                for e2 in range(2):
                    for icc in range(i0, i1):
                        last = e.matmul(
                            pst.ap()[:, e2 * 512:(e2 + 1) * 512],
                            self.bf("HQT", 128, sub=icc * 1024 + ts_ * 256),
                            self.bf("QWD", 512, sub=icc * 2048 + e2 * 1024),
                            start=(icc == 0), stop=(icc == 21))
                return last
            if part == "A":
                dw = [f"hqldA_{tc}"]
                if ts_ == 3:
                    dw.append(f"dev3_{tc}_0")
                elif tc == 0:
                    dw += ["stage2_act", "stage2_dve"]
                elif ts_ == 0:
                    dw.append(f"dev3_{tc - 1}_3")
                else:
                    dw.append(f"dev3_{tc - 1}_{ts_}")
            else:
                dw = [f"hqldB_{tc}", f"hqldC_{tc}"]
            self.t("tensor", dmm, waits=dw, inc="s_pe")
            self.mark(f"dmm{part}_{tc}_{ts_}", "s_pe")

        def emit_gu(tc):
            for ci, (c0, cw) in enumerate(CCH):
                for ts_ in range(4):
                    ch = tc * 24 + ci * 4 + ts_

                    def gmm(e, ts_=ts_, c0=c0, cw=cw):
                        last = None
                        for kc in range(8):
                            last = e.matmul(
                                self.psOA.ap()[:, 0:cw],
                                self.bf("XQT", 128, sub=kc * 1024 + ts_ * 256),
                                self.bf("QWG", cw, sub=kc * 5632 + c0 * 2),
                                start=(kc == 0), stop=(kc == 7))
                        return last
                    gw = [f"xqm_{tc}"]
                    if ch == 0:
                        gw.append("stage2_dve")
                    else:
                        gw.append(f"silu_{ch - 1}")
                    self.t("tensor", gmm, waits=gw, inc="s_pe")
                    gm = self.at("s_pe")
                    sgt = self.f32v("SGT", 512)
                    sw = [gm]
                    if ch > 0:
                        sw.append(f"hm_{ch - 1}")
                    self.t("scalar",
                           lambda a, cw=cw, sgt=sgt: a.activation(
                               sgt[:, 0:cw], self.psOA.ap()[:, 0:cw], AF.Silu),
                           waits=sw, inc="s_act")
                    self.mark(f"silu_{ch}", "s_act")

                    def umm(e, ts_=ts_, c0=c0, cw=cw):
                        last = None
                        for kc in range(8):
                            last = e.matmul(
                                self.psOB.ap()[:, 0:cw],
                                self.bf("XQT", 128, sub=kc * 1024 + ts_ * 256),
                                self.bf("QWU", cw, sub=kc * 5632 + c0 * 2),
                                start=(kc == 0), stop=(kc == 7))
                        return last
                    uw = [f"xqm_{tc}"]
                    if ch == 0:
                        uw.append("stage2_dve")
                    else:
                        uw.append(f"hm_{ch - 1}")
                    self.t("tensor", umm, waits=uw, inc="s_pe")
                    um = self.at("s_pe")
                    ht = self.f32v("HT", 512)

                    def hmul(v, cw=cw, sgt=sgt, ht=ht):
                        return v.tensor_tensor(ht[:, 0:cw], sgt[:, 0:cw],
                                               self.psOB.ap()[:, 0:cw], ALU.mult)
                    self.t("vector", hmul, waits=[um, f"silu_{ch}"], inc="s_dve")
                    self.mark(f"hm_{ch}", "s_dve")
                    hq = self.bf("HQ", 512, sub=(ch % 2) * 1024)
                    hqsem = f"s_hqsp{ch % 2}"
                    self.emit_quant2(ht[:, 0:cw], hq[:, 0:cw], cw,
                                     waits_first=[self.at(hqsem)],
                                     inc_last="s_dve", pool=False)
                    trow = tc * 512 + ts_ * 128
                    self.t("sync",
                           lambda e, trow=trow, c0=c0, cw=cw, hq=hq: e.dma_start(
                               out=self.hq_s[trow:trow + 128, c0:c0 + cw],
                               in_=hq[:, 0:cw]),
                           waits=[self.at("s_dve")], inc=hqsem, n=16)
                if ci == 2:
                    emit_hqtld(tc, "A", 0, 12,
                               [f"dmmA_{tc - 1}_3"] if tc else [])
                elif ci == 3:
                    emit_dchain(tc, 0, "A", 0, 12)
                elif ci == 4:
                    emit_hqtld(tc, "B", 12, 18,
                               [f"dmmB_{tc - 1}_3"] if tc else [])
                    emit_dchain(tc, 1, "A", 0, 12)
                    emit_dchain(tc, 2, "A", 0, 12)
            emit_hqtld(tc, "C", 18, 22,
                       [f"dmmB_{tc - 1}_3"] if tc else [])
            self.mark(f"gu_done_{tc}", "s_pe")

        def emit_dmm(tc):
            for ts_ in range(4):
                t = tc * 4 + ts_
                slot = t % 2
                trow = tc * 512 + ts_ * 128
                xr = self.f32v("XR", 1024, sub=slot * 4096)
                self.t("sync",
                       lambda e, trow=trow, xr=xr: e.dma_start(
                           out=xr, in_=self.x1_s[trow:trow + 128, :]),
                       waits=[self.xr_user[slot]], inc=f"s_xld{slot}", n=16)
                if ts_ == 3:
                    emit_dchain(tc, 3, "A", 0, 12)
                emit_dchain(tc, ts_, "B", 12, 22)
                outt = self.f32v("OUTT", 1024)

                def dev(v, outt=outt, xr=xr, pst=DPS[ts_]):
                    return v.tensor_tensor(outt, pst.ap()[:], xr, ALU.add)
                self.t("vector", dev,
                       waits=[self.at("s_pe"), self.at(f"s_xld{slot}"),
                              self.at("s_out")],
                       inc="s_dve")
                self.mark(f"dev3_{tc}_{ts_}", "s_dve")
                self.xr_user[slot] = f"dev3_{tc}_{ts_}"
                self.t("sync",
                       lambda e, trow=trow, outt=outt: e.dma_start(
                           out=self.out[trow:trow + 128, :], in_=outt),
                       waits=[self.at("s_dve")], inc="s_out", n=16)

        emit_ln2(0)
        emit_xqtld(0)
        for tc in range(TCN):
            if tc + 1 < TCN:
                emit_ln2(tc + 1)
            emit_gu(tc)
            if tc + 1 < TCN:
                emit_xqtld(tc + 1)
            emit_dmm(tc)

        self.t("sync", lambda e: None, waits=[self.at("s_out")])

    def finalize(self):
        nc = self.nc
        per_eng = {"sync": [], "tensor": [], "vector": [], "scalar": [], "gpsimd": []}
        for task in self.tasks:
            key = {"pool": "gpsimd", "scalardma": "scalar"}.get(task.eng, task.eng)
            per_eng[key].append(task)
        last_wait = {}

        class DmaProxy:
            def __init__(self, eng, sem):
                self._eng, self._sem = eng, sem
                self.count = 0

            def dma_start(self, **kw):
                self.count += 1
                return self._eng.dma_start(**kw).then_inc(self._sem, 16)

            def __getattr__(self, name):
                return getattr(self._eng, name)

        class SerialProxy:

            _EXEMPT = {"wait_ge", "then_inc", "dma_start"}

            def __init__(self, eng, selfsem):
                object.__setattr__(self, "_eng", eng)
                object.__setattr__(self, "_sem", selfsem)
                object.__setattr__(self, "_count", 0)
                object.__setattr__(self, "_prev", None)
                object.__setattr__(self, "_token", None)
                object.__setattr__(self, "_emitted", {})

            def _pre_op(self):
                if self._prev is not None:
                    object.__setattr__(self, "_count", self._count + 1)
                    self._prev.then_inc(self._sem, 1)
                    object.__setattr__(self, "_token", (self._sem, self._count))
                    object.__setattr__(self, "_prev", None)
                if self._token is not None:
                    self._eng.wait_ge(*self._token)
                    object.__setattr__(self, "_token", None)

            def wait_ge(self, sem, val):
                return self._eng.wait_ge(sem, val)

            def finish_task(self, inc_sem, inc_n, planned_total):
                ins = self._prev
                assert ins is not None
                ins.then_inc(inc_sem, inc_n)
                object.__setattr__(self, "_prev", None)
                object.__setattr__(self, "_token", (inc_sem, planned_total))

            def __getattr__(self, name):
                target = getattr(self._eng, name)
                if name.startswith("_") or not callable(target):
                    return target

                def wrapped(*a, **kw):
                    self._pre_op()
                    ins = target(*a, **kw)
                    object.__setattr__(self, "_prev", ins)
                    return ins
                return wrapped

        def runner(eng_name):
            def run(eng_obj):
                emitted = {}
                proxy = None
                if eng_name in ("vector", "gpsimd"):
                    proxy = SerialProxy(eng_obj, self.sems[f"s_self_{eng_name}"])
                for task in per_eng[eng_name]:
                    for s, vv in task.waits:
                        if vv <= 0:
                            continue
                        key = (eng_name, s)
                        if last_wait.get(key, -1) >= vv:
                            continue
                        last_wait[key] = vv
                        eng_obj.wait_ge(self.sems[s], vv)
                    if task.inc is not None and task.eng in ("sync", "gpsimd",
                                                              "scalardma"):
                        dproxy = DmaProxy(eng_obj, self.sems[task.inc])
                        task.fn(dproxy)
                        assert dproxy.count * 16 == task.n, \
                            f"{eng_name} dma count {dproxy.count}*16 != {task.n}"
                    elif proxy is not None:
                        ret = task.fn(proxy)
                        if task.inc is not None:
                            assert ret is not None, f"{eng_name} needs instr ({task.inc})"
                            emitted[task.inc] = emitted.get(task.inc, 0) + task.n
                            proxy.finish_task(self.sems[task.inc], task.n,
                                              emitted[task.inc])
                    else:
                        ret = task.fn(eng_obj)
                        if task.inc is not None:
                            assert ret is not None, f"{eng_name} needs instr ({task.inc})"
                            ret.then_inc(self.sems[task.inc], task.n)
            return run

        self.sem("s_self_vector")
        self.sem("s_self_scalar")
        self.sem("s_self_gpsimd")
        blk = self._es.enter_context(nc.Block())
        blk.sync(runner("sync"))
        blk.tensor(runner("tensor"))
        blk.vector(runner("vector"))
        blk.scalar(runner("scalar"))
        blk.gpsimd(runner("gpsimd"))
        self._es.close()
        return nc


def build(debug=False):
    b = Builder(debug=debug)
    b.schedule()
    return b.finalize()


_cached = {}


def kernel(L0_attn_x, L0_attn_W_w, L0_attn_W_b, L0_attn_out_w, L0_attn_out_b,
           L0_ffn_w_gate, L0_ffn_w_up, L0_ffn_w_down):
    x = np.asarray(L0_attn_x, dtype=np.float32)
    wqkvT = np.ascontiguousarray(np.asarray(L0_attn_W_w, np.float32).T)
    woT = np.ascontiguousarray(np.asarray(L0_attn_out_w, np.float32).T)
    bqkv = np.ascontiguousarray(np.asarray(L0_attn_W_b, np.float32))
    bo = np.ascontiguousarray(np.asarray(L0_attn_out_b, np.float32))
    wg = np.ascontiguousarray(np.asarray(L0_ffn_w_gate, np.float32))
    wu = np.ascontiguousarray(np.asarray(L0_ffn_w_up, np.float32))
    wd = np.ascontiguousarray(np.asarray(L0_ffn_w_down, np.float32))
    ones_col = np.ones((128, 1), ml_dtypes.bfloat16)
    ones_row = np.ones((1, 128), ml_dtypes.bfloat16)

    if "nc" not in _cached:
        _cached["nc"] = build(debug=False)
    nc = _cached["nc"]
    in_maps = []
    for b_ in range(8):
        in_maps.append({
            "x": np.ascontiguousarray(x[b_]),
            "wqkvT": wqkvT, "woT": woT, "bqkv": bqkv, "bo": bo,
            "wg": wg, "wu": wu, "wd": wd,
            "ones_col": ones_col, "ones_row": ones_row,
        })
    res = run_bass_kernel_spmd(nc, in_maps, core_ids=list(range(8)))
    return np.stack([r["out"] for r in res.results], axis=0)
